# version 06
# baseline (speedup 1.0000x reference)
import numpy as np
import ml_dtypes
from contextlib import ExitStack

import concourse.bass as bass
import concourse.mybir as mybir
from concourse.bass_utils import run_bass_kernel_spmd

F32 = mybir.dt.float32
BF16 = mybir.dt.bfloat16
AF = mybir.ActivationFunctionType
ALU = mybir.AluOpType
AXL = mybir.AxisListType

NCORE = 8
B = 16
NSEQ = 2048
NLOC = NSEQ // NCORE
C = 1024
T = B * NLOC
NTT = T // 128
NTS = T // 512
H1, D1 = 32, 32
H2, D2, KVH = 16, 64, 4
FF = 2048
NE = 8
EPS_LN = 1e-5
EPS_ATTN = 1e-15
NEG_BIG = -3.0e38
VKW = B * 8 * 33
NGRP = 4
TG = T // NGRP

FAM_X, FAM_X1, FAM_Y, FAM_LN2, FAM_FFN = 0, 10, 22, 30, 38
SW = 46

DEBUG_TAPS = []
STOP_AFTER = None
P3_TILES = NTT
P3_SUB = "all"
P4_SUB = "all"
P4B_SUB = "all"
P4B_CHAIN = 8
DEBUG_SHAPES = {
    "k0": (128, 1024), "v0": (128, 1056), "vkbf": (128, VKW),
    "qh0": (128, T), "projin0": (128, C), "y0": (128, C),
    "x1t0": (128, C), "qin0": (128, T), "q20": (128, T),
    "attfm0": (128, T), "h1t0": (128, C), "xm0": (128, T),
    "gates": (128, NE), "acc0": (128, C), "k2sp": (128, T), "v2t": (128, 2048),
    "xmsp": (128, T), "v2sp": (128, T), "v2w0": (128, T), "cb0": (128, T), "rows_a": (128, T), "rows_b": (128, T), "statw": (128, NTT * SW),
}

ENGINES = ("pe", "act", "dve", "pool", "sync")
FAMS = ("cc",)
NDMASEM = 24


class Ev:
    __slots__ = ("sem", "value", "step")
    def __init__(self, sem, step):
        self.sem = sem
        self.value = None
        self.step = step


class Step:
    __slots__ = ("fn", "waits", "ev", "inc", "_sem", "_inc_amt", "_engine")
    def __init__(self, fn):
        self.fn = fn
        self.waits = []
        self.ev = None
        self.inc = 0


class _StopBuild(Exception):
    pass


class KB:
    def __init__(self, nc, ctx):
        self.nc = nc
        self.streams = {e: [] for e in ENGINES}
        self.sems = {}
        for s in ENGINES + FAMS:
            self.sems[s] = ctx.enter_context(nc.semaphore("s_" + s))
        self.dma_pools = {}
        for q in ("sync", "pool", "act", "dve"):
            n = NDMASEM if q in ("sync", "pool") else 4
            sems = []
            for i in range(n):
                nm = "d_%s%d" % (q, i)
                self.sems[nm] = ctx.enter_context(nc.semaphore(nm))
                sems.append(nm)
            self.dma_pools[q] = {"sems": sems, "i": 0, "last": [None] * n}
        self.fence_evs = []
        self.fence_todo = set()

    def op(self, engine, fn, deps=(), sem=None, inc=1):
        st = Step(fn)
        if engine in self.fence_todo:
            self.fence_todo.discard(engine)
            st.waits.extend(self.fence_evs)
        for d in deps:
            if d is not None:
                st.waits.append(d)
        st._sem = sem or engine
        st._inc_amt = inc
        st._engine = engine
        self.streams[engine].append(st)
        return st

    def ev(self, step):
        if step.ev is None:
            step.ev = Ev(step._sem, step)
            step.inc = step._inc_amt
        return step.ev

    def tail(self, engine):
        s = self.streams[engine]
        return self.ev(s[-1]) if s else None

    def fence(self, extra=()):
        evs = [self.tail(e) for e in ENGINES] + list(extra)
        for p in self.dma_pools.values():
            evs += [e for e in p["last"] if e is not None]
        self.fence_evs = [e for e in evs if e is not None]
        self.fence_todo = set(ENGINES)

    def dma(self, queue, fam, out, in_, deps=(), **kw):
        p = self.dma_pools[queue]
        n = len(p["sems"])
        i = p["i"] % n
        p["i"] += 1
        dd = list(deps)
        if p["last"][i] is not None:
            dd.append(p["last"][i])
        st = self.op(queue,
                     lambda e, out=out, in_=in_, kw=kw: e.dma_start(out=out, in_=in_, **kw),
                     deps=dd, sem=p["sems"][i], inc=16)
        ev = self.ev(st)
        p["last"][i] = ev
        return ev

    def finalize(self):
        counts = {s: 0 for s in self.sems}
        for e in ENGINES:
            for st in self.streams[e]:
                if st.ev is not None:
                    counts[st._sem] += st.inc
                    st.ev.value = counts[st._sem]

    def emit(self, engine, eng):
        seen = {}
        for st in self.streams[engine]:
            for w in st.waits:
                assert w.value is not None, "un-finalized event"
                if seen.get(w.sem, 0) >= w.value:
                    continue
                seen[w.sem] = w.value
                eng.wait_ge(self.sems[w.sem], w.value)
            ins = st.fn(eng)
            if st.ev is not None:
                ins.then_inc(self.sems[st._sem], st.inc)


class Rot:
    def __init__(self, kb, items):
        self.items = items
        self.evs = [[] for _ in items]
        self.i = 0

    def take(self):
        i = self.i % len(self.items)
        self.i += 1
        return self.items[i], list(self.evs[i]), i

    def done(self, i, evs):
        if not isinstance(evs, (list, tuple)):
            evs = [evs]
        self.evs[i] = [e for e in evs if e is not None]


def bc_last(ap, n):
    return bass.AP(ap.tensor, ap.offset, [list(d) for d in ap.ap] + [[0, n]])


def build_program():
    nc = bass.Bass("TRN2", target_bir_lowering=False)
    ctx = ExitStack()
    kb = KB(nc, ctx)
    try:
        _build_phases(nc, ctx, kb)
    except _StopBuild:
        pass
    _do_finalize(kb, nc)
    try:
        ctx.close()
    except AssertionError:
        pass
    return nc


def _do_finalize(kb, nc):
    tails = [kb.tail(e) for e in ENGINES]
    kb.op("pool", lambda e: e.nop(), deps=[t for t in tails if t is not None])
    kb.finalize()
    with nc.Block() as block:
        @block.sync
        def _(eng):
            kb.emit("sync", eng)

        @block.scalar
        def _(eng):
            kb.emit("act", eng)

        @block.vector
        def _(eng):
            kb.emit("dve", eng)

        @block.tensor
        def _(eng):
            kb.emit("pe", eng)

        @block.gpsimd
        def _(eng):
            kb.emit("pool", eng)


def _build_phases(nc, ctx, kb):
    E = ctx.enter_context

    def din(name, shape, dt=F32):
        return nc.dram_tensor(name, shape, dt, kind="ExternalInput").ap()

    x_tm_h = din("x_tm", [T, C])
    x_fm_h = din("x_fm", [C, T])
    wq1_h = din("wq1T", [C, 3 * C])
    wp1_h = din("wp1T", [C, C])
    bp1_h = din("bp1", [1, C])
    wq2_h = din("wq2T", [C, 1536])
    wp2_h = din("wp2T", [C, C])
    wr_h = din("wrT", [C, NE])
    w1_h = din("w1", [NE, C, FF])
    w2_h = din("w2", [NE, FF, C])
    lncol_h = din("lncols", [128, 8, 4])
    iden_h = din("iden", [128, 128], BF16)
    ones4_h = din("ones4", [128, 512], BF16)
    onesc_h = din("onescol", [128, 1], BF16)
    mask_h = din("mask", [128, 512])

    out_h = nc.dram_tensor("out", [T, C], F32, kind="ExternalOutput").ap()

    vk_in = nc.dram_tensor("vk_in", [128, VKW], F32, kind="Internal").ap()
    vk_out = nc.dram_tensor("vk_out", [128, VKW], F32, kind="Internal").ap()
    x1fm_sp = nc.dram_tensor("x1fm_sp", [8, 128, T], F32, kind="Internal").ap()
    x1tm_sp = nc.dram_tensor("x1tm_sp", [NTT, 128, C], F32, kind="Internal").ap()
    h1tm_sp = nc.dram_tensor("h1tm_sp", [NTT, 128, C], BF16, kind="Internal").ap()
    q2_sp = nc.dram_tensor("q2_sp", [8, 128, T], BF16, kind="Internal").ap()
    projin_sp = nc.dram_tensor("projin_sp", [NTT, 128, C], BF16, kind="Internal").ap()
    k2_sp = nc.dram_tensor("k2_sp", [2, 128, T], BF16, kind="Internal").ap()
    v2_sp = nc.dram_tensor("v2_sp", [2, 128, T], BF16, kind="Internal").ap()
    xm_sp = nc.dram_tensor("xm_sp", [8, 128, T], BF16, kind="Internal").ap()
    w1b = nc.dram_tensor("w1b", [NE, 8, 128, FF], BF16, kind="Internal").ap()
    w2b = nc.dram_tensor("w2b", [NE, 16, 128, C], BF16, kind="Internal").ap()

    dbg = {}
    _BF_TAPS = {"k2sp", "v2sp", "q20", "projin0", "xmsp"}
    for name in DEBUG_TAPS:
        dt_ = BF16 if name in _BF_TAPS else F32
        dbg[name] = nc.dram_tensor("dbg_" + name, list(DEBUG_SHAPES[name]), dt_,
                                   kind="ExternalOutput").ap()

    def tap(name, src_ap, deps):
        if name in dbg:
            tmp = dbg[name]
            return kb.dma("pool", "fo", tmp, src_ap, deps=deps)
        return None

    statw = E(nc.sbuf_tensor("statw", [128, NTT, SW], F32))
    gates = E(nc.sbuf_tensor("gates", [128, NTT, NE], F32))
    lncol = E(nc.sbuf_tensor("lncolsb", [128, 8, 4], F32))
    iden_sb = E(nc.sbuf_tensor("idensb", [128, 128], BF16))
    ones4 = E(nc.sbuf_tensor("ones4sb", [128, 512], BF16))
    onesc = E(nc.sbuf_tensor("onescsb", [128, 1], BF16))
    statbf = E(nc.sbuf_tensor("statbf", [128, 4], BF16))
    cknst = E(nc.sbuf_tensor("cknst", [128, 2], F32))
    sqs = E(nc.sbuf_tensor("sqs", [128, 512], F32))

    psF = [E(nc.psum_tensor("pf%d" % i, [128, 512], F32)) for i in range(6)]
    psB = [E(nc.psum_tensor("pb%d" % i, [128, 1024], BF16)) for i in range(2)]
    fb = Rot(kb, psF)
    bb = Rot(kb, psB)
    statbf_rot = Rot(kb, [statbf[:, i:i + 1] for i in range(4)])

    prow = ExitStack()
    rows_a = prow.enter_context(nc.sbuf_tensor("rows_a", [128, T], BF16))
    rows_b = prow.enter_context(nc.sbuf_tensor("rows_b", [128, T], BF16))
    colsums = prow.enter_context(nc.sbuf_tensor("colsums", [128, 4096], BF16))

    def maybe_stop(tag):
        if STOP_AFTER == tag:
            raise _StopBuild()

    def pe(fn, deps=()):
        return kb.op("pe", fn, deps)

    def dve(fn, deps=()):
        return kb.op("dve", fn, deps)

    def act(fn, deps=()):
        return kb.op("act", fn, deps)

    def pool(fn, deps=()):
        return kb.op("pool", fn, deps)

    def ln_stats(pieces, tt, fam, deps):
        s = lambda k: statw[:, tt, fam + k:fam + k + 1]
        n_feat = float(sum(w for _, w in pieces))
        ap0, w0 = pieces[0]
        st = dve(lambda e: e.reduce_sum(out=s(2), in_=ap0, axis=AXL.X), deps)
        ev_sum = kb.ev(st)
        st = act(lambda e: e.activation(out=sqs[:, 0:w0], in_=ap0, func=AF.Square,
                                        accum_out=s(3)), deps + [ev_sum])
        ev_sq = kb.ev(st)
        for ap1, w1_ in pieces[1:]:
            st = dve(lambda e, ap1=ap1: e.reduce_sum(out=s(6), in_=ap1, axis=AXL.X), deps)
            e2 = kb.ev(st)
            st = pool(lambda e: e.tensor_add(s(2), s(2), s(6)), [ev_sum, e2])
            ev_sum = kb.ev(st)
            st = act(lambda e, ap1=ap1, w1_=w1_: e.activation(
                out=sqs[:, 0:w1_], in_=ap1, func=AF.Square, accum_out=s(7)), deps + [e2, ev_sq])
            e2 = kb.ev(st)
            st = pool(lambda e: e.tensor_add(s(3), s(3), s(7)), [ev_sq, e2])
            ev_sq = kb.ev(st)
        st = pool(lambda e: e.tensor_scalar_mul(s(4), s(2), 1.0 / n_feat), [ev_sum])
        ev_mu = kb.ev(st)
        st = pool(lambda e: e.tensor_scalar_mul(s(1), s(2), -1.0 / n_feat), [ev_sum])
        ev_negmu = kb.ev(st)
        st = pool(lambda e: e.tensor_scalar_mul(s(5), s(3), 1.0 / n_feat), [ev_sq])
        kb.ev(st)
        st = pool(lambda e: e.tensor_mul(s(6), s(4), s(4)), [ev_mu])
        m2ev = kb.ev(st)
        st = pool(lambda e: e.tensor_sub(s(5), s(5), s(6)), [m2ev])
        ev_var = kb.ev(st)
        st = act(lambda e: e.activation(out=s(7), in_=s(5), func=AF.Ln, bias=cknst[:, 0:1]), [ev_var])
        lnev = kb.ev(st)
        st = act(lambda e: e.activation(out=s(0), in_=s(7), func=AF.Exp, scale=cknst[:, 1:2]), [lnev])
        ev_r = kb.ev(st)
        return {"r": ev_r, "negmu": ev_negmu, "var": ev_var, "var_ap": s(5)}

    def make_row(col_ap, dst_tensor, row_base, col0, deps, ncols=128):
        cb, cbd, cbi = statbf_rot.take()
        st = dve(lambda e: e.tensor_copy(out=cb, in_=col_ap), list(deps) + cbd)
        cev = kb.ev(st)
        bt, bd, bi = bb.take()
        tp = (0, row_base) if row_base else None
        st = pe(lambda e: e.transpose(bt[row_base:row_base + 1, 0:ncols], cb, iden_sb[:, :],
                                      tile_position=tp), [cev] + bd)
        tev = kb.ev(st)
        st = dve(lambda e: e.tensor_copy(
            out=dst_tensor[row_base:row_base + 1, col0:col0 + ncols],
            in_=bt[row_base:row_base + 1, 0:ncols]), [tev])
        rev = kb.ev(st)
        bb.done(bi, rev)
        statbf_rot.done(cbi, rev)
        return rev

    def bcast(row_tensor, row_base, col0, n, deps):
        bt, bd, bi = fb.take()
        tp = (row_base, 0) if row_base else None
        st = pe(lambda e: e.matmul(bt[:, 0:n], ones4[row_base:row_base + 1, 0:128],
                                   row_tensor[row_base:row_base + 1, col0:col0 + n],
                                   start=True, stop=True, tile_position=tp), list(deps) + bd)
        return bt, kb.ev(st), bi

    kb.ev(kb.op("dve", lambda e: e.memset(cknst[:, 0:1], EPS_LN)))
    kb.ev(kb.op("dve", lambda e: e.memset(cknst[:, 1:2], -0.5)))
    ld_c = []
    for dst, src in ((iden_sb[:, :], iden_h[:, :]), (ones4[:, :], ones4_h[:, :]),
                     (onesc[:, :], onesc_h[:, :]),
                     (lncol[:, :, :], lncol_h[:, :, :])):
        ld_c.append(kb.dma("sync", "fc", dst, src))


    p0 = ExitStack()
    NPSL = 2
    wstage = p0.enter_context(nc.sbuf_tensor("wstage", [128, NPSL, C], F32))
    wcst = p0.enter_context(nc.sbuf_tensor("wcst", [128, NPSL, C], BF16))
    _jobs = [("w1", e_, ch, hf) for e_ in range(NE) for ch in range(8)
             for hf in range(2)] + \
            [("w2", e_, ch, 0) for e_ in range(NE) for ch in range(16)]

    def _job_aps(i):
        kind, e_, ch, hf = _jobs[i]
        if kind == "w1":
            src = w1_h[e_, ch * 128:(ch + 1) * 128, hf * C:(hf + 1) * C]
            dst = w1b[e_, ch, :, hf * C:(hf + 1) * C]
        else:
            src = w2_h[e_, ch * 128:(ch + 1) * 128, :]
            dst = w2b[e_, ch, :, :]
        return kind, e_, src, dst

    _pump = {"issue": 0, "proc": 0, "load": {}, "store": [None] * NPSL, "done": {}}

    def _pump_issue_to(bound):
        while _pump["issue"] < min(bound, len(_jobs)):
            i = _pump["issue"]
            _pump["issue"] = i + 1
            kind, e_, src, dst = _job_aps(i)
            sl = i % NPSL
            dep = [_pump["store"][sl]] if _pump["store"][sl] else []
            _pump["load"][i] = kb.dma("pool", "fs", wstage[:, sl, :], src, deps=dep)

    def pump(k=1):
        for _ in range(k):
            _pump_issue_to(_pump["proc"] + NPSL)
            i = _pump["proc"]
            if i >= len(_jobs):
                return
            _pump["proc"] = i + 1
            kind, e_, src, dst = _job_aps(i)
            sl = i % NPSL
            dep = [_pump["load"].pop(i)]
            if _pump["store"][sl] is not None:
                dep.append(_pump["store"][sl])
            st = dve(lambda e, sl=sl: e.tensor_copy(out=wcst[:, sl, :],
                                                    in_=wstage[:, sl, :]), dep)
            cev = kb.ev(st)
            sev = kb.dma("pool", "fs", dst, wcst[:, sl, :], deps=[cev])
            _pump["store"][sl] = sev
            _pump["done"].setdefault((kind, e_), []).append(sev)

    def pump_flush():
        while _pump["proc"] < len(_jobs):
            pump(2)


    def cc_test(tag):
        if STOP_AFTER == tag:
            z0 = kb.dma("pool", "fp", vk_in[0:128, 0:128], sqs[:, 0:128], deps=list(ld_c))
            st_ = kb.op("pool", lambda e: e.collective_compute(
                "AllReduce", ALU.add, replica_groups=[list(range(NCORE))],
                ins=[vk_in.opt()], outs=[vk_out.opt()]), deps=[z0], sem="cc", inc=1)
            kb.ev(st_)
            raise _StopBuild()
    if STOP_AFTER == "cc0":
        z0 = kb.dma("pool", "fp", vk_in[0:128, 0:128], sqs[:, 0:128], deps=list(ld_c))
        st = kb.op("pool", lambda e: e.collective_compute(
            "AllReduce", ALU.add, replica_groups=[list(range(NCORE))],
            ins=[vk_in.opt()], outs=[vk_out.opt()]), deps=[z0], sem="cc", inc=1)
        kb.ev(st)
        raise _StopBuild()

    p12 = ExitStack()
    xfmbf = p12.enter_context(nc.sbuf_tensor("xfmbf", [128, 8, T], BF16))
    stage_t = p12.enter_context(nc.sbuf_tensor("stage", [128, 2, 2048], F32))
    xslab = p12.enter_context(nc.sbuf_tensor("xslab", [128, 2, C], F32))
    p1only = ExitStack()
    wkv_sb = p1only.enter_context(nc.sbuf_tensor("wkv", [128, 8, 2048], BF16))
    kpool = p1only.enter_context(nc.sbuf_tensor("kpool", [128, 3, 1024], BF16))
    vpool = p1only.enter_context(nc.sbuf_tensor("vpool", [128, 3, 1056], BF16))
    vksb = p1only.enter_context(nc.sbuf_tensor("vksb", [128, VKW], F32))
    st_rot = Rot(kb, [stage_t[:, 0, :], stage_t[:, 1, :]])
    xslab_rot = Rot(kb, [xslab[:, 0, :], xslab[:, 1, :]])

    xstat_r = []
    for tt in range(NTT):
        sl_ap, sl_dep, sl_i = xslab_rot.take()
        lev = kb.dma("sync", "fx", sl_ap, x_tm_h[tt * 128:(tt + 1) * 128, :],
                     deps=sl_dep + (list(ld_c) if tt == 0 else []))
        evs = ln_stats([(sl_ap[:, 0:512], 512), (sl_ap[:, 512:1024], 512)],
                       tt, FAM_X, [lev])
        rev = make_row(statw[:, tt, FAM_X + 1:FAM_X + 2], rows_a, 0, tt * 128, [evs["negmu"]])
        xstat_r.append(evs["r"])
        xslab_rot.done(sl_i, [evs["r"], rev])
        pump(1)
    negmu0_rows_done = kb.tail("dve")
    cc_test("cc1")

    xfm_evs = []
    for ch in range(8):
        for half in range(2):
            sg, sd, si = st_rot.take()
            lev = kb.dma("sync", "fx", sg,
                         x_fm_h[ch * 128:(ch + 1) * 128, half * 2048:(half + 1) * 2048], deps=sd)
            st = dve(lambda e, sg=sg, ch=ch, half=half: e.tensor_copy(
                out=xfmbf[:, ch, half * 2048:(half + 1) * 2048], in_=sg), [lev])
            ev_ = kb.ev(st)
            st_rot.done(si, ev_)
            xfm_evs.append(ev_)
    xfm_done = xfm_evs[-1]

    wkv_evs = []
    for ch in range(8):
        sg, sd, si = st_rot.take()
        lev = kb.dma("sync", "fw", sg, wq1_h[ch * 128:(ch + 1) * 128, 1024:3072], deps=sd)
        st = dve(lambda e, sg=sg, ch=ch: e.tensor_copy(out=wkv_sb[:, ch, :], in_=sg), [lev])
        ev_ = kb.ev(st)
        st_rot.done(si, ev_)
        wkv_evs.append(ev_)
    wkv_done = wkv_evs[-1]
    cc_test("cc2")

    csk_evs = []
    for kvo in range(4):
        bt, bd, bi = fb.take()
        for ch in range(8):
            st = pe(lambda e, ch=ch, kvo=kvo, bt=bt: e.matmul(
                bt[0:1, 0:512], onesc[:, :], wkv_sb[:, ch, kvo * 512:(kvo + 1) * 512],
                start=(ch == 0), stop=(ch == 7)),
                ([wkv_done, ] + list(ld_c) + bd) if ch == 0 else [])
        mev = kb.ev(st)
        st = dve(lambda e, kvo=kvo, bt=bt: e.tensor_copy(
            out=colsums[0:1, kvo * 512:(kvo + 1) * 512], in_=bt[0:1, 0:512]), [mev])
        cev = kb.ev(st)
        fb.done(bi, cev)
        csk_evs.append(cev)
    colsum_kv_ev = csk_evs[-1]

    vones = []
    for i in range(3):
        ap = vpool[:, i, :].rearrange("p (h e) -> p h e", e=33)[:, :, 32:33]
        st = dve(lambda e, ap=ap: e.memset(ap, 1.0), [])
        vones.append(kb.ev(st))

    cc_test("cc3")
    fb_kv = Rot(kb, psF[0:2])
    kv_ready = [None] * NTT
    kv_free = [[] for _ in range(3)]
    vk_bank_busy = [[] for _ in range(4)]
    vk_evac_evs = []

    for b in range(B):
        for half in range(2):
            tt = 2 * b + half
            slot = tt % 3
            base_deps = [xfm_done, wkv_done, colsum_kv_ev, negmu0_rows_done,
                         xstat_r[tt], vones[slot]] + kv_free[slot]
            evac_evs = []
            for kh in range(2):
                bts = []
                dps = list(base_deps) if kh == 0 else []
                for kk in range(2):
                    bt, bd, bi = fb_kv.take()
                    bts.append((bt, bi))
                    dps += bd
                for ch in range(8):
                    for kk in range(2):
                        kvo = kh * 2 + kk
                        bt = bts[kk][0]
                        st = pe(lambda e, ch=ch, kvo=kvo, bt=bt, tt=tt: e.matmul(
                            bt[:, 0:512], xfmbf[:, ch, tt * 128:(tt + 1) * 128],
                            wkv_sb[:, ch, kvo * 512:(kvo + 1) * 512],
                            start=(ch == 0), stop=False),
                            dps if (ch == 0 and kk == 0) else [])
                for kk in range(2):
                    kvo = kh * 2 + kk
                    bt = bts[kk][0]
                    st = pe(lambda e, kvo=kvo, bt=bt, tt=tt: e.matmul(
                        bt[:, 0:512], rows_a[0:1, tt * 128:(tt + 1) * 128],
                        colsums[0:1, kvo * 512:(kvo + 1) * 512],
                        start=False, stop=True), [])
                    mmev = kb.ev(st)
                    if kvo < 2:
                        dst = kpool[:, slot, kvo * 512:(kvo + 1) * 512]
                    else:
                        vv = vpool[:, slot, :].rearrange("p (h e) -> p h e", e=33)
                        h0 = (kvo - 2) * 16
                        dst = vv[:, h0:h0 + 16, 0:32]
                    st = act(lambda e, dst=dst, bt=bt, tt=tt: e.activation(
                        out=dst, in_=bt[:, 0:512], func=AF.Relu,
                        scale=statw[:, tt, FAM_X:FAM_X + 1]), [mmev])
                    eev = kb.ev(st)
                    evac_evs.append(eev)
                    fb_kv.done(bts[kk][1], eev)
            kv_ready[tt] = evac_evs
            if tt == 0:
                tap("k0", kpool[:, 0, :], evac_evs)
                tap("v0", vpool[:, 0, :], evac_evs)
            pump(2)
        sa, sb_ = (2 * b) % 3, (2 * b + 1) % 3
        vdeps = kv_ready[2 * b] + kv_ready[2 * b + 1]
        for hg in range(8):
            for j in range(4):
                h = 4 * hg + j
                bt = psF[j + 2]
                dps = []
                if hg == 0 and j == 0:
                    dps += vdeps
                if hg == 0:
                    dps += vk_bank_busy[j]
                st = pe(lambda e, h=h, j=j, bt=bt, sa=sa, hg=hg: e.matmul(
                    bt[32 * j:32 * (j + 1), hg * 33:(hg + 1) * 33],
                    kpool[:, sa, h * 32:(h + 1) * 32],
                    vpool[:, sa, h * 33:(h + 1) * 33],
                    start=True, stop=False, tile_position=(0, 32 * j)), dps)
                st = pe(lambda e, h=h, j=j, bt=bt, sb_=sb_, hg=hg: e.matmul(
                    bt[32 * j:32 * (j + 1), hg * 33:(hg + 1) * 33],
                    kpool[:, sb_, h * 32:(h + 1) * 32],
                    vpool[:, sb_, h * 33:(h + 1) * 33],
                    start=False, stop=True, tile_position=(0, 32 * j)), [])
        pe_b_ev = kb.ev(st)
        kv_free[sa] = [pe_b_ev]
        kv_free[sb_] = [pe_b_ev]
        for j in range(4):
            bt = psF[j + 2]
            st = dve(lambda e, j=j, bt=bt, b=b: e.tensor_copy(
                out=vksb[32 * j:32 * (j + 1), b * 264:(b + 1) * 264],
                in_=bt[32 * j:32 * (j + 1), 0:264]), [pe_b_ev])
            eev = kb.ev(st)
            vk_bank_busy[j] = [eev]
            vk_evac_evs.append(eev)
        pump(2)

    cc_test("cc4")
    for j in range(4):
        fb.evs[2 + j] = vk_bank_busy[j]
    fb.evs[0] = list(fb_kv.evs[0])
    fb.evs[1] = list(fb_kv.evs[1])

    maybe_stop("p1")
    vkin_ev = kb.dma("pool", "fp", vk_in, vksb[:, :], deps=vk_evac_evs)
    maybe_stop("p1cc0")
    if STOP_AFTER == "cc5":
        st = kb.op("pool", lambda e: e.collective_compute(
            "AllReduce", ALU.add, replica_groups=[list(range(NCORE))],
            ins=[vk_in.opt()], outs=[vk_out.opt()]), deps=[vkin_ev], sem="cc", inc=1)
        kb.ev(st)
        raise _StopBuild()
    kb.fence(extra=[vkin_ev])
    p1only.close()
    if STOP_AFTER and STOP_AFTER.startswith("cct"):
        n = int(STOP_AFTER[3:])
        specs = [("vkbf", [128, VKW], BF16), ("qch", [128, 2, T], BF16),
                 ("wqs", [128, 2, 8, 128], BF16), ("dentmp", [128, 4, 140], F32),
                 ("pib", [128, 4, 128], BF16)]
        for nm, shp, dt_ in specs[:n]:
            nc.sbuf_tensor(nm, shp, dt_).__enter__()
        st = kb.op("pool", lambda e: e.collective_compute(
            "AllReduce", ALU.add, replica_groups=[list(range(NCORE))],
            ins=[vk_in.opt()], outs=[vk_out.opt()]), deps=[vkin_ev], sem="cc", inc=1)
        kb.ev(st)
        raise _StopBuild()
    if STOP_AFTER and STOP_AFTER.startswith("ccsz"):
        nbytes = int(STOP_AFTER[4:]) * 1024
        dummy = nc.sbuf_tensor("dummy_cc9", [128, nbytes], mybir.dt.uint8)
        dummy.__enter__()
        st = kb.op("pool", lambda e: e.collective_compute(
            "AllReduce", ALU.add, replica_groups=[list(range(NCORE))],
            ins=[vk_in.opt()], outs=[vk_out.opt()]), deps=[vkin_ev], sem="cc", inc=1)
        kb.ev(st)
        raise _StopBuild()
    if STOP_AFTER == "cc6":
        st = kb.op("pool", lambda e: e.collective_compute(
            "AllReduce", ALU.add, replica_groups=[list(range(NCORE))],
            ins=[vk_in.opt()], outs=[vk_out.opt()]), deps=[vkin_ev], sem="cc", inc=1)
        kb.ev(st)
        raise _StopBuild()

    p2 = ExitStack()
    vkbf = p2.enter_context(nc.sbuf_tensor("vkbf", [128, VKW], BF16))
    qch = p2.enter_context(nc.sbuf_tensor("qch", [128, 2, T], BF16))
    wqs = p2.enter_context(nc.sbuf_tensor("wqs", [128, 2, 8, 128], BF16))
    dentmp = p2.enter_context(nc.sbuf_tensor("dentmp", [128, 4, 140], F32))
    pib = p2.enter_context(nc.sbuf_tensor("pib", [128, 4, 128], BF16))
    den_rot = Rot(kb, [dentmp[:, i, :] for i in range(4)])
    pib_rot = Rot(kb, [pib[:, i, :] for i in range(4)])

    maybe_stop("p1f")
    st = kb.op("pool", lambda e: e.collective_compute(
        "AllReduce", ALU.add, replica_groups=[list(range(NCORE))],
        ins=[vk_in.opt()], outs=[vk_out.opt()]), deps=[vkin_ev], sem="cc", inc=1)
    cc_ev = kb.ev(st)
    maybe_stop("p1cc")

    rel_evs = []
    for p_ in range(3):
        w = 2048 if p_ < 2 else VKW - 4096
        sg, sd, si = st_rot.take()
        lev = kb.dma("sync", "fr", sg[:, 0:w], vk_out[:, p_ * 2048:p_ * 2048 + w],
                     deps=[cc_ev] + sd)
        st = dve(lambda e, sg=sg, p_=p_, w=w: e.tensor_copy(
            out=vkbf[:, p_ * 2048:p_ * 2048 + w], in_=sg[:, 0:w]), [lev])
        ev_ = kb.ev(st)
        st_rot.done(si, ev_)
        rel_evs.append(ev_)
    vk_ready_ev = rel_evs[-1]
    tap("vkbf", vkbf[:, :], [vk_ready_ev])

    qch_free = [[], []]
    projin_evs = {}
    for ci in range(8):
        sl = ci % 2
        sg, sd, si = st_rot.take()
        sg_ap = sg[:, 0:1024].rearrange("p (a b) -> p a b", b=128)
        lev = kb.dma("sync", "fw", sg_ap,
                     wq1_h[:, ci * 128:(ci + 1) * 128].rearrange("(a p) q -> p a q", p=128),
                     deps=sd)
        st = dve(lambda e, sl=sl, sg_ap=sg_ap: e.tensor_copy(out=wqs[:, sl, :, :], in_=sg_ap), [lev])
        wq_ev = kb.ev(st)
        st_rot.done(si, wq_ev)
        bt, bd, bi = fb.take()
        for ch in range(8):
            st = pe(lambda e, ch=ch, bt=bt, sl=sl: e.matmul(
                bt[0:1, 0:128], onesc[:, :], wqs[:, sl, ch, :],
                start=(ch == 0), stop=(ch == 7)), ([wq_ev] + bd) if ch == 0 else [])
        mev = kb.ev(st)
        st = dve(lambda e, ci=ci, bt=bt: e.tensor_copy(
            out=colsums[0:1, 2048 + ci * 128:2048 + (ci + 1) * 128], in_=bt[0:1, 0:128]), [mev])
        csq_ev = kb.ev(st)
        fb.done(bi, csq_ev)
        qhat_evs = []
        for ts in range(NTS):
            bt, bd, bi = fb.take()
            for ch in range(8):
                dps = []
                if ch == 0:
                    dps = [csq_ev, wq_ev, negmu0_rows_done] + bd
                    if ts == 0:
                        dps += qch_free[sl]
                st = pe(lambda e, ch=ch, ts=ts, bt=bt, sl=sl: e.matmul(
                    bt[:, 0:512], wqs[:, sl, ch, :], xfmbf[:, ch, ts * 512:(ts + 1) * 512],
                    start=(ch == 0), stop=False), dps)
            st = pe(lambda e, ci=ci, ts=ts, bt=bt: e.matmul(
                bt[:, 0:512], colsums[0:1, 2048 + ci * 128:2048 + (ci + 1) * 128],
                rows_a[0:1, ts * 512:(ts + 1) * 512], start=False, stop=True), [])
            mmev = kb.ev(st)
            st = act(lambda e, sl=sl, ts=ts, bt=bt: e.activation(
                out=qch[:, sl, ts * 512:(ts + 1) * 512], in_=bt[:, 0:512], func=AF.Relu), [mmev])
            eev = kb.ev(st)
            qhat_evs.append(eev)
            fb.done(bi, eev)
        qhat_ev = qhat_evs[-1]
        if ci == 0:
            tap("qh0", qch[:, 0, :], [qhat_ev])
        last_vkq = None
        for tt in range(NTT):
            b = tt // 2
            jb = []
            for j in range(4):
                bt, bd, bi = fb.take()
                st = pe(lambda e, j=j, tt=tt, b=b, ci=ci, sl=sl, bt=bt: e.matmul(
                    bt[:, 0:33],
                    qch[32 * j:32 * (j + 1), sl, tt * 128:(tt + 1) * 128],
                    vkbf[32 * j:32 * (j + 1), (b * 8 + ci) * 33:(b * 8 + ci + 1) * 33],
                    start=True, stop=True, tile_position=(32 * j, 0)),
                    [qhat_ev, vk_ready_ev] + bd)
                jb.append((bt, kb.ev(st), bi))
                last_vkq = st
            dap, dd, di = den_rot.take()
            con_evs = []
            for j in range(4):
                btj, mmev, bij = jb[j]
                st = act(lambda e, j=j, btj=btj, dap=dap: e.copy(
                    out=dap[:, j * 33:(j + 1) * 33], in_=btj[:, 0:33]),
                    [mmev] + (dd if j == 0 else []))
                cev = kb.ev(st)
                con_evs.append(cev)
                fb.done(bij, cev)
            dv = dap[:, 0:132].rearrange("p (j e) -> p j e", e=33)
            st = dve(lambda e, dap=dap, dv=dv: e.tensor_scalar_add(
                dap[:, 132:136].rearrange("p (j o) -> p j o", o=1), dv[:, :, 32:33], EPS_ATTN),
                con_evs)
            st = dve(lambda e, dap=dap: e.reciprocal(
                out=dap[:, 136:140].rearrange("p (j o) -> p j o", o=1),
                in_=dap[:, 132:136].rearrange("p (j o) -> p j o", o=1)), [kb.ev(st)])
            rec_ev = kb.ev(st)
            rec_bc = bc_last(dap[:, 136:140], 32)
            pa, pad, pai = pib_rot.take()
            dst = pa.rearrange("p (j d) -> p j d", d=32)
            st = pool(lambda e, dv=dv, rec_bc=rec_bc, dst=dst: e.tensor_tensor(
                out=dst, in0=dv[:, :, 0:32], in1=rec_bc, op=ALU.mult), [rec_ev] + pad)
            dev_ = kb.ev(st)
            den_rot.done(di, dev_)
            spv = kb.dma("pool", "fp", projin_sp[tt, :, ci * 128:(ci + 1) * 128], pa,
                         deps=[dev_])
            pib_rot.done(pai, spv)
            projin_evs[(ci, tt)] = spv
        qch_free[sl] = [kb.ev(last_vkq)]
        pump(4)

    maybe_stop("p2")
    if "projin0" in dbg:
        kb.dma("pool", "fo", dbg["projin0"], projin_sp[0, :, :], deps=[projin_evs[(7, 0)]])
    kb.fence()
    p2.close()
    p12.close()

    p3 = ExitStack()
    wp1_sb = p3.enter_context(nc.sbuf_tensor("wp1sb", [128, 8, C], BF16))
    pit = p3.enter_context(nc.sbuf_tensor("pit", [128, 4, C], BF16))
    stage3 = p3.enter_context(nc.sbuf_tensor("stage3", [128, 4, 1024], F32))
    pif = p3.enter_context(nc.sbuf_tensor("pif", [128, 2, 8, 128], BF16))
    t1buf = p3.enter_context(nc.sbuf_tensor("t1buf", [128, 2, C], F32))
    t1bf = p3.enter_context(nc.sbuf_tensor("t1bf", [128, 2, C], BF16))
    t1fm = p3.enter_context(nc.sbuf_tensor("t1fm", [128, 2, 8, 128], BF16))
    x1tb = p3.enter_context(nc.sbuf_tensor("x1tb", [128, 3, C], F32))
    x1fb = p3.enter_context(nc.sbuf_tensor("x1fb", [128, 3, 8, 128], F32))
    xsl3 = p3.enter_context(nc.sbuf_tensor("xsl3", [128, 4, C], F32))

    st3_rot = Rot(kb, [stage3[:, i, :] for i in range(4)])
    pif_rot = Rot(kb, [pif[:, 0], pif[:, 1]])
    t1_rot = Rot(kb, [t1buf[:, 0, :], t1buf[:, 1, :]])
    t1bf_rot = Rot(kb, [t1bf[:, 0, :], t1bf[:, 1, :]])
    t1fm_rot = Rot(kb, [t1fm[:, 0], t1fm[:, 1]])
    x1t_rot = Rot(kb, [x1tb[:, i, :] for i in range(3)])
    x1f_rot = Rot(kb, [x1fb[:, i] for i in range(3)])
    xsl3_rot = Rot(kb, [xsl3[:, i, :] for i in range(4)])
    pit_rot = Rot(kb, [pit[:, i, :] for i in range(4)])

    wp1_evs = []
    for ch in range(8):
        sg, sd, si = st3_rot.take()
        lev = kb.dma("sync", "fw", sg, wp1_h[ch * 128:(ch + 1) * 128, :], deps=sd)
        st = dve(lambda e, sg=sg, ch=ch: e.tensor_copy(out=wp1_sb[:, ch, :], in_=sg), [lev])
        ev_ = kb.ev(st)
        st3_rot.done(si, ev_)
        wp1_evs.append(ev_)
    wp1_done = wp1_evs[-1]
    sgb, sdb, sib = st3_rot.take()
    lev = kb.dma("sync", "fw", sgb[0:1, 0:1024], bp1_h[:, :], deps=sdb)
    st = dve(lambda e, sgb=sgb: e.tensor_copy(out=colsums[0:1, 3072:4096], in_=sgb[0:1, 0:1024]),
             [lev])
    bias_ev = kb.ev(st)
    st3_rot.done(sib, bias_ev)

    def p3_stageA(tt):
        pta, ptd_, pti_ = pit_rot.take()
        plev = kb.dma("sync", "fr", pta, projin_sp[tt, :, :], deps=ptd_)
        pf, pfd, pfi = pif_rot.take()
        bt, bd, bi = bb.take()
        for ch in range(8):
            st = pe(lambda e, ch=ch, bt=bt, pta=pta: e.transpose(
                bt[:, ch * 128:(ch + 1) * 128], pta[:, ch * 128:(ch + 1) * 128],
                iden_sb[:, :]), ([plev] + bd + pfd) if ch == 0 else [])
        tev = kb.ev(st)
        pit_rot.done(pti_, tev)
        st = dve(lambda e, bt=bt, pf=pf: e.tensor_copy(
            out=pf.rearrange("p a b -> p (a b)"), in_=bt[:, 0:1024]), [tev])
        pif_ev = kb.ev(st)
        bb.done(bi, pif_ev)
        ybanks = []
        for oh in range(2):
            btY, bdY, biY = fb.take()
            for ch in range(8):
                st = pe(lambda e, ch=ch, oh=oh, btY=btY, pf=pf: e.matmul(
                    btY[:, 0:512], pf[:, ch, :], wp1_sb[:, ch, oh * 512:(oh + 1) * 512],
                    start=(ch == 0), stop=False),
                    ([pif_ev, wp1_done, bias_ev] + bdY) if ch == 0 else [])
            st = pe(lambda e, oh=oh, btY=btY: e.matmul(
                btY[:, 0:512], ones4[0:1, 0:128],
                colsums[0:1, 3072 + oh * 512:3072 + (oh + 1) * 512],
                start=False, stop=True), [])
            ybanks.append([btY, kb.ev(st), biY])
        pif_rot.done(pfi, [ybanks[0][1], ybanks[1][1]])
        return dict(tt=tt, ybanks=ybanks)

    def p3_stageB(sa):
        tt = sa["tt"]
        ybanks = sa["ybanks"]
        yevs = ln_stats([(ybanks[0][0][:, 0:512], 512), (ybanks[1][0][:, 0:512], 512)],
                        tt, FAM_Y, [ybanks[0][1], ybanks[1][1]])
        t1a, t1d, t1i = t1_rot.take()
        t1evs = []
        for oh in range(2):
            btY = ybanks[oh][0]
            st = dve(lambda e, btY=btY, oh=oh, t1a=t1a, tt=tt: e.tensor_scalar(
                out=t1a[:, oh * 512:(oh + 1) * 512], in0=btY[:, 0:512],
                scalar1=statw[:, tt, FAM_Y + 1:FAM_Y + 2],
                scalar2=statw[:, tt, FAM_Y:FAM_Y + 1],
                op0=ALU.add, op1=ALU.mult),
                [yevs["r"], yevs["negmu"], ybanks[oh][1]] + (t1d if oh == 0 else []))
            ev_ = kb.ev(st)
            t1evs.append(ev_)
            fb.done(ybanks[oh][2], ev_)
        xs, xsd, xsi = xsl3_rot.take()
        xlev = kb.dma("sync", "fx", xs, x_tm_h[tt * 128:(tt + 1) * 128, :], deps=xsd)
        x1a, x1d, x1i = x1t_rot.take()
        st = dve(lambda e, x1a=x1a, xs=xs, t1a=t1a: e.tensor_add(x1a, xs, t1a),
                 t1evs + [xlev] + x1d)
        x1ev = kb.ev(st)
        xsl3_rot.done(xsi, x1ev)
        spev = kb.dma("pool", "fp", x1tm_sp[tt, :, :], x1a, deps=[x1ev])
        if tt == 0:
            tap("y0", t1a, t1evs)
            tap("x1t0", x1a, [x1ev])
        x1evs = ln_stats([(x1a[:, 0:512], 512), (x1a[:, 512:1024], 512)],
                         tt, FAM_X1, [x1ev])
        x1t_rot.done(x1i, [spev, x1evs["r"]])
        s0 = statw[:, tt, FAM_X1 + 0:FAM_X1 + 1]
        s3 = statw[:, tt, FAM_X1 + 3:FAM_X1 + 4]
        s4 = statw[:, tt, FAM_X1 + 4:FAM_X1 + 5]
        s6 = statw[:, tt, FAM_X1 + 6:FAM_X1 + 7]
        s8 = statw[:, tt, FAM_X1 + 8:FAM_X1 + 9]
        va_ = x1evs["var_ap"]
        st = pool(lambda e, s3=s3, va_=va_, s0=s0: e.tensor_mul(s3, va_, s0),
                  [x1evs["var"], x1evs["r"]])
        st = pool(lambda e, s3=s3, s0=s0: e.tensor_mul(s3, s3, s0), [kb.ev(st)])
        v1ev = kb.ev(st)
        st = act(lambda e, s4=s4, s3=s3: e.activation(out=s4, in_=s3, func=AF.Ln,
                                                      bias=cknst[:, 0:1]), [v1ev])
        st = act(lambda e, s6=s6, s4=s4: e.activation(out=s6, in_=s4, func=AF.Exp,
                                                      scale=cknst[:, 1:2]), [kb.ev(st)])
        r1ev = kb.ev(st)
        st = pool(lambda e, s8=s8, s6=s6, s0=s0: e.tensor_mul(s8, s6, s0), [r1ev])
        aev = kb.ev(st)
        make_row(statw[:, tt, FAM_X1 + 1:FAM_X1 + 2], rows_a, 32, tt * 128,
                 [x1evs["negmu"]])
        make_row(s8, rows_a, 64, tt * 128, [aev])
        make_row(s0, rows_b, 0, tt * 128, [x1evs["r"]])
        tb, tbd, tbi = t1bf_rot.take()
        st = dve(lambda e, tb=tb, t1a=t1a: e.tensor_copy(out=tb, in_=t1a), t1evs + tbd)
        tbev = kb.ev(st)
        t1_rot.done(t1i, [x1ev, tbev])
        tf, tfd, tfi = t1fm_rot.take()
        bt2, bd2, bi2 = bb.take()
        for ch in range(8):
            st = pe(lambda e, ch=ch, bt2=bt2, tb=tb: e.transpose(
                bt2[:, ch * 128:(ch + 1) * 128], tb[:, ch * 128:(ch + 1) * 128], iden_sb[:, :]),
                ([tbev] + bd2) if ch == 0 else [])
        tev2 = kb.ev(st)
        t1bf_rot.done(tbi, tev2)
        st = dve(lambda e, bt2=bt2, tf=tf: e.tensor_copy(
            out=tf.rearrange("p a b -> p (a b)"), in_=bt2[:, 0:1024]), [tev2] + tfd)
        tfev = kb.ev(st)
        bb.done(bi2, tfev)
        x1f, x1fd, x1fi = x1f_rot.take()
        sgx, sgd, sgi = st3_rot.take()
        sgx_v = sgx.rearrange("p (a b) -> p a b", b=128)
        xcl = kb.dma("sync", "fx", sgx_v,
                     x_fm_h[:, tt * 128:(tt + 1) * 128].rearrange("(a p) q -> p a q", p=128),
                     deps=sgd)
        st = dve(lambda e, x1f=x1f, sgx_v=sgx_v, tf=tf: e.tensor_add(
            x1f.rearrange("p a b -> p (a b)"), sgx_v.rearrange("p a b -> p (a b)"),
            tf.rearrange("p a b -> p (a b)")), [xcl, tfev] + x1fd)
        x1fev = kb.ev(st)
        st3_rot.done(sgi, x1fev)
        t1fm_rot.done(tfi, x1fev)
        dst = x1fm_sp[:, :, tt * 128:(tt + 1) * 128].rearrange("a p t -> p a t")
        spev2 = kb.dma("pool", "fp", dst, x1f[:, :, :], deps=[x1fev])
        x1f_rot.done(x1fi, spev2)
        pump(4)

    p3_pend = None
    for tt in range(P3_TILES):
        sa = p3_stageA(tt)
        if p3_pend is not None:
            p3_stageB(p3_pend)
        p3_pend = sa
    p3_stageB(p3_pend)

    x1_rows_done = kb.tail("dve")
    if "statw" in dbg:
        kb.dma("pool", "fo", dbg["statw"], statw[:, :, :].rearrange("p a b -> p (a b)"),
               deps=[x1_rows_done])
    if "rows_a" in dbg:
        kb.dma("pool", "fo", dbg["rows_a"], rows_a[:, :], deps=[x1_rows_done])
    if "rows_b" in dbg:
        kb.dma("pool", "fo", dbg["rows_b"], rows_b[:, :], deps=[x1_rows_done])
    maybe_stop("p3")
    kb.fence()
    p3.close()

    p4a = ExitStack()
    qin = p4a.enter_context(nc.sbuf_tensor("qin", [128, 8, T], BF16))
    wq2_sb = p4a.enter_context(nc.sbuf_tensor("wq2sb", [128, 8, 1536], BF16))
    stage4 = p4a.enter_context(nc.sbuf_tensor("stage4", [128, 2, 1536], F32))
    u4 = p4a.enter_context(nc.sbuf_tensor("u4", [128, 2, 512], F32))
    v4 = p4a.enter_context(nc.sbuf_tensor("v4", [128, 2, 512], F32))
    q2stg = p4a.enter_context(nc.sbuf_tensor("q2stg", [128, 2, T], BF16))
    st4_rot = Rot(kb, [stage4[:, 0, :], stage4[:, 1, :]])
    u4_rot = Rot(kb, [u4[:, 0, :], u4[:, 1, :]])
    v4_rot = Rot(kb, [v4[:, 0, :], v4[:, 1, :]])
    cb_rot = Rot(kb, [q2stg[:, 0, :], q2stg[:, 1, :]])

    wq2_evs = []
    for ch in range(8):
        sg, sd, si = st4_rot.take()
        lev = kb.dma("sync", "fw", sg, wq2_h[ch * 128:(ch + 1) * 128, :], deps=sd)
        st = dve(lambda e, sg=sg, ch=ch: e.tensor_copy(out=wq2_sb[:, ch, :], in_=sg), [lev])
        ev_ = kb.ev(st)
        st4_rot.done(si, ev_)
        wq2_evs.append(ev_)
    wq2_done = wq2_evs[-1]
    if P4_SUB == "w":
        maybe_stop("p4a")

    qin_evs = []
    for ts in range(NTS):
        bmu, bmu_ev, bmu_i = bcast(rows_a, 32, ts * 512, 512, [x1_rows_done])
        ba, ba_ev, ba_i = bcast(rows_a, 64, ts * 512, 512, [x1_rows_done])
        last = None
        for ch in range(8):
            sg, sd, si = st4_rot.take()
            lev = kb.dma("sync", "fr", sg[:, 0:512], x1fm_sp[ch, :, ts * 512:(ts + 1) * 512],
                         deps=sd)
            ua, ud, ui = u4_rot.take()
            st = dve(lambda e, ua=ua, sg=sg, bmu=bmu: e.tensor_add(ua, sg[:, 0:512],
                                                                   bmu[:, 0:512]),
                     [lev, bmu_ev] + ud)
            uev = kb.ev(st)
            st4_rot.done(si, uev)
            va, vd, vi = v4_rot.take()
            st = dve(lambda e, va=va, ua=ua, ch=ch, ba=ba: e.scalar_tensor_tensor(
                out=va, in0=ua, scalar=lncol[:, ch, 0:1], in1=ba[:, 0:512],
                op0=ALU.mult, op1=ALU.mult), [uev, ba_ev] + vd)
            vev = kb.ev(st)
            u4_rot.done(ui, vev)
            st = dve(lambda e, va=va, ch=ch, ts=ts: e.tensor_scalar_add(
                qin[:, ch, ts * 512:(ts + 1) * 512], va, lncol[:, ch, 1:2]), [vev])
            qev = kb.ev(st)
            v4_rot.done(vi, qev)
            qin_evs.append(qev)
            last = qev
        fb.done(bmu_i, last)
        fb.done(ba_i, last)
        pump(1)
    qin_done = qin_evs[-1]
    tap("qin0", qin[:, 0, :], [qin_done])
    if P4_SUB == "qin":
        maybe_stop("p4a")

    def qkv2_block(och, dram, didx):
        cb, cbd, cbi = cb_rot.take()
        cb_v = cb.rearrange("p (n b) -> p b n", b=16)
        evac_evs = []
        for tsp in range(4):
            bts = []
            dps = [qin_done, wq2_done] + (cbd if tsp == 0 else [])
            for kk in range(2):
                bt, bd, bi = fb.take()
                bts.append([bt, bi])
                dps += bd
            for ch in range(8):
                for kk in range(2):
                    ts = tsp * 2 + kk
                    bt = bts[kk][0]
                    st = pe(lambda e, ch=ch, ts=ts, bt=bt, och=och: e.matmul(
                        bt[:, 0:512], wq2_sb[:, ch, och * 128:(och + 1) * 128],
                        qin[:, ch, ts * 512:(ts + 1) * 512],
                        start=(ch == 0), stop=(ch == 7)),
                        dps if (ch == 0 and kk == 0) else [])
            tailev = kb.ev(st)
            for kk in range(2):
                ts = tsp * 2 + kk
                bt, bi = bts[kk]
                dst = cb_v[:, 2 * ts:2 * ts + 2, :]
                srcv = bt[:, 0:512].rearrange("p (b n) -> p b n", n=256)
                st = act(lambda e, dst=dst, srcv=srcv: e.copy(out=dst, in_=srcv), [tailev])
                eev = kb.ev(st)
                fb.done(bi, eev)
                evac_evs.append(eev)
        spv = kb.dma("pool", "fp", dram[didx, :, :], cb, deps=evac_evs)
        cb_rot.done(cbi, spv)
        return [spv]

    k2_evs = qkv2_block(8, k2_sp, 0) + qkv2_block(9, k2_sp, 1)
    if "cb0" in dbg:
        kb.dma("pool", "fo", dbg["cb0"], q2stg[:, 0, :], deps=k2_evs)
    if P4_SUB == "kv":
        maybe_stop("p4a")
    v2_evs = qkv2_block(10, v2_sp, 0) + qkv2_block(11, v2_sp, 1)
    if P4_SUB == "v":
        maybe_stop("p4a")
    q2sp_evs = []
    for och in range(8):
        q2sp_evs += qkv2_block(och, q2_sp, och)
        pump(2)
        if P4_SUB == "q%d" % och:
            maybe_stop("p4a")
    assert len(q2sp_evs) == 8
    maybe_stop("p4a")

    kb.fence()
    p4a.close()

    p45 = ExitStack()
    att_fm = p45.enter_context(nc.sbuf_tensor("attfm", [128, 8, T], BF16))
    p4b = ExitStack()
    qc2 = p4b.enter_context(nc.sbuf_tensor("qc2", [128, 2, T], BF16))
    v2w = p4b.enter_context(nc.sbuf_tensor("v2w", [128, T], BF16))
    k2dup = p4b.enter_context(nc.sbuf_tensor("k2dup", [128, T], BF16))
    v2t = p4b.enter_context(nc.sbuf_tensor("v2t", [128, KVH, 2048], BF16))
    scb = p4b.enter_context(nc.sbuf_tensor("scb", [128, 4, 512], F32))
    prb = p4b.enter_context(nc.sbuf_tensor("prb", [128, 4, 512], BF16))
    ptb = p4b.enter_context(nc.sbuf_tensor("ptb", [128, 3, 512], BF16))
    a64b = p4b.enter_context(nc.sbuf_tensor("a64b", [128, 3, 256], BF16))
    sxb = p4b.enter_context(nc.sbuf_tensor("sxb", [128, 4, 12], F32))
    mask_sb = p4b.enter_context(nc.sbuf_tensor("masksb", [128, 512], F32))
    mask_ld = kb.dma("sync", "fc", mask_sb[:, :], mask_h[:, :])
    sc_rot = Rot(kb, [scb[:, i, :] for i in range(4)])
    pr_rot = Rot(kb, [prb[:, i, :] for i in range(4)])
    pt_rot = Rot(kb, [ptb[:, i, :] for i in range(3)])
    a6_rot = Rot(kb, [a64b[:, i, :] for i in range(3)])
    sx_rot = Rot(kb, [sxb[:, i, :] for i in range(4)])

    v2t_evs = []
    v2w_last = []
    for half in range(2):
        wlev = kb.dma("sync", "fr", v2w[:, :], v2_sp[half, :, :],
                      deps=list(v2_evs) + v2w_last)
        if half == 0 and "v2w0" in dbg:
            kb.dma("pool", "fo", dbg["v2w0"], v2w[:, :], deps=[wlev])
        for kvh in (2 * half, 2 * half + 1):
            base = (kvh % 2) * 64
            for g in range(NTT):
                bt, bd, bi = bb.take()
                src = v2w[base:base + 64, g * 128:(g + 1) * 128]
                idn = iden_sb[base:base + 64, base:base + 64]
                st = pe(lambda e, src=src, bt=bt, idn=idn: e.transpose(
                    bt[0:128, 0:64], src, idn), [wlev] + bd)
                tev = kb.ev(st)
                st = dve(lambda e, bt=bt, kvh=kvh, g=g: e.tensor_copy(
                    out=v2t[:, kvh, g * 64:(g + 1) * 64], in_=bt[0:128, 0:64]), [tev])
                eev = kb.ev(st)
                bb.done(bi, eev)
                v2t_evs.append(eev)
            pump(2)
        v2w_last = [kb.tail("pe")]
    v2t_done = v2t_evs[-1]
    if "v2t" in dbg:
        kb.dma("pool", "fo", dbg["v2t"], v2t[:, 0, :], deps=[v2t_done])
    if P4B_SUB == "v2t":
        maybe_stop("p4b")

    rotA = Rot(kb, [psB[0][:, 0:512], psB[0][:, 512:1024]])
    rotB = Rot(kb, [psB[1][:, 0:512], psB[1][:, 512:1024]])
    rotA.evs = [list(bb.evs[0]), list(bb.evs[0])]
    rotB.evs = [list(bb.evs[1]), list(bb.evs[1])]

    cur = {"kvh": -1, "k2dup_ready": None, "last_qk": None, "qlev": {}}
    qc2_free = [[], []]
    att_evs = []
    stripes = [(och, hh, gq) for och in range(8) for hh in range(2)
               for gq in range(8)]

    def p4b_stageA(idx):
        och, hh, gq = stripes[idx]
        sl = och % 2
        base = hh * 64
        kvh = och // 2
        if hh == 0 and gq == 0:
            cur["qlev"][och] = kb.dma("sync", "fr", qc2[:, sl, :], q2_sp[och, :, :],
                                      deps=[q2sp_evs[och]] + qc2_free[sl])
            if kvh != cur["kvh"]:
                cur["kvh"] = kvh
                src_half = k2_sp[kvh // 2, (kvh % 2) * 64:(kvh % 2) * 64 + 64, :]
                dd = list(k2_evs)
                if cur["last_qk"] is not None:
                    dd.append(cur["last_qk"])
                e1 = kb.dma("sync", "fr", k2dup[0:64, :], src_half, deps=dd)
                e2 = kb.dma("sync", "fr", k2dup[64:128, :], src_half, deps=[e1])
                cur["k2dup_ready"] = e2
        sc, scd, sci = sc_rot.take()
        sx, sxd, sxi = sx_rot.take()
        btq, bqd, bqi = fb.take()
        tp = (base, 0) if base else None
        deps0 = [cur["qlev"][och], cur["k2dup_ready"], v2t_done] + bqd
        for j in range(4):
            g = gq * 4 + j
            st = pe(lambda e, j=j, g=g, btq=btq, sl=sl, base=base, tp=tp: e.matmul(
                btq[:, j * 128:(j + 1) * 128],
                qc2[base:base + 64, sl, g * 128:(g + 1) * 128],
                k2dup[base:base + 64, g * 128:(g + 1) * 128],
                start=True, stop=True, tile_position=tp), deps0 if j == 0 else [])
        qkev = kb.ev(st)
        cur["last_qk"] = qkev
        if hh == 1 and gq == 7:
            qc2_free[sl] = [qkev]
        st = dve(lambda e, sc=sc, btq=btq: e.scalar_tensor_tensor(
            out=sc, in0=btq[:, 0:512], scalar=0.125, in1=mask_sb[:, :],
            op0=ALU.mult, op1=ALU.add), [qkev, mask_ld] + scd)
        mev = kb.ev(st)
        fb.done(bqi, mev)
        mxevs = []
        for j in range(4):
            st = dve(lambda e, sc=sc, sx=sx, j=j: e.tensor_reduce(
                out=sx[:, j:j + 1], in_=sc[:, j * 128:(j + 1) * 128], axis=AXL.X,
                op=ALU.max, negate=True), ([mev] + sxd) if j == 0 else [])
            mxevs.append(kb.ev(st))
        pr, prd, pri = pr_rot.take()
        exevs = []
        for j in range(4):
            st = act(lambda e, pr=pr, sc=sc, sx=sx, j=j: e.activation(
                out=pr[:, j * 128:(j + 1) * 128], in_=sc[:, j * 128:(j + 1) * 128],
                func=AF.Exp, bias=sx[:, j:j + 1], accum_out=sx[:, 4 + j:5 + j]),
                [mxevs[j]] + (prd if j == 0 else []))
            exevs.append(kb.ev(st))
        sc_rot.done(sci, exevs[-1])
        st = dve(lambda e, sx=sx: e.reciprocal(out=sx[:, 8:12], in_=sx[:, 4:8]),
                 [exevs[-1]])
        rcev = kb.ev(st)
        return dict(och=och, gq=gq, base=base, kvh=kvh, pr=pr, pri=pri,
                    sx=sx, sxi=sxi, exev=exevs[-1], rcev=rcev)

    def p4b_stageB(s):
        base, kvh, gq, och = s["base"], s["kvh"], s["gq"], s["och"]
        btA, adp, ai = rotA.take()
        for j in range(4):
            st = pe(lambda e, btA=btA, pr=s["pr"], j=j: e.transpose(
                btA[:, j * 128:(j + 1) * 128], pr[:, j * 128:(j + 1) * 128],
                iden_sb[:, :]), ([s["exev"]] + adp) if j == 0 else [])
        tprev = kb.ev(st)
        pr_rot.done(s["pri"], tprev)
        pt, ptd, pti = pt_rot.take()
        st = act(lambda e, pt=pt, btA=btA: e.copy(out=pt, in_=btA[:, 0:512]),
                 [tprev] + ptd)
        ptcev = kb.ev(st)
        rotA.done(ai, ptcev)
        btv, bvd, bvi = fb.take()
        for j in range(4):
            g = gq * 4 + j
            st = pe(lambda e, btv=btv, pt=pt, kvh=kvh, g=g, j=j: e.matmul(
                btv[:, j * 64:(j + 1) * 64], pt[:, j * 128:(j + 1) * 128],
                v2t[:, kvh, g * 64:(g + 1) * 64], start=True, stop=True),
                ([ptcev] + bvd) if j == 0 else [])
        avev = kb.ev(st)
        pt_rot.done(pti, avev)
        a6, a6d, a6i = a6_rot.take()
        scevs = []
        for j in range(4):
            st = act(lambda e, a6=a6, btv=btv, sx=s["sx"], j=j: e.activation(
                out=a6[:, j * 64:(j + 1) * 64], in_=btv[:, j * 64:(j + 1) * 64],
                func=AF.Copy, scale=sx[:, 8 + j:9 + j]),
                ([avev, s["rcev"]] + a6d) if j == 0 else [])
            scevs.append(kb.ev(st))
        fb.done(bvi, scevs[-1])
        sx_rot.done(s["sxi"], scevs[-1])
        btB, bbd, bi2 = rotB.take()
        tp2 = (0, base) if base else None
        for j in range(4):
            st = pe(lambda e, btB=btB, a6=a6, j=j, base=base, tp2=tp2: e.transpose(
                btB[base:base + 64, j * 128:(j + 1) * 128], a6[:, j * 64:(j + 1) * 64],
                iden_sb[:, :], tile_position=tp2), ([scevs[-1]] + bbd) if j == 0 else [])
        t2ev = kb.ev(st)
        a6_rot.done(a6i, t2ev)
        dst = att_fm[:, och, :].rearrange("p (b n) -> p b n", n=256)[:, :, gq * 32:(gq + 1) * 32]
        dst = dst.rearrange("p b n -> p n b")[base:base + 64]
        src = btB[base:base + 64, 0:512].rearrange("p (n b) -> p n b", b=16)
        st = dve(lambda e, dst=dst, src=src: e.tensor_copy(out=dst, in_=src), [t2ev])
        fev = kb.ev(st)
        rotB.done(bi2, fev)
        att_evs.append(fev)

    pend = []
    for idx in range(len(stripes)):
        pend.append(p4b_stageA(idx))
        if len(pend) > 2:
            p4b_stageB(pend.pop(0))
        if idx % 4 == 0:
            pump(1)
    for s_ in pend:
        p4b_stageB(s_)
    att_done = att_evs[-1]
    tap("attfm0", att_fm[:, 0, :], [att_done])
    maybe_stop("p4b")
    kb.fence()
    p4b.close()

    p5 = ExitStack()
    wp2_sb = p5.enter_context(nc.sbuf_tensor("wp2sb", [128, 8, C], BF16))
    stage5 = p5.enter_context(nc.sbuf_tensor("stage5", [128, 3, 1024], F32))
    u5 = p5.enter_context(nc.sbuf_tensor("u5", [128, 2, 512], F32))
    v5 = p5.enter_context(nc.sbuf_tensor("v5", [128, 2, 512], F32))
    h1f = p5.enter_context(nc.sbuf_tensor("h1f", [128, 8, 512], F32))
    h1bfb = p5.enter_context(nc.sbuf_tensor("h1bfb", [128, 2, 1024], BF16))
    h1tmb = p5.enter_context(nc.sbuf_tensor("h1tmb", [128, 2, 1024], BF16))
    xmstg = p5.enter_context(nc.sbuf_tensor("xmstg", [128, 2, 512], BF16))
    st5_rot = Rot(kb, [stage5[:, i, :] for i in range(3)])
    u5_rot = Rot(kb, [u5[:, 0, :], u5[:, 1, :]])
    v5_rot = Rot(kb, [v5[:, 0, :], v5[:, 1, :]])
    h1bf_rot = Rot(kb, [h1bfb[:, 0, :], h1bfb[:, 1, :]])
    h1tm_rot = Rot(kb, [h1tmb[:, 0, :], h1tmb[:, 1, :]])
    xm_rot = Rot(kb, [xmstg[:, 0, :], xmstg[:, 1, :]])

    wp2_evs = []
    for ch in range(8):
        sg, sd, si = st5_rot.take()
        lev = kb.dma("sync", "fw", sg, wp2_h[ch * 128:(ch + 1) * 128, :], deps=sd)
        st = dve(lambda e, sg=sg, ch=ch: e.tensor_copy(out=wp2_sb[:, ch, :], in_=sg), [lev])
        ev_ = kb.ev(st)
        st5_rot.done(si, ev_)
        wp2_evs.append(ev_)
    wp2_done = wp2_evs[-1]

    prev_bc2 = []
    xm_sp_evs = {}
    h1_free = [[] for _ in range(8)]
    fb5 = Rot(kb, psF[0:4])
    bcA, bcB = psF[4], psF[5]
    bc_prev = [list(fb.evs[4]), list(fb.evs[5])]

    def bcast_at(bank, row_tensor, row_base, col0, n, deps):
        tp = (row_base, 0) if row_base else None
        st = pe(lambda e: e.matmul(bank[:, 0:n], ones4[row_base:row_base + 1, 0:128],
                                   row_tensor[row_base:row_base + 1, col0:col0 + n],
                                   start=True, stop=True, tile_position=tp), list(deps))
        return kb.ev(st)

    for ts in range(NTS):
        bmu_ev = bcast_at(bcA, rows_a, 32, ts * 512, 512, [x1_rows_done] + bc_prev[0])
        br0_ev = bcast_at(bcB, rows_b, 0, ts * 512, 512, [x1_rows_done] + bc_prev[1])
        bmu, br0 = bcA, bcB
        h1f_evs = []
        last_h1 = None
        for och in range(8):
            btp, bdp, bip = fb5.take()
            for ch in range(8):
                st = pe(lambda e, ch=ch, och=och, ts=ts, btp=btp: e.matmul(
                    btp[:, 0:512], wp2_sb[:, ch, och * 128:(och + 1) * 128],
                    att_fm[:, ch, ts * 512:(ts + 1) * 512],
                    start=(ch == 0), stop=(ch == 7)),
                    ([wp2_done, att_done] + bdp) if ch == 0 else [])
            p2ev = kb.ev(st)
            sg, sd, si = st5_rot.take()
            lev = kb.dma("sync", "fr", sg[:, 0:512], x1fm_sp[och, :, ts * 512:(ts + 1) * 512],
                         deps=sd)
            ua, ud, ui = u5_rot.take()
            st = dve(lambda e, ua=ua, sg=sg, bmu=bmu: e.tensor_add(ua, sg[:, 0:512],
                                                                   bmu[:, 0:512]),
                     [lev, bmu_ev] + ud)
            uev = kb.ev(st)
            st5_rot.done(si, uev)
            va, vd, vi = v5_rot.take()
            st = dve(lambda e, va=va, ua=ua, br0=br0: e.tensor_mul(va, ua, br0[:, 0:512]),
                     [uev, br0_ev] + vd)
            wev = kb.ev(st)
            u5_rot.done(ui, wev)
            st = dve(lambda e, va=va, btp=btp, och=och: e.tensor_add(
                h1f[:, och, :], va, btp[:, 0:512]), [wev, p2ev] + h1_free[och])
            h1ev = kb.ev(st)
            v5_rot.done(vi, h1ev)
            fb5.done(bip, h1ev)
            h1f_evs.append(h1ev)
            last_h1 = h1ev
        row2_evs = []
        h1tm_spill_evs = []
        for ttl in range(4):
            tt = ts * 4 + ttl
            hb, hbd, hbi = h1bf_rot.take()
            st = dve(lambda e, hb=hb, ttl=ttl: e.tensor_copy(
                out=hb.rearrange("p (a b) -> p a b", b=128),
                in_=h1f[:, :, ttl * 128:(ttl + 1) * 128]), h1f_evs + hbd)
            cbev = kb.ev(st)
            btx, bdx, bix = bb.take()
            for ch in range(8):
                st = pe(lambda e, ch=ch, btx=btx, hb=hb: e.transpose(
                    btx[:, ch * 128:(ch + 1) * 128], hb[:, ch * 128:(ch + 1) * 128],
                    iden_sb[:, :]), ([cbev] + bdx) if ch == 0 else [])
            tev = kb.ev(st)
            h1bf_rot.done(hbi, tev)
            ht, htd, hti = h1tm_rot.take()
            st = dve(lambda e, ht=ht, btx=btx: e.tensor_copy(out=ht, in_=btx[:, 0:1024]),
                     [tev] + htd)
            htev = kb.ev(st)
            bb.done(bix, htev)
            l2 = ln_stats([(ht[:, 0:512], 512), (ht[:, 512:1024], 512)],
                          tt, FAM_LN2, [htev])
            r1 = make_row(statw[:, tt, FAM_LN2 + 1:FAM_LN2 + 2], rows_b, 32, ttl * 128,
                          [l2["negmu"]] + prev_bc2)
            r2_ = make_row(statw[:, tt, FAM_LN2:FAM_LN2 + 1], rows_b, 64, ttl * 128,
                           [l2["r"]] + prev_bc2)
            row2_evs += [r1, r2_]
            spv = kb.dma("pool", "fp", h1tm_sp[tt, :, :], ht, deps=[htev])
            h1tm_rot.done(hti, [spv, l2["r"]])
            if tt == 0:
                tap("h1t0", ht, [htev])
        bm2_ev = bcast_at(bcA, rows_b, 32, 0, 512, row2_evs + [last_h1])
        br2_ev = bcast_at(bcB, rows_b, 64, 0, 512, row2_evs + [last_h1])
        bm2, br2 = bcA, bcB
        prev_bc2 = [bm2_ev, br2_ev]
        last_xm = None
        for ch in range(8):
            ua, ud, ui = u5_rot.take()
            st = dve(lambda e, ua=ua, ch=ch, bm2=bm2: e.tensor_add(
                ua, h1f[:, ch, :], bm2[:, 0:512]), [bm2_ev] + h1f_evs + ud)
            uev = kb.ev(st)
            va, vd, vi = v5_rot.take()
            st = dve(lambda e, va=va, ua=ua, ch=ch, br2=br2: e.scalar_tensor_tensor(
                out=va, in0=ua, scalar=lncol[:, ch, 2:3], in1=br2[:, 0:512],
                op0=ALU.mult, op1=ALU.mult), [uev, br2_ev] + vd)
            vev = kb.ev(st)
            u5_rot.done(ui, vev)
            xa, xd, xi = xm_rot.take()
            st = dve(lambda e, xa=xa, va=va, ch=ch: e.tensor_scalar_add(
                xa, va, lncol[:, ch, 3:4]), [vev] + xd)
            xev = kb.ev(st)
            v5_rot.done(vi, xev)
            h1_free[ch] = [xev]
            spv = kb.dma("pool", "fp", xm_sp[ch, :, ts * 512:(ts + 1) * 512], xa, deps=[xev])
            xm_rot.done(xi, spv)
            xm_sp_evs[(ts, ch)] = spv
            last_xm = xev
        bc_prev = [[last_xm], [last_xm]]
        pump(4)

    if "xmsp" in dbg:
        kb.dma("pool", "fo", dbg["xmsp"], xm_sp[0, :, :], deps=[xm_sp_evs[(NTS - 1, 7)]])
    fb.evs[4] = list(bc_prev[0])
    fb.evs[5] = list(bc_prev[1])
    for i in range(4):
        fb.evs[i] = list(fb5.evs[i])
    maybe_stop("p5")
    pump_flush()
    kb.fence()
    p5.close()
    p45.close()
    p0.close()
    prow.close()

    p6 = ExitStack()
    w1_sb = p6.enter_context(nc.sbuf_tensor("w1sb", [128, 8, FF], BF16))
    w2_sb = p6.enter_context(nc.sbuf_tensor("w2sb", [128, 16, C], BF16))
    xmg = p6.enter_context(nc.sbuf_tensor("xmg", [128, 8, TG], BF16))
    hmid = p6.enter_context(nc.sbuf_tensor("hmid", [128, 16, TG], BF16))
    accb = p6.enter_context(nc.sbuf_tensor("accb", [128, 2, 8, C], F32))
    wr_sb = p6.enter_context(nc.sbuf_tensor("wrsb", [128, 8, 8], BF16))
    stage6 = p6.enter_context(nc.sbuf_tensor("stage6", [128, 1, 64], F32))
    gt6 = p6.enter_context(nc.sbuf_tensor("gt6", [128, 2, 24], F32))
    h1r = p6.enter_context(nc.sbuf_tensor("h1r", [128, 2, C], BF16))
    xrb = p6.enter_context(nc.sbuf_tensor("xrb", [128, 2, C], F32))
    otb = p6.enter_context(nc.sbuf_tensor("otb", [128, 2, C], F32))
    st6_rot = Rot(kb, [stage6[:, 0, :]])
    gt6_rot = Rot(kb, [gt6[:, 0, :], gt6[:, 1, :]])
    h1r_rot = Rot(kb, [h1r[:, 0, :], h1r[:, 1, :]])
    xrb_rot = Rot(kb, [xrb[:, 0, :], xrb[:, 1, :]])
    otb_rot = Rot(kb, [otb[:, 0, :], otb[:, 1, :]])

    sg, sd, si = st6_rot.take()
    sg_v = sg[:, 0:64].rearrange("p (a b) -> p a b", b=8)
    lev = kb.dma("sync", "fw", sg_v, wr_h.rearrange("(a p) q -> p a q", p=128), deps=sd)
    st = dve(lambda e: e.tensor_copy(out=wr_sb[:, :, :], in_=sg_v), [lev])
    wr_done = kb.ev(st)
    st6_rot.done(si, wr_done)

    out_evs = []
    acc_free = [[[] for _ in range(8)] for _ in range(2)]
    acc_ev = [[[] for _ in range(8)] for _ in range(2)]

    def emit_tail(g, ttl):
        ping = g % 2
        tt = g * 8 + ttl
        hr, hrd, hri = h1r_rot.take()
        hlev = kb.dma("sync", "fr", hr, h1tm_sp[tt, :, :], deps=hrd)
        fa = accb[:, ping, ttl, :]
        st = dve(lambda e, fa=fa, hr=hr: e.tensor_add(fa, fa, hr),
                 [hlev] + acc_ev[ping][ttl])
        fev = kb.ev(st)
        h1r_rot.done(hri, fev)
        fstats = ln_stats([(fa[:, 0:512], 512), (fa[:, 512:1024], 512)],
                          tt, FAM_FFN, [fev])
        xr, xrd, xri = xrb_rot.take()
        xlev = kb.dma("sync", "fr", xr, x1tm_sp[tt, :, :], deps=xrd)
        ob, obd, obi = otb_rot.take()
        st = dve(lambda e, ob=ob, fa=fa, tt=tt: e.tensor_scalar(
            out=ob, in0=fa, scalar1=statw[:, tt, FAM_FFN + 1:FAM_FFN + 2],
            scalar2=statw[:, tt, FAM_FFN:FAM_FFN + 1], op0=ALU.add, op1=ALU.mult),
            [fstats["r"], fstats["negmu"]] + obd)
        t2ev = kb.ev(st)
        acc_free[ping][ttl] = [t2ev]
        st = dve(lambda e, ob=ob, xr=xr: e.tensor_add(ob, ob, xr), [t2ev, xlev])
        oev = kb.ev(st)
        xrb_rot.done(xri, oev)
        sev = kb.dma("pool", "fo", out_h[tt * 128:(tt + 1) * 128, :], ob, deps=[oev])
        otb_rot.done(obi, sev)
        out_evs.append(sev)
    w1_ready = None
    w2_ready = None
    xmg_free = []
    for grp in range(NGRP):
        xmg_evs = []
        for ch in range(8):
            dep = [xm_sp_evs[(2 * grp, ch)], xm_sp_evs[(2 * grp + 1, ch)]] + xmg_free
            lev = kb.dma("sync", "fr", xmg[:, ch, :], xm_sp[ch, :, grp * TG:(grp + 1) * TG],
                         deps=dep)
            xmg_evs.append(lev)
        xmg_done = xmg_evs[-1]
        gate_evs = [None] * 8
        for ttl in range(8):
            tt = grp * 8 + ttl
            bt, bd, bi = fb.take()
            for ch in range(8):
                st = pe(lambda e, ch=ch, ttl=ttl, bt=bt: e.matmul(
                    bt[:, 0:8], xmg[:, ch, ttl * 128:(ttl + 1) * 128], wr_sb[:, ch, :],
                    start=(ch == 0), stop=(ch == 7)),
                    ([wr_done] + xmg_evs + bd) if ch == 0 else [])
            mmev = kb.ev(st)
            ga, gd, gi = gt6_rot.take()
            st = dve(lambda e, ga=ga, bt=bt: e.tensor_reduce(
                out=ga[:, 0:1], in_=bt[:, 0:8], axis=AXL.X, op=ALU.max, negate=True),
                [mmev] + gd)
            st = act(lambda e, ga=ga, bt=bt: e.activation(
                out=ga[:, 3:11], in_=bt[:, 0:8], func=AF.Exp, bias=ga[:, 0:1],
                accum_out=ga[:, 1:2]), [kb.ev(st)])
            exv = kb.ev(st)
            fb.done(bi, exv)
            st = dve(lambda e, ga=ga: e.reciprocal(out=ga[:, 2:3], in_=ga[:, 1:2]), [exv])
            st = dve(lambda e, ga=ga: e.tensor_scalar(
                out=ga[:, 3:11], in0=ga[:, 3:11], scalar1=ga[:, 2:3], scalar2=None,
                op0=ALU.mult), [kb.ev(st)])
            gev = kb.ev(st)
            st = dve(lambda e, ga=ga: e.tensor_reduce(
                out=ga[:, 19:20], in_=ga[:, 3:11], axis=AXL.X, op=ALU.max), [gev])
            st = dve(lambda e, ga=ga: e.tensor_scalar(
                out=ga[:, 11:19], in0=ga[:, 3:11], scalar1=ga[:, 19:20], scalar2=None,
                op0=ALU.is_equal), [kb.ev(st)])
            st = dve(lambda e, ga=ga: e.scalar_tensor_tensor(
                out=ga[:, 11:19], in0=ga[:, 11:19], scalar=-1e30, in1=ga[:, 3:11],
                op0=ALU.mult, op1=ALU.add), [kb.ev(st)])
            st = dve(lambda e, ga=ga: e.tensor_reduce(
                out=ga[:, 20:21], in_=ga[:, 11:19], axis=AXL.X, op=ALU.max), [kb.ev(st)])
            st = dve(lambda e, ga=ga: e.tensor_scalar(
                out=ga[:, 11:19], in0=ga[:, 3:11], scalar1=ga[:, 20:21], scalar2=None,
                op0=ALU.is_ge), [kb.ev(st)])
            st = dve(lambda e, ga=ga, tt=tt: e.tensor_mul(
                gates[:, tt, :], ga[:, 3:11], ga[:, 11:19]), [kb.ev(st)])
            gate_evs[ttl] = kb.ev(st)
            gt6_rot.done(gi, gate_evs[ttl])
        if grp == 0:
            tap("gates", gates[:, 0, :], [gate_evs[0]])
        for e_ in range(NE):
            if w1_ready is None:
                w1_ready = kb.dma("sync", "fr", w1_sb[:, :, :],
                                  w1b[e_].rearrange("a p f -> p a f"),
                                  deps=_pump["done"][("w1", e_)])
            if w2_ready is None:
                w2_ready = kb.dma("sync", "fr", w2_sb[:, :, :],
                                  w2b[e_].rearrange("a p f -> p a f"),
                                  deps=_pump["done"][("w2", e_)])
            gelu_evs = []
            last_w1_mm = None
            for f in range(16):
                bts = []
                dps = [w1_ready, xmg_done]
                for kk in range(2):
                    bt, bd, bi = fb.take()
                    bts.append([bt, bi])
                    dps += bd
                for ch in range(8):
                    for kk in range(2):
                        bt = bts[kk][0]
                        st = pe(lambda e, ch=ch, kk=kk, f=f, bt=bt: e.matmul(
                            bt[:, 0:512], w1_sb[:, ch, f * 128:(f + 1) * 128],
                            xmg[:, ch, kk * 512:(kk + 1) * 512],
                            start=(ch == 0), stop=(ch == 7)),
                            dps if (ch == 0 and kk == 0) else [])
                last_w1_mm = st
                mmev = kb.ev(st)
                for kk in range(2):
                    bt, bi = bts[kk]
                    st = act(lambda e, bt=bt, f=f, kk=kk: e.activation(
                        out=hmid[:, f, kk * 512:(kk + 1) * 512], in_=bt[:, 0:512],
                        func=AF.Gelu_apprx_tanh), [mmev])
                    eev = kb.ev(st)
                    gelu_evs.append(eev)
                    fb.done(bi, eev)
            w1_free = kb.ev(last_w1_mm)
            if e_ == NE - 1:
                xmg_free = [w1_free]
            if not (grp == NGRP - 1 and e_ == NE - 1):
                ne_ = (e_ + 1) % NE
                w1_ready = kb.dma("sync", "fr", w1_sb[:, :, :],
                                  w1b[ne_].rearrange("a p f -> p a f"),
                                  deps=[w1_free] + _pump["done"][("w1", ne_)])
            gelu_done = gelu_evs[-1]
            if grp > 0 and e_ < 4:
                emit_tail(grp - 1, 2 * e_)
                emit_tail(grp - 1, 2 * e_ + 1)
            last_w2_mm = None
            for ttl in range(8):
                tt = grp * 8 + ttl
                for oh in range(2):
                    bt, bd, bi = fb.take()
                    for f in range(16):
                        st = pe(lambda e, f=f, ttl=ttl, oh=oh, bt=bt: e.matmul(
                            bt[:, 0:512], hmid[:, f, ttl * 128:(ttl + 1) * 128],
                            w2_sb[:, f, oh * 512:(oh + 1) * 512],
                            start=(f == 0), stop=(f == 15)),
                            ([w2_ready, gelu_done] + bd) if f == 0 else [])
                    last_w2_mm = st
                    mmev = kb.ev(st)
                    adst = accb[:, grp % 2, ttl, oh * 512:(oh + 1) * 512]
                    gcol = gates[:, tt, e_:e_ + 1]
                    if e_ == 0:
                        st = dve(lambda e, adst=adst, bt=bt, gcol=gcol: e.tensor_scalar(
                            out=adst, in0=bt[:, 0:512], scalar1=gcol, scalar2=None,
                            op0=ALU.mult),
                            [mmev, gate_evs[ttl]] + acc_free[grp % 2][ttl])
                    else:
                        st = dve(lambda e, adst=adst, bt=bt, gcol=gcol: e.scalar_tensor_tensor(
                            out=adst, in0=bt[:, 0:512], scalar=gcol, in1=adst,
                            op0=ALU.mult, op1=ALU.add), [mmev])
                    aev = kb.ev(st)
                    if oh == 0:
                        acc_ev[grp % 2][ttl] = [aev]
                    else:
                        acc_ev[grp % 2][ttl].append(aev)
                    fb.done(bi, aev)
            w2_free = kb.ev(last_w2_mm)
            if not (grp == NGRP - 1 and e_ == NE - 1):
                ne_ = (e_ + 1) % NE
                w2_ready = kb.dma("sync", "fr", w2_sb[:, :, :],
                                  w2b[ne_].rearrange("a p f -> p a f"),
                                  deps=[w2_free] + _pump["done"][("w2", ne_)])
    for ttl in range(8):
        emit_tail(NGRP - 1, ttl)

    kb.op("pool", lambda e: e.nop(), deps=list(out_evs[-24:]))


_NC_CACHE = {}


def _get_nc():
    key = tuple(DEBUG_TAPS)
    if key not in _NC_CACHE:
        _NC_CACHE[key] = build_program()
    return _NC_CACHE[key]


def _host_inputs(inputs):
    f32 = np.float32
    x = np.asarray(inputs["L0_attn_x"], f32)
    wq1T = np.ascontiguousarray(np.asarray(inputs["L0_attn_qkv_weight"], f32).T)
    wp1T = np.ascontiguousarray(np.asarray(inputs["L0_attn_proj_weight"], f32).T)
    bp1 = np.asarray(inputs["L0_attn_proj_bias"], f32).reshape(1, C)
    wq2T = np.ascontiguousarray(np.asarray(inputs["L0_ffn_qkv_weight"], f32).T)
    wp2T = np.ascontiguousarray(np.asarray(inputs["L0_ffn_proj_weight"], f32).T)
    wrT = np.ascontiguousarray(np.asarray(inputs["L0_ffn_router_weight"], f32).T)
    w1 = np.ascontiguousarray(np.asarray(inputs["L0_ffn_moe_w1"], f32))
    w2 = np.ascontiguousarray(np.asarray(inputs["L0_ffn_moe_w2"], f32))
    g1 = np.asarray(inputs["L0_ffn_ln1_weight"], f32)
    b1 = np.asarray(inputs["L0_ffn_ln1_bias"], f32)
    g2 = np.asarray(inputs["L0_ffn_ln2_weight"], f32)
    b2 = np.asarray(inputs["L0_ffn_ln2_bias"], f32)
    lncols = np.stack([v.reshape(8, 128).T for v in (g1, b1, g2, b2)], axis=2)
    lncols = np.ascontiguousarray(lncols)

    iden = np.eye(128, dtype=np.float32).astype(ml_dtypes.bfloat16)
    ones4 = np.ones((128, 512), np.float32).astype(ml_dtypes.bfloat16)
    onescol = np.ones((128, 1), np.float32).astype(ml_dtypes.bfloat16)
    mask = np.full((128, 128), NEG_BIG, np.float32)
    for i in range(8):
        for s_ in range(16):
            for t_ in range(s_ + 1):
                mask[i * 16 + s_, i * 16 + t_] = 0.0
    mask = np.ascontiguousarray(np.tile(mask, (1, 4)))

    shared = dict(wq1T=wq1T, wp1T=wp1T, bp1=bp1, wq2T=wq2T, wp2T=wp2T, wrT=wrT,
                  w1=w1, w2=w2, lncols=lncols, iden=iden, ones4=ones4,
                  onescol=onescol, mask=mask)
    in_maps = []
    for c in range(NCORE):
        xs = np.ascontiguousarray(x[:, c * NLOC:(c + 1) * NLOC, :].reshape(T, C))
        m = dict(shared)
        m["x_tm"] = xs
        m["x_fm"] = np.ascontiguousarray(xs.T)
        in_maps.append(m)
    return in_maps


def kernel(**inputs):
    nc = _get_nc()
    in_maps = _host_inputs(inputs)
    res = run_bass_kernel_spmd(nc, in_maps, core_ids=list(range(NCORE)))
    kernel.last_results = res
    out = np.empty((B, NSEQ, C), np.float32)
    for c in range(NCORE):
        out[:, c * NLOC:(c + 1) * NLOC, :] = \
            np.asarray(res.results[c]["out"]).reshape(B, NLOC, C)
    return out



# version 07
# speedup vs baseline: 1.1012x; 1.1012x over previous
import numpy as np
import ml_dtypes
from contextlib import ExitStack

import concourse.bass as bass
import concourse.mybir as mybir
from concourse.bass_utils import run_bass_kernel_spmd

F32 = mybir.dt.float32
BF16 = mybir.dt.bfloat16
AF = mybir.ActivationFunctionType
ALU = mybir.AluOpType
AXL = mybir.AxisListType

NCORE = 8
B = 16
NSEQ = 2048
NLOC = NSEQ // NCORE
C = 1024
T = B * NLOC
NTT = T // 128
NTS = T // 512
H1, D1 = 32, 32
H2, D2, KVH = 16, 64, 4
FF = 2048
NE = 8
EPS_LN = 1e-5
EPS_ATTN = 1e-15
NEG_BIG = -3.0e38
VKW = B * 8 * 33
NGRP = 4
TG = T // NGRP

FAM_X, FAM_X1, FAM_Y, FAM_LN2, FAM_FFN = 0, 10, 22, 30, 38
SW = 46

DEBUG_TAPS = []
STOP_AFTER = None
P3_TILES = NTT
P3_SUB = "all"
P4_SUB = "all"
P4B_SUB = "all"
P4B_CHAIN = 8
DEBUG_SHAPES = {
    "k0": (128, 1024), "v0": (128, 1056), "vkbf": (128, VKW),
    "qh0": (128, T), "projin0": (128, C), "y0": (128, C),
    "x1t0": (128, C), "qin0": (128, T), "q20": (128, T),
    "attfm0": (128, T), "h1t0": (128, C), "xm0": (128, T),
    "gates": (128, NE), "acc0": (128, C), "k2sp": (128, T), "v2t": (128, 2048),
    "xmsp": (128, T), "v2sp": (128, T), "v2w0": (128, T), "cb0": (128, T), "rows_a": (128, T), "rows_b": (128, T), "statw": (128, NTT * SW),
}

ENGINES = ("pe", "act", "dve", "pool", "sync")
FAMS = ("cc",)
NDMASEM = 24


class Ev:
    __slots__ = ("sem", "value", "step")
    def __init__(self, sem, step):
        self.sem = sem
        self.value = None
        self.step = step


class Step:
    __slots__ = ("fn", "waits", "ev", "inc", "_sem", "_inc_amt", "_engine")
    def __init__(self, fn):
        self.fn = fn
        self.waits = []
        self.ev = None
        self.inc = 0


class _StopBuild(Exception):
    pass


class KB:
    def __init__(self, nc, ctx):
        self.nc = nc
        self.streams = {e: [] for e in ENGINES}
        self.sems = {}
        for s in ENGINES + FAMS:
            self.sems[s] = ctx.enter_context(nc.semaphore("s_" + s))
        self.dma_pools = {}
        for q in ("sync", "pool", "act", "dve"):
            n = NDMASEM if q in ("sync", "pool") else 4
            sems = []
            for i in range(n):
                nm = "d_%s%d" % (q, i)
                self.sems[nm] = ctx.enter_context(nc.semaphore(nm))
                sems.append(nm)
            self.dma_pools[q] = {"sems": sems, "i": 0, "last": [None] * n}
        self.fence_evs = []
        self.fence_todo = set()

    def op(self, engine, fn, deps=(), sem=None, inc=1):
        st = Step(fn)
        if engine in self.fence_todo:
            self.fence_todo.discard(engine)
            st.waits.extend(self.fence_evs)
        for d in deps:
            if d is not None:
                st.waits.append(d)
        st._sem = sem or engine
        st._inc_amt = inc
        st._engine = engine
        self.streams[engine].append(st)
        return st

    def ev(self, step):
        if step.ev is None:
            step.ev = Ev(step._sem, step)
            step.inc = step._inc_amt
        return step.ev

    def tail(self, engine):
        s = self.streams[engine]
        return self.ev(s[-1]) if s else None

    def fence(self, extra=()):
        evs = [self.tail(e) for e in ENGINES] + list(extra)
        for p in self.dma_pools.values():
            evs += [e for e in p["last"] if e is not None]
        self.fence_evs = [e for e in evs if e is not None]
        self.fence_todo = set(ENGINES)

    def dma(self, queue, fam, out, in_, deps=(), **kw):
        p = self.dma_pools[queue]
        n = len(p["sems"])
        i = p["i"] % n
        p["i"] += 1
        dd = list(deps)
        if p["last"][i] is not None:
            dd.append(p["last"][i])
        st = self.op(queue,
                     lambda e, out=out, in_=in_, kw=kw: e.dma_start(out=out, in_=in_, **kw),
                     deps=dd, sem=p["sems"][i], inc=16)
        ev = self.ev(st)
        p["last"][i] = ev
        return ev

    def finalize(self):
        counts = {s: 0 for s in self.sems}
        for e in ENGINES:
            for st in self.streams[e]:
                if st.ev is not None:
                    counts[st._sem] += st.inc
                    st.ev.value = counts[st._sem]

    def emit(self, engine, eng):
        seen = {}
        for st in self.streams[engine]:
            for w in st.waits:
                assert w.value is not None, "un-finalized event"
                if seen.get(w.sem, 0) >= w.value:
                    continue
                seen[w.sem] = w.value
                eng.wait_ge(self.sems[w.sem], w.value)
            ins = st.fn(eng)
            if st.ev is not None:
                ins.then_inc(self.sems[st._sem], st.inc)


class Rot:
    def __init__(self, kb, items):
        self.items = items
        self.evs = [[] for _ in items]
        self.i = 0

    def take(self):
        i = self.i % len(self.items)
        self.i += 1
        return self.items[i], list(self.evs[i]), i

    def done(self, i, evs):
        if not isinstance(evs, (list, tuple)):
            evs = [evs]
        self.evs[i] = [e for e in evs if e is not None]


def bc_last(ap, n):
    return bass.AP(ap.tensor, ap.offset, [list(d) for d in ap.ap] + [[0, n]])


def build_program():
    nc = bass.Bass("TRN2", target_bir_lowering=False)
    ctx = ExitStack()
    kb = KB(nc, ctx)
    try:
        _build_phases(nc, ctx, kb)
    except _StopBuild:
        pass
    _do_finalize(kb, nc)
    try:
        ctx.close()
    except AssertionError:
        pass
    return nc


def _do_finalize(kb, nc):
    tails = [kb.tail(e) for e in ENGINES]
    kb.op("pool", lambda e: e.nop(), deps=[t for t in tails if t is not None])
    kb.finalize()
    with nc.Block() as block:
        @block.sync
        def _(eng):
            kb.emit("sync", eng)

        @block.scalar
        def _(eng):
            kb.emit("act", eng)

        @block.vector
        def _(eng):
            kb.emit("dve", eng)

        @block.tensor
        def _(eng):
            kb.emit("pe", eng)

        @block.gpsimd
        def _(eng):
            kb.emit("pool", eng)


def _build_phases(nc, ctx, kb):
    E = ctx.enter_context

    def din(name, shape, dt=F32):
        return nc.dram_tensor(name, shape, dt, kind="ExternalInput").ap()

    x_tm_h = din("x_tm", [T, C])
    x_fm_h = din("x_fm", [C, T])
    wq1_h = din("wq1T", [C, 3 * C])
    wp1_h = din("wp1T", [C, C])
    bp1_h = din("bp1", [1, C])
    wq2_h = din("wq2T", [C, 1536])
    wp2_h = din("wp2T", [C, C])
    wr_h = din("wrT", [C, NE])
    w1_h = din("w1", [NE, C, FF])
    w2_h = din("w2", [NE, FF, C])
    lncol_h = din("lncols", [128, 8, 4])
    iden_h = din("iden", [128, 128], BF16)
    ones4_h = din("ones4", [128, 512], BF16)
    onesc_h = din("onescol", [128, 1], BF16)
    mask_h = din("mask", [128, 512])

    out_h = nc.dram_tensor("out", [T, C], F32, kind="ExternalOutput").ap()

    vk_in = nc.dram_tensor("vk_in", [128, VKW], F32, kind="Internal").ap()
    vk_out = nc.dram_tensor("vk_out", [128, VKW], F32, kind="Internal").ap()
    x1fm_sp = nc.dram_tensor("x1fm_sp", [8, 128, T], F32, kind="Internal").ap()
    x1tm_sp = nc.dram_tensor("x1tm_sp", [NTT, 128, C], F32, kind="Internal").ap()
    h1tm_sp = nc.dram_tensor("h1tm_sp", [NTT, 128, C], BF16, kind="Internal").ap()
    q2_sp = nc.dram_tensor("q2_sp", [8, 128, T], BF16, kind="Internal").ap()
    projin_sp = nc.dram_tensor("projin_sp", [NTT, 128, C], BF16, kind="Internal").ap()
    k2_sp = nc.dram_tensor("k2_sp", [2, 128, T], BF16, kind="Internal").ap()
    v2_sp = nc.dram_tensor("v2_sp", [2, 128, T], BF16, kind="Internal").ap()
    xm_sp = nc.dram_tensor("xm_sp", [8, 128, T], BF16, kind="Internal").ap()
    w1b = nc.dram_tensor("w1b", [NE, 8, 128, FF], BF16, kind="Internal").ap()
    w2b = nc.dram_tensor("w2b", [NE, 16, 128, C], BF16, kind="Internal").ap()

    dbg = {}
    _BF_TAPS = {"k2sp", "v2sp", "q20", "projin0", "xmsp"}
    for name in DEBUG_TAPS:
        dt_ = BF16 if name in _BF_TAPS else F32
        dbg[name] = nc.dram_tensor("dbg_" + name, list(DEBUG_SHAPES[name]), dt_,
                                   kind="ExternalOutput").ap()

    def tap(name, src_ap, deps):
        if name in dbg:
            tmp = dbg[name]
            return kb.dma("pool", "fo", tmp, src_ap, deps=deps)
        return None

    statw = E(nc.sbuf_tensor("statw", [128, NTT, SW], F32))
    gates = E(nc.sbuf_tensor("gates", [128, NTT, NE], F32))
    lncol = E(nc.sbuf_tensor("lncolsb", [128, 8, 4], F32))
    iden_sb = E(nc.sbuf_tensor("idensb", [128, 128], BF16))
    ones4 = E(nc.sbuf_tensor("ones4sb", [128, 512], BF16))
    onesc = E(nc.sbuf_tensor("onescsb", [128, 1], BF16))
    statbf = E(nc.sbuf_tensor("statbf", [128, 4], BF16))
    cknst = E(nc.sbuf_tensor("cknst", [128, 2], F32))
    sqs = E(nc.sbuf_tensor("sqs", [128, 512], F32))

    psF = [E(nc.psum_tensor("pf%d" % i, [128, 512], F32)) for i in range(6)]
    psB = [E(nc.psum_tensor("pb%d" % i, [128, 1024], BF16)) for i in range(2)]
    fb = Rot(kb, psF)
    bb = Rot(kb, psB)
    statbf_rot = Rot(kb, [statbf[:, i:i + 1] for i in range(4)])

    prow = ExitStack()
    rows_a = prow.enter_context(nc.sbuf_tensor("rows_a", [128, T], BF16))
    rows_b = prow.enter_context(nc.sbuf_tensor("rows_b", [128, T], BF16))
    colsums = prow.enter_context(nc.sbuf_tensor("colsums", [128, 4096], BF16))

    def maybe_stop(tag):
        if STOP_AFTER == tag:
            raise _StopBuild()

    def pe(fn, deps=()):
        return kb.op("pe", fn, deps)

    def dve(fn, deps=()):
        return kb.op("dve", fn, deps)

    def act(fn, deps=()):
        return kb.op("act", fn, deps)

    def pool(fn, deps=()):
        return kb.op("pool", fn, deps)

    def ln_stats(pieces, tt, fam, deps):
        s = lambda k: statw[:, tt, fam + k:fam + k + 1]
        n_feat = float(sum(w for _, w in pieces))
        ap0, w0 = pieces[0]
        st = dve(lambda e: e.reduce_sum(out=s(2), in_=ap0, axis=AXL.X), deps)
        ev_sum = kb.ev(st)
        st = act(lambda e: e.activation(out=sqs[:, 0:w0], in_=ap0, func=AF.Square,
                                        accum_out=s(3)), deps + [ev_sum])
        ev_sq = kb.ev(st)
        for ap1, w1_ in pieces[1:]:
            st = dve(lambda e, ap1=ap1: e.reduce_sum(out=s(6), in_=ap1, axis=AXL.X), deps)
            e2 = kb.ev(st)
            st = pool(lambda e: e.tensor_add(s(2), s(2), s(6)), [ev_sum, e2])
            ev_sum = kb.ev(st)
            st = act(lambda e, ap1=ap1, w1_=w1_: e.activation(
                out=sqs[:, 0:w1_], in_=ap1, func=AF.Square, accum_out=s(7)), deps + [e2, ev_sq])
            e2 = kb.ev(st)
            st = pool(lambda e: e.tensor_add(s(3), s(3), s(7)), [ev_sq, e2])
            ev_sq = kb.ev(st)
        st = pool(lambda e: e.tensor_scalar_mul(s(4), s(2), 1.0 / n_feat), [ev_sum])
        ev_mu = kb.ev(st)
        st = pool(lambda e: e.tensor_scalar_mul(s(1), s(2), -1.0 / n_feat), [ev_sum])
        ev_negmu = kb.ev(st)
        st = pool(lambda e: e.tensor_scalar_mul(s(5), s(3), 1.0 / n_feat), [ev_sq])
        kb.ev(st)
        st = pool(lambda e: e.tensor_mul(s(6), s(4), s(4)), [ev_mu])
        m2ev = kb.ev(st)
        st = pool(lambda e: e.tensor_sub(s(5), s(5), s(6)), [m2ev])
        ev_var = kb.ev(st)
        st = act(lambda e: e.activation(out=s(7), in_=s(5), func=AF.Ln, bias=cknst[:, 0:1]), [ev_var])
        lnev = kb.ev(st)
        st = act(lambda e: e.activation(out=s(0), in_=s(7), func=AF.Exp, scale=cknst[:, 1:2]), [lnev])
        ev_r = kb.ev(st)
        return {"r": ev_r, "negmu": ev_negmu, "var": ev_var, "var_ap": s(5)}

    def make_row(col_ap, dst_tensor, row_base, col0, deps, ncols=128):
        cb, cbd, cbi = statbf_rot.take()
        st = dve(lambda e: e.tensor_copy(out=cb, in_=col_ap), list(deps) + cbd)
        cev = kb.ev(st)
        bt, bd, bi = bb.take()
        tp = (0, row_base) if row_base else None
        st = pe(lambda e: e.transpose(bt[row_base:row_base + 1, 0:ncols], cb, iden_sb[:, :],
                                      tile_position=tp), [cev] + bd)
        tev = kb.ev(st)
        st = dve(lambda e: e.tensor_copy(
            out=dst_tensor[row_base:row_base + 1, col0:col0 + ncols],
            in_=bt[row_base:row_base + 1, 0:ncols]), [tev])
        rev = kb.ev(st)
        bb.done(bi, rev)
        statbf_rot.done(cbi, rev)
        return rev

    def bcast(row_tensor, row_base, col0, n, deps):
        bt, bd, bi = fb.take()
        tp = (row_base, 0) if row_base else None
        st = pe(lambda e: e.matmul(bt[:, 0:n], ones4[row_base:row_base + 1, 0:128],
                                   row_tensor[row_base:row_base + 1, col0:col0 + n],
                                   start=True, stop=True, tile_position=tp), list(deps) + bd)
        return bt, kb.ev(st), bi

    kb.ev(kb.op("dve", lambda e: e.memset(cknst[:, 0:1], EPS_LN)))
    kb.ev(kb.op("dve", lambda e: e.memset(cknst[:, 1:2], -0.5)))
    ld_c = []
    for dst, src in ((iden_sb[:, :], iden_h[:, :]), (ones4[:, :], ones4_h[:, :]),
                     (onesc[:, :], onesc_h[:, :]),
                     (lncol[:, :, :], lncol_h[:, :, :])):
        ld_c.append(kb.dma("sync", "fc", dst, src))


    p0 = ExitStack()
    NPSL = 2
    wstage = p0.enter_context(nc.sbuf_tensor("wstage", [128, NPSL, C], F32))
    wcst = p0.enter_context(nc.sbuf_tensor("wcst", [128, NPSL, C], BF16))
    _jobs = [("w1", e_, ch, hf) for e_ in range(NE) for ch in range(8)
             for hf in range(2)] + \
            [("w2", e_, ch, 0) for e_ in range(NE) for ch in range(16)]

    def _job_aps(i):
        kind, e_, ch, hf = _jobs[i]
        if kind == "w1":
            src = w1_h[e_, ch * 128:(ch + 1) * 128, hf * C:(hf + 1) * C]
            dst = w1b[e_, ch, :, hf * C:(hf + 1) * C]
        else:
            src = w2_h[e_, ch * 128:(ch + 1) * 128, :]
            dst = w2b[e_, ch, :, :]
        return kind, e_, src, dst

    _pump = {"issue": 0, "proc": 0, "load": {}, "store": [None] * NPSL, "done": {}}

    def _pump_issue_to(bound):
        while _pump["issue"] < min(bound, len(_jobs)):
            i = _pump["issue"]
            _pump["issue"] = i + 1
            kind, e_, src, dst = _job_aps(i)
            sl = i % NPSL
            dep = [_pump["store"][sl]] if _pump["store"][sl] else []
            _pump["load"][i] = kb.dma("pool", "fs", wstage[:, sl, :], src, deps=dep)

    def pump(k=1):
        for _ in range(k):
            _pump_issue_to(_pump["proc"] + NPSL)
            i = _pump["proc"]
            if i >= len(_jobs):
                return
            _pump["proc"] = i + 1
            kind, e_, src, dst = _job_aps(i)
            sl = i % NPSL
            dep = [_pump["load"].pop(i)]
            if _pump["store"][sl] is not None:
                dep.append(_pump["store"][sl])
            st = act(lambda e, sl=sl: e.copy(out=wcst[:, sl, :],
                                             in_=wstage[:, sl, :]), dep)
            cev = kb.ev(st)
            sev = kb.dma("pool", "fs", dst, wcst[:, sl, :], deps=[cev])
            _pump["store"][sl] = sev
            _pump["done"].setdefault((kind, e_), []).append(sev)

    def pump_flush():
        while _pump["proc"] < len(_jobs):
            pump(2)


    def cc_test(tag):
        if STOP_AFTER == tag:
            z0 = kb.dma("pool", "fp", vk_in[0:128, 0:128], sqs[:, 0:128], deps=list(ld_c))
            st_ = kb.op("pool", lambda e: e.collective_compute(
                "AllReduce", ALU.add, replica_groups=[list(range(NCORE))],
                ins=[vk_in.opt()], outs=[vk_out.opt()]), deps=[z0], sem="cc", inc=1)
            kb.ev(st_)
            raise _StopBuild()
    if STOP_AFTER == "cc0":
        z0 = kb.dma("pool", "fp", vk_in[0:128, 0:128], sqs[:, 0:128], deps=list(ld_c))
        st = kb.op("pool", lambda e: e.collective_compute(
            "AllReduce", ALU.add, replica_groups=[list(range(NCORE))],
            ins=[vk_in.opt()], outs=[vk_out.opt()]), deps=[z0], sem="cc", inc=1)
        kb.ev(st)
        raise _StopBuild()

    p12 = ExitStack()
    xfmbf = p12.enter_context(nc.sbuf_tensor("xfmbf", [128, 8, T], BF16))
    stage_t = p12.enter_context(nc.sbuf_tensor("stage", [128, 2, 2048], F32))
    xslab = p12.enter_context(nc.sbuf_tensor("xslab", [128, 2, C], F32))
    p1only = ExitStack()
    wkv_sb = p1only.enter_context(nc.sbuf_tensor("wkv", [128, 8, 2048], BF16))
    kpool = p1only.enter_context(nc.sbuf_tensor("kpool", [128, 3, 1024], BF16))
    vpool = p1only.enter_context(nc.sbuf_tensor("vpool", [128, 3, 1056], BF16))
    vksb = p1only.enter_context(nc.sbuf_tensor("vksb", [128, VKW], F32))
    st_rot = Rot(kb, [stage_t[:, 0, :], stage_t[:, 1, :]])
    xslab_rot = Rot(kb, [xslab[:, 0, :], xslab[:, 1, :]])

    xstat_r = []
    for tt in range(NTT):
        sl_ap, sl_dep, sl_i = xslab_rot.take()
        lev = kb.dma("sync", "fx", sl_ap, x_tm_h[tt * 128:(tt + 1) * 128, :],
                     deps=sl_dep + (list(ld_c) if tt == 0 else []))
        evs = ln_stats([(sl_ap[:, 0:512], 512), (sl_ap[:, 512:1024], 512)],
                       tt, FAM_X, [lev])
        rev = make_row(statw[:, tt, FAM_X + 1:FAM_X + 2], rows_a, 0, tt * 128, [evs["negmu"]])
        xstat_r.append(evs["r"])
        xslab_rot.done(sl_i, [evs["r"], rev])
        pump(1)
    negmu0_rows_done = kb.tail("dve")
    cc_test("cc1")

    xfm_evs = []
    for ch in range(8):
        for half in range(2):
            sg, sd, si = st_rot.take()
            lev = kb.dma("sync", "fx", sg,
                         x_fm_h[ch * 128:(ch + 1) * 128, half * 2048:(half + 1) * 2048], deps=sd)
            st = dve(lambda e, sg=sg, ch=ch, half=half: e.tensor_copy(
                out=xfmbf[:, ch, half * 2048:(half + 1) * 2048], in_=sg), [lev])
            ev_ = kb.ev(st)
            st_rot.done(si, ev_)
            xfm_evs.append(ev_)
    xfm_done = xfm_evs[-1]

    wkv_evs = []
    for ch in range(8):
        sg, sd, si = st_rot.take()
        lev = kb.dma("sync", "fw", sg, wq1_h[ch * 128:(ch + 1) * 128, 1024:3072], deps=sd)
        st = dve(lambda e, sg=sg, ch=ch: e.tensor_copy(out=wkv_sb[:, ch, :], in_=sg), [lev])
        ev_ = kb.ev(st)
        st_rot.done(si, ev_)
        wkv_evs.append(ev_)
    wkv_done = wkv_evs[-1]
    cc_test("cc2")

    csk_evs = []
    for kvo in range(4):
        bt, bd, bi = fb.take()
        for ch in range(8):
            st = pe(lambda e, ch=ch, kvo=kvo, bt=bt: e.matmul(
                bt[0:1, 0:512], onesc[:, :], wkv_sb[:, ch, kvo * 512:(kvo + 1) * 512],
                start=(ch == 0), stop=(ch == 7)),
                ([wkv_done, ] + list(ld_c) + bd) if ch == 0 else [])
        mev = kb.ev(st)
        st = dve(lambda e, kvo=kvo, bt=bt: e.tensor_copy(
            out=colsums[0:1, kvo * 512:(kvo + 1) * 512], in_=bt[0:1, 0:512]), [mev])
        cev = kb.ev(st)
        fb.done(bi, cev)
        csk_evs.append(cev)
    colsum_kv_ev = csk_evs[-1]

    vones = []
    for i in range(3):
        ap = vpool[:, i, :].rearrange("p (h e) -> p h e", e=33)[:, :, 32:33]
        st = dve(lambda e, ap=ap: e.memset(ap, 1.0), [])
        vones.append(kb.ev(st))

    cc_test("cc3")
    fb_kv = Rot(kb, psF[0:2])
    kv_ready = [None] * NTT
    kv_free = [[] for _ in range(3)]
    vk_bank_busy = [[] for _ in range(4)]
    vk_evac_evs = []

    for b in range(B):
        for half in range(2):
            tt = 2 * b + half
            slot = tt % 3
            base_deps = [xfm_done, wkv_done, colsum_kv_ev, negmu0_rows_done,
                         xstat_r[tt], vones[slot]] + kv_free[slot]
            evac_evs = []
            for kh in range(2):
                bts = []
                dps = list(base_deps) if kh == 0 else []
                for kk in range(2):
                    bt, bd, bi = fb_kv.take()
                    bts.append((bt, bi))
                    dps += bd
                for ch in range(8):
                    for kk in range(2):
                        kvo = kh * 2 + kk
                        bt = bts[kk][0]
                        st = pe(lambda e, ch=ch, kvo=kvo, bt=bt, tt=tt: e.matmul(
                            bt[:, 0:512], xfmbf[:, ch, tt * 128:(tt + 1) * 128],
                            wkv_sb[:, ch, kvo * 512:(kvo + 1) * 512],
                            start=(ch == 0), stop=False),
                            dps if (ch == 0 and kk == 0) else [])
                for kk in range(2):
                    kvo = kh * 2 + kk
                    bt = bts[kk][0]
                    st = pe(lambda e, kvo=kvo, bt=bt, tt=tt: e.matmul(
                        bt[:, 0:512], rows_a[0:1, tt * 128:(tt + 1) * 128],
                        colsums[0:1, kvo * 512:(kvo + 1) * 512],
                        start=False, stop=True), [])
                    mmev = kb.ev(st)
                    if kvo < 2:
                        dst = kpool[:, slot, kvo * 512:(kvo + 1) * 512]
                    else:
                        vv = vpool[:, slot, :].rearrange("p (h e) -> p h e", e=33)
                        h0 = (kvo - 2) * 16
                        dst = vv[:, h0:h0 + 16, 0:32]
                    st = act(lambda e, dst=dst, bt=bt, tt=tt: e.activation(
                        out=dst, in_=bt[:, 0:512], func=AF.Relu,
                        scale=statw[:, tt, FAM_X:FAM_X + 1]), [mmev])
                    eev = kb.ev(st)
                    evac_evs.append(eev)
                    fb_kv.done(bts[kk][1], eev)
            kv_ready[tt] = evac_evs
            if tt == 0:
                tap("k0", kpool[:, 0, :], evac_evs)
                tap("v0", vpool[:, 0, :], evac_evs)
            pump(2)
        sa, sb_ = (2 * b) % 3, (2 * b + 1) % 3
        vdeps = kv_ready[2 * b] + kv_ready[2 * b + 1]
        for hg in range(8):
            for j in range(4):
                h = 4 * hg + j
                bt = psF[j + 2]
                dps = []
                if hg == 0 and j == 0:
                    dps += vdeps
                if hg == 0:
                    dps += vk_bank_busy[j]
                st = pe(lambda e, h=h, j=j, bt=bt, sa=sa, hg=hg: e.matmul(
                    bt[32 * j:32 * (j + 1), hg * 33:(hg + 1) * 33],
                    kpool[:, sa, h * 32:(h + 1) * 32],
                    vpool[:, sa, h * 33:(h + 1) * 33],
                    start=True, stop=False, tile_position=(0, 32 * j)), dps)
                st = pe(lambda e, h=h, j=j, bt=bt, sb_=sb_, hg=hg: e.matmul(
                    bt[32 * j:32 * (j + 1), hg * 33:(hg + 1) * 33],
                    kpool[:, sb_, h * 32:(h + 1) * 32],
                    vpool[:, sb_, h * 33:(h + 1) * 33],
                    start=False, stop=True, tile_position=(0, 32 * j)), [])
        pe_b_ev = kb.ev(st)
        kv_free[sa] = [pe_b_ev]
        kv_free[sb_] = [pe_b_ev]
        for j in range(4):
            bt = psF[j + 2]
            st = dve(lambda e, j=j, bt=bt, b=b: e.tensor_copy(
                out=vksb[32 * j:32 * (j + 1), b * 264:(b + 1) * 264],
                in_=bt[32 * j:32 * (j + 1), 0:264]), [pe_b_ev])
            eev = kb.ev(st)
            vk_bank_busy[j] = [eev]
            vk_evac_evs.append(eev)
        pump(2)

    cc_test("cc4")
    for j in range(4):
        fb.evs[2 + j] = vk_bank_busy[j]
    fb.evs[0] = list(fb_kv.evs[0])
    fb.evs[1] = list(fb_kv.evs[1])

    maybe_stop("p1")
    vkin_ev = kb.dma("pool", "fp", vk_in, vksb[:, :], deps=vk_evac_evs)
    maybe_stop("p1cc0")
    if STOP_AFTER == "cc5":
        st = kb.op("pool", lambda e: e.collective_compute(
            "AllReduce", ALU.add, replica_groups=[list(range(NCORE))],
            ins=[vk_in.opt()], outs=[vk_out.opt()]), deps=[vkin_ev], sem="cc", inc=1)
        kb.ev(st)
        raise _StopBuild()
    kb.fence(extra=[vkin_ev])
    p1only.close()
    if STOP_AFTER and STOP_AFTER.startswith("cct"):
        n = int(STOP_AFTER[3:])
        specs = [("vkbf", [128, VKW], BF16), ("qch", [128, 2, T], BF16),
                 ("wqs", [128, 2, 8, 128], BF16), ("dentmp", [128, 4, 140], F32),
                 ("pib", [128, 4, 128], BF16)]
        for nm, shp, dt_ in specs[:n]:
            nc.sbuf_tensor(nm, shp, dt_).__enter__()
        st = kb.op("pool", lambda e: e.collective_compute(
            "AllReduce", ALU.add, replica_groups=[list(range(NCORE))],
            ins=[vk_in.opt()], outs=[vk_out.opt()]), deps=[vkin_ev], sem="cc", inc=1)
        kb.ev(st)
        raise _StopBuild()
    if STOP_AFTER and STOP_AFTER.startswith("ccsz"):
        nbytes = int(STOP_AFTER[4:]) * 1024
        dummy = nc.sbuf_tensor("dummy_cc9", [128, nbytes], mybir.dt.uint8)
        dummy.__enter__()
        st = kb.op("pool", lambda e: e.collective_compute(
            "AllReduce", ALU.add, replica_groups=[list(range(NCORE))],
            ins=[vk_in.opt()], outs=[vk_out.opt()]), deps=[vkin_ev], sem="cc", inc=1)
        kb.ev(st)
        raise _StopBuild()
    if STOP_AFTER == "cc6":
        st = kb.op("pool", lambda e: e.collective_compute(
            "AllReduce", ALU.add, replica_groups=[list(range(NCORE))],
            ins=[vk_in.opt()], outs=[vk_out.opt()]), deps=[vkin_ev], sem="cc", inc=1)
        kb.ev(st)
        raise _StopBuild()

    p2 = ExitStack()
    vkbf = p2.enter_context(nc.sbuf_tensor("vkbf", [128, VKW], BF16))
    qch = p2.enter_context(nc.sbuf_tensor("qch", [128, 2, T], BF16))
    wqs = p2.enter_context(nc.sbuf_tensor("wqs", [128, 2, 8, 128], BF16))
    dentmp = p2.enter_context(nc.sbuf_tensor("dentmp", [128, 4, 140], F32))
    pib = p2.enter_context(nc.sbuf_tensor("pib", [128, 4, 128], BF16))
    den_rot = Rot(kb, [dentmp[:, i, :] for i in range(4)])
    pib_rot = Rot(kb, [pib[:, i, :] for i in range(4)])

    maybe_stop("p1f")
    st = kb.op("pool", lambda e: e.collective_compute(
        "AllReduce", ALU.add, replica_groups=[list(range(NCORE))],
        ins=[vk_in.opt()], outs=[vk_out.opt()]), deps=[vkin_ev], sem="cc", inc=1)
    cc_ev = kb.ev(st)
    maybe_stop("p1cc")

    rel_evs = []
    for p_ in range(3):
        w = 2048 if p_ < 2 else VKW - 4096
        sg, sd, si = st_rot.take()
        lev = kb.dma("sync", "fr", sg[:, 0:w], vk_out[:, p_ * 2048:p_ * 2048 + w],
                     deps=[cc_ev] + sd)
        st = dve(lambda e, sg=sg, p_=p_, w=w: e.tensor_copy(
            out=vkbf[:, p_ * 2048:p_ * 2048 + w], in_=sg[:, 0:w]), [lev])
        ev_ = kb.ev(st)
        st_rot.done(si, ev_)
        rel_evs.append(ev_)
    vk_ready_ev = rel_evs[-1]
    tap("vkbf", vkbf[:, :], [vk_ready_ev])

    qch_free = [[], []]
    projin_evs = {}
    for ci in range(8):
        sl = ci % 2
        sg, sd, si = st_rot.take()
        sg_ap = sg[:, 0:1024].rearrange("p (a b) -> p a b", b=128)
        lev = kb.dma("sync", "fw", sg_ap,
                     wq1_h[:, ci * 128:(ci + 1) * 128].rearrange("(a p) q -> p a q", p=128),
                     deps=sd)
        st = dve(lambda e, sl=sl, sg_ap=sg_ap: e.tensor_copy(out=wqs[:, sl, :, :], in_=sg_ap), [lev])
        wq_ev = kb.ev(st)
        st_rot.done(si, wq_ev)
        bt, bd, bi = fb.take()
        for ch in range(8):
            st = pe(lambda e, ch=ch, bt=bt, sl=sl: e.matmul(
                bt[0:1, 0:128], onesc[:, :], wqs[:, sl, ch, :],
                start=(ch == 0), stop=(ch == 7)), ([wq_ev] + bd) if ch == 0 else [])
        mev = kb.ev(st)
        st = dve(lambda e, ci=ci, bt=bt: e.tensor_copy(
            out=colsums[0:1, 2048 + ci * 128:2048 + (ci + 1) * 128], in_=bt[0:1, 0:128]), [mev])
        csq_ev = kb.ev(st)
        fb.done(bi, csq_ev)
        qhat_evs = []
        for ts in range(NTS):
            bt, bd, bi = fb.take()
            for ch in range(8):
                dps = []
                if ch == 0:
                    dps = [csq_ev, wq_ev, negmu0_rows_done] + bd
                    if ts == 0:
                        dps += qch_free[sl]
                st = pe(lambda e, ch=ch, ts=ts, bt=bt, sl=sl: e.matmul(
                    bt[:, 0:512], wqs[:, sl, ch, :], xfmbf[:, ch, ts * 512:(ts + 1) * 512],
                    start=(ch == 0), stop=False), dps)
            st = pe(lambda e, ci=ci, ts=ts, bt=bt: e.matmul(
                bt[:, 0:512], colsums[0:1, 2048 + ci * 128:2048 + (ci + 1) * 128],
                rows_a[0:1, ts * 512:(ts + 1) * 512], start=False, stop=True), [])
            mmev = kb.ev(st)
            st = act(lambda e, sl=sl, ts=ts, bt=bt: e.activation(
                out=qch[:, sl, ts * 512:(ts + 1) * 512], in_=bt[:, 0:512], func=AF.Relu), [mmev])
            eev = kb.ev(st)
            qhat_evs.append(eev)
            fb.done(bi, eev)
        qhat_ev = qhat_evs[-1]
        if ci == 0:
            tap("qh0", qch[:, 0, :], [qhat_ev])
        last_vkq = None
        for tt in range(NTT):
            b = tt // 2
            jb = []
            for j in range(4):
                bt, bd, bi = fb.take()
                st = pe(lambda e, j=j, tt=tt, b=b, ci=ci, sl=sl, bt=bt: e.matmul(
                    bt[:, 0:33],
                    qch[32 * j:32 * (j + 1), sl, tt * 128:(tt + 1) * 128],
                    vkbf[32 * j:32 * (j + 1), (b * 8 + ci) * 33:(b * 8 + ci + 1) * 33],
                    start=True, stop=True, tile_position=(32 * j, 0)),
                    [qhat_ev, vk_ready_ev] + bd)
                jb.append((bt, kb.ev(st), bi))
                last_vkq = st
            dap, dd, di = den_rot.take()
            con_evs = []
            for j in range(4):
                btj, mmev, bij = jb[j]
                st = act(lambda e, j=j, btj=btj, dap=dap: e.copy(
                    out=dap[:, j * 33:(j + 1) * 33], in_=btj[:, 0:33]),
                    [mmev] + (dd if j == 0 else []))
                cev = kb.ev(st)
                con_evs.append(cev)
                fb.done(bij, cev)
            dv = dap[:, 0:132].rearrange("p (j e) -> p j e", e=33)
            st = dve(lambda e, dap=dap, dv=dv: e.tensor_scalar_add(
                dap[:, 132:136].rearrange("p (j o) -> p j o", o=1), dv[:, :, 32:33], EPS_ATTN),
                con_evs)
            st = dve(lambda e, dap=dap: e.reciprocal(
                out=dap[:, 136:140].rearrange("p (j o) -> p j o", o=1),
                in_=dap[:, 132:136].rearrange("p (j o) -> p j o", o=1)), [kb.ev(st)])
            rec_ev = kb.ev(st)
            rec_bc = bc_last(dap[:, 136:140], 32)
            pa, pad, pai = pib_rot.take()
            dst = pa.rearrange("p (j d) -> p j d", d=32)
            st = pool(lambda e, dv=dv, rec_bc=rec_bc, dst=dst: e.tensor_tensor(
                out=dst, in0=dv[:, :, 0:32], in1=rec_bc, op=ALU.mult), [rec_ev] + pad)
            dev_ = kb.ev(st)
            den_rot.done(di, dev_)
            spv = kb.dma("pool", "fp", projin_sp[tt, :, ci * 128:(ci + 1) * 128], pa,
                         deps=[dev_])
            pib_rot.done(pai, spv)
            projin_evs[(ci, tt)] = spv
        qch_free[sl] = [kb.ev(last_vkq)]
        pump(4)

    maybe_stop("p2")
    if "projin0" in dbg:
        kb.dma("pool", "fo", dbg["projin0"], projin_sp[0, :, :], deps=[projin_evs[(7, 0)]])
    kb.fence()
    p2.close()
    p12.close()

    p3 = ExitStack()
    wp1_sb = p3.enter_context(nc.sbuf_tensor("wp1sb", [128, 8, C], BF16))
    pit = p3.enter_context(nc.sbuf_tensor("pit", [128, 4, C], BF16))
    stage3 = p3.enter_context(nc.sbuf_tensor("stage3", [128, 4, 1024], F32))
    pif = p3.enter_context(nc.sbuf_tensor("pif", [128, 2, 8, 128], BF16))
    t1buf = p3.enter_context(nc.sbuf_tensor("t1buf", [128, 2, C], F32))
    t1bf = p3.enter_context(nc.sbuf_tensor("t1bf", [128, 2, C], BF16))
    t1fm = p3.enter_context(nc.sbuf_tensor("t1fm", [128, 2, 8, 128], BF16))
    x1tb = p3.enter_context(nc.sbuf_tensor("x1tb", [128, 3, C], F32))
    x1fb = p3.enter_context(nc.sbuf_tensor("x1fb", [128, 3, 8, 128], F32))
    xsl3 = p3.enter_context(nc.sbuf_tensor("xsl3", [128, 4, C], F32))

    st3_rot = Rot(kb, [stage3[:, i, :] for i in range(4)])
    pif_rot = Rot(kb, [pif[:, 0], pif[:, 1]])
    t1_rot = Rot(kb, [t1buf[:, 0, :], t1buf[:, 1, :]])
    t1bf_rot = Rot(kb, [t1bf[:, 0, :], t1bf[:, 1, :]])
    t1fm_rot = Rot(kb, [t1fm[:, 0], t1fm[:, 1]])
    x1t_rot = Rot(kb, [x1tb[:, i, :] for i in range(3)])
    x1f_rot = Rot(kb, [x1fb[:, i] for i in range(3)])
    xsl3_rot = Rot(kb, [xsl3[:, i, :] for i in range(4)])
    pit_rot = Rot(kb, [pit[:, i, :] for i in range(4)])

    wp1_evs = []
    for ch in range(8):
        sg, sd, si = st3_rot.take()
        lev = kb.dma("sync", "fw", sg, wp1_h[ch * 128:(ch + 1) * 128, :], deps=sd)
        st = dve(lambda e, sg=sg, ch=ch: e.tensor_copy(out=wp1_sb[:, ch, :], in_=sg), [lev])
        ev_ = kb.ev(st)
        st3_rot.done(si, ev_)
        wp1_evs.append(ev_)
    wp1_done = wp1_evs[-1]
    sgb, sdb, sib = st3_rot.take()
    lev = kb.dma("sync", "fw", sgb[0:1, 0:1024], bp1_h[:, :], deps=sdb)
    st = dve(lambda e, sgb=sgb: e.tensor_copy(out=colsums[0:1, 3072:4096], in_=sgb[0:1, 0:1024]),
             [lev])
    bias_ev = kb.ev(st)
    st3_rot.done(sib, bias_ev)

    def p3_stageA(tt):
        pta, ptd_, pti_ = pit_rot.take()
        plev = kb.dma("sync", "fr", pta, projin_sp[tt, :, :], deps=ptd_)
        pf, pfd, pfi = pif_rot.take()
        bt, bd, bi = bb.take()
        for ch in range(8):
            st = pe(lambda e, ch=ch, bt=bt, pta=pta: e.transpose(
                bt[:, ch * 128:(ch + 1) * 128], pta[:, ch * 128:(ch + 1) * 128],
                iden_sb[:, :]), ([plev] + bd + pfd) if ch == 0 else [])
        tev = kb.ev(st)
        pit_rot.done(pti_, tev)
        st = dve(lambda e, bt=bt, pf=pf: e.tensor_copy(
            out=pf.rearrange("p a b -> p (a b)"), in_=bt[:, 0:1024]), [tev])
        pif_ev = kb.ev(st)
        bb.done(bi, pif_ev)
        ybanks = []
        for oh in range(2):
            btY, bdY, biY = fb.take()
            for ch in range(8):
                st = pe(lambda e, ch=ch, oh=oh, btY=btY, pf=pf: e.matmul(
                    btY[:, 0:512], pf[:, ch, :], wp1_sb[:, ch, oh * 512:(oh + 1) * 512],
                    start=(ch == 0), stop=False),
                    ([pif_ev, wp1_done, bias_ev] + bdY) if ch == 0 else [])
            st = pe(lambda e, oh=oh, btY=btY: e.matmul(
                btY[:, 0:512], ones4[0:1, 0:128],
                colsums[0:1, 3072 + oh * 512:3072 + (oh + 1) * 512],
                start=False, stop=True), [])
            ybanks.append([btY, kb.ev(st), biY])
        pif_rot.done(pfi, [ybanks[0][1], ybanks[1][1]])
        return dict(tt=tt, ybanks=ybanks)

    def p3_stageB(sa):
        tt = sa["tt"]
        ybanks = sa["ybanks"]
        yevs = ln_stats([(ybanks[0][0][:, 0:512], 512), (ybanks[1][0][:, 0:512], 512)],
                        tt, FAM_Y, [ybanks[0][1], ybanks[1][1]])
        t1a, t1d, t1i = t1_rot.take()
        yb_ = statw[:, tt, FAM_Y + 2:FAM_Y + 3]
        st = pool(lambda e, yb_=yb_, tt=tt: e.tensor_mul(
            yb_, statw[:, tt, FAM_Y:FAM_Y + 1], statw[:, tt, FAM_Y + 1:FAM_Y + 2]),
            [yevs["r"], yevs["negmu"]])
        ybev = kb.ev(st)
        t1evs = []
        for oh in range(2):
            btY = ybanks[oh][0]
            st = act(lambda e, btY=btY, oh=oh, t1a=t1a, tt=tt, yb_=yb_: e.activation(
                out=t1a[:, oh * 512:(oh + 1) * 512], in_=btY[:, 0:512],
                func=AF.Identity, bias=yb_, scale=statw[:, tt, FAM_Y:FAM_Y + 1]),
                [ybev, ybanks[oh][1]] + (t1d if oh == 0 else []))
            ev_ = kb.ev(st)
            t1evs.append(ev_)
            fb.done(ybanks[oh][2], ev_)
        xs, xsd, xsi = xsl3_rot.take()
        xlev = kb.dma("sync", "fx", xs, x_tm_h[tt * 128:(tt + 1) * 128, :], deps=xsd)
        x1a, x1d, x1i = x1t_rot.take()
        st = dve(lambda e, x1a=x1a, xs=xs, t1a=t1a: e.tensor_add(x1a, xs, t1a),
                 t1evs + [xlev] + x1d)
        x1ev = kb.ev(st)
        xsl3_rot.done(xsi, x1ev)
        spev = kb.dma("pool", "fp", x1tm_sp[tt, :, :], x1a, deps=[x1ev])
        if tt == 0:
            tap("y0", t1a, t1evs)
            tap("x1t0", x1a, [x1ev])
        x1evs = ln_stats([(x1a[:, 0:512], 512), (x1a[:, 512:1024], 512)],
                         tt, FAM_X1, [x1ev])
        x1t_rot.done(x1i, [spev, x1evs["r"]])
        s0 = statw[:, tt, FAM_X1 + 0:FAM_X1 + 1]
        s3 = statw[:, tt, FAM_X1 + 3:FAM_X1 + 4]
        s4 = statw[:, tt, FAM_X1 + 4:FAM_X1 + 5]
        s6 = statw[:, tt, FAM_X1 + 6:FAM_X1 + 7]
        s8 = statw[:, tt, FAM_X1 + 8:FAM_X1 + 9]
        va_ = x1evs["var_ap"]
        st = pool(lambda e, s3=s3, va_=va_, s0=s0: e.tensor_mul(s3, va_, s0),
                  [x1evs["var"], x1evs["r"]])
        st = pool(lambda e, s3=s3, s0=s0: e.tensor_mul(s3, s3, s0), [kb.ev(st)])
        v1ev = kb.ev(st)
        st = act(lambda e, s4=s4, s3=s3: e.activation(out=s4, in_=s3, func=AF.Ln,
                                                      bias=cknst[:, 0:1]), [v1ev])
        st = act(lambda e, s6=s6, s4=s4: e.activation(out=s6, in_=s4, func=AF.Exp,
                                                      scale=cknst[:, 1:2]), [kb.ev(st)])
        r1ev = kb.ev(st)
        st = pool(lambda e, s8=s8, s6=s6, s0=s0: e.tensor_mul(s8, s6, s0), [r1ev])
        aev = kb.ev(st)
        make_row(statw[:, tt, FAM_X1 + 1:FAM_X1 + 2], rows_a, 32, tt * 128,
                 [x1evs["negmu"]])
        make_row(s8, rows_a, 64, tt * 128, [aev])
        make_row(s0, rows_b, 0, tt * 128, [x1evs["r"]])
        tb, tbd, tbi = t1bf_rot.take()
        st = act(lambda e, tb=tb, t1a=t1a: e.copy(out=tb, in_=t1a), t1evs + tbd)
        tbev = kb.ev(st)
        t1_rot.done(t1i, [x1ev, tbev])
        tf, tfd, tfi = t1fm_rot.take()
        bt2, bd2, bi2 = bb.take()
        for ch in range(8):
            st = pe(lambda e, ch=ch, bt2=bt2, tb=tb: e.transpose(
                bt2[:, ch * 128:(ch + 1) * 128], tb[:, ch * 128:(ch + 1) * 128], iden_sb[:, :]),
                ([tbev] + bd2) if ch == 0 else [])
        tev2 = kb.ev(st)
        t1bf_rot.done(tbi, tev2)
        st = dve(lambda e, bt2=bt2, tf=tf: e.tensor_copy(
            out=tf.rearrange("p a b -> p (a b)"), in_=bt2[:, 0:1024]), [tev2] + tfd)
        tfev = kb.ev(st)
        bb.done(bi2, tfev)
        x1f, x1fd, x1fi = x1f_rot.take()
        sgx, sgd, sgi = st3_rot.take()
        sgx_v = sgx.rearrange("p (a b) -> p a b", b=128)
        xcl = kb.dma("sync", "fx", sgx_v,
                     x_fm_h[:, tt * 128:(tt + 1) * 128].rearrange("(a p) q -> p a q", p=128),
                     deps=sgd)
        st = dve(lambda e, x1f=x1f, sgx_v=sgx_v, tf=tf: e.tensor_add(
            x1f.rearrange("p a b -> p (a b)"), sgx_v.rearrange("p a b -> p (a b)"),
            tf.rearrange("p a b -> p (a b)")), [xcl, tfev] + x1fd)
        x1fev = kb.ev(st)
        st3_rot.done(sgi, x1fev)
        t1fm_rot.done(tfi, x1fev)
        dst = x1fm_sp[:, :, tt * 128:(tt + 1) * 128].rearrange("a p t -> p a t")
        spev2 = kb.dma("pool", "fp", dst, x1f[:, :, :], deps=[x1fev])
        x1f_rot.done(x1fi, spev2)
        pump(4)

    p3_pend = None
    for tt in range(P3_TILES):
        sa = p3_stageA(tt)
        if p3_pend is not None:
            p3_stageB(p3_pend)
        p3_pend = sa
    p3_stageB(p3_pend)

    x1_rows_done = kb.tail("dve")
    if "statw" in dbg:
        kb.dma("pool", "fo", dbg["statw"], statw[:, :, :].rearrange("p a b -> p (a b)"),
               deps=[x1_rows_done])
    if "rows_a" in dbg:
        kb.dma("pool", "fo", dbg["rows_a"], rows_a[:, :], deps=[x1_rows_done])
    if "rows_b" in dbg:
        kb.dma("pool", "fo", dbg["rows_b"], rows_b[:, :], deps=[x1_rows_done])
    maybe_stop("p3")
    kb.fence()
    p3.close()

    p4a = ExitStack()
    qin = p4a.enter_context(nc.sbuf_tensor("qin", [128, 8, T], BF16))
    wq2_sb = p4a.enter_context(nc.sbuf_tensor("wq2sb", [128, 8, 1536], BF16))
    stage4 = p4a.enter_context(nc.sbuf_tensor("stage4", [128, 2, 1536], F32))
    u4 = p4a.enter_context(nc.sbuf_tensor("u4", [128, 2, 512], F32))
    v4 = p4a.enter_context(nc.sbuf_tensor("v4", [128, 2, 512], F32))
    q2stg = p4a.enter_context(nc.sbuf_tensor("q2stg", [128, 2, T], BF16))
    st4_rot = Rot(kb, [stage4[:, 0, :], stage4[:, 1, :]])
    u4_rot = Rot(kb, [u4[:, 0, :], u4[:, 1, :]])
    v4_rot = Rot(kb, [v4[:, 0, :], v4[:, 1, :]])
    cb_rot = Rot(kb, [q2stg[:, 0, :], q2stg[:, 1, :]])

    wq2_evs = []
    for ch in range(8):
        sg, sd, si = st4_rot.take()
        lev = kb.dma("sync", "fw", sg, wq2_h[ch * 128:(ch + 1) * 128, :], deps=sd)
        st = dve(lambda e, sg=sg, ch=ch: e.tensor_copy(out=wq2_sb[:, ch, :], in_=sg), [lev])
        ev_ = kb.ev(st)
        st4_rot.done(si, ev_)
        wq2_evs.append(ev_)
    wq2_done = wq2_evs[-1]
    if P4_SUB == "w":
        maybe_stop("p4a")

    qin_evs = []
    for ts in range(NTS):
        bmu, bmu_ev, bmu_i = bcast(rows_a, 32, ts * 512, 512, [x1_rows_done])
        ba, ba_ev, ba_i = bcast(rows_a, 64, ts * 512, 512, [x1_rows_done])
        last = None
        for ch in range(8):
            sg, sd, si = st4_rot.take()
            lev = kb.dma("sync", "fr", sg[:, 0:512], x1fm_sp[ch, :, ts * 512:(ts + 1) * 512],
                         deps=sd)
            ua, ud, ui = u4_rot.take()
            st = dve(lambda e, ua=ua, sg=sg, bmu=bmu: e.tensor_add(ua, sg[:, 0:512],
                                                                   bmu[:, 0:512]),
                     [lev, bmu_ev] + ud)
            uev = kb.ev(st)
            st4_rot.done(si, uev)
            va, vd, vi = v4_rot.take()
            st = dve(lambda e, va=va, ua=ua, ch=ch, ba=ba: e.scalar_tensor_tensor(
                out=va, in0=ua, scalar=lncol[:, ch, 0:1], in1=ba[:, 0:512],
                op0=ALU.mult, op1=ALU.mult), [uev, ba_ev] + vd)
            vev = kb.ev(st)
            u4_rot.done(ui, vev)
            st = dve(lambda e, va=va, ch=ch, ts=ts: e.tensor_scalar_add(
                qin[:, ch, ts * 512:(ts + 1) * 512], va, lncol[:, ch, 1:2]), [vev])
            qev = kb.ev(st)
            v4_rot.done(vi, qev)
            qin_evs.append(qev)
            last = qev
        fb.done(bmu_i, last)
        fb.done(ba_i, last)
        pump(1)
    qin_done = qin_evs[-1]
    tap("qin0", qin[:, 0, :], [qin_done])
    if P4_SUB == "qin":
        maybe_stop("p4a")

    def qkv2_block(och, dram, didx):
        cb, cbd, cbi = cb_rot.take()
        cb_v = cb.rearrange("p (n b) -> p b n", b=16)
        evac_evs = []
        for tsp in range(4):
            bts = []
            dps = [qin_done, wq2_done] + (cbd if tsp == 0 else [])
            for kk in range(2):
                bt, bd, bi = fb.take()
                bts.append([bt, bi])
                dps += bd
            for ch in range(8):
                for kk in range(2):
                    ts = tsp * 2 + kk
                    bt = bts[kk][0]
                    st = pe(lambda e, ch=ch, ts=ts, bt=bt, och=och: e.matmul(
                        bt[:, 0:512], wq2_sb[:, ch, och * 128:(och + 1) * 128],
                        qin[:, ch, ts * 512:(ts + 1) * 512],
                        start=(ch == 0), stop=(ch == 7)),
                        dps if (ch == 0 and kk == 0) else [])
            tailev = kb.ev(st)
            for kk in range(2):
                ts = tsp * 2 + kk
                bt, bi = bts[kk]
                dst = cb_v[:, 2 * ts:2 * ts + 2, :]
                srcv = bt[:, 0:512].rearrange("p (b n) -> p b n", n=256)
                st = act(lambda e, dst=dst, srcv=srcv: e.copy(out=dst, in_=srcv), [tailev])
                eev = kb.ev(st)
                fb.done(bi, eev)
                evac_evs.append(eev)
        spv = kb.dma("pool", "fp", dram[didx, :, :], cb, deps=evac_evs)
        cb_rot.done(cbi, spv)
        return [spv]

    k2_evs = qkv2_block(8, k2_sp, 0) + qkv2_block(9, k2_sp, 1)
    if "cb0" in dbg:
        kb.dma("pool", "fo", dbg["cb0"], q2stg[:, 0, :], deps=k2_evs)
    if P4_SUB == "kv":
        maybe_stop("p4a")
    v2_evs = qkv2_block(10, v2_sp, 0) + qkv2_block(11, v2_sp, 1)
    if P4_SUB == "v":
        maybe_stop("p4a")
    q2sp_evs = []
    for och in range(8):
        q2sp_evs += qkv2_block(och, q2_sp, och)
        pump(2)
        if P4_SUB == "q%d" % och:
            maybe_stop("p4a")
    assert len(q2sp_evs) == 8
    maybe_stop("p4a")

    kb.fence()
    p4a.close()

    p45 = ExitStack()
    att_fm = p45.enter_context(nc.sbuf_tensor("attfm", [128, 8, T], BF16))
    p4b = ExitStack()
    qc2 = p4b.enter_context(nc.sbuf_tensor("qc2", [128, 2, T], BF16))
    v2w = p4b.enter_context(nc.sbuf_tensor("v2w", [128, T], BF16))
    k2dup = p4b.enter_context(nc.sbuf_tensor("k2dup", [128, T], BF16))
    v2t = p4b.enter_context(nc.sbuf_tensor("v2t", [128, KVH, 2048], BF16))
    scb = p4b.enter_context(nc.sbuf_tensor("scb", [128, 4, 512], F32))
    prb = p4b.enter_context(nc.sbuf_tensor("prb", [128, 4, 512], BF16))
    ptb = p4b.enter_context(nc.sbuf_tensor("ptb", [128, 3, 512], BF16))
    a64b = p4b.enter_context(nc.sbuf_tensor("a64b", [128, 3, 256], BF16))
    sxb = p4b.enter_context(nc.sbuf_tensor("sxb", [128, 4, 12], F32))
    mask_sb = p4b.enter_context(nc.sbuf_tensor("masksb", [128, 512], F32))
    mask_ld = kb.dma("sync", "fc", mask_sb[:, :], mask_h[:, :])
    sc_rot = Rot(kb, [scb[:, i, :] for i in range(4)])
    pr_rot = Rot(kb, [prb[:, i, :] for i in range(4)])
    pt_rot = Rot(kb, [ptb[:, i, :] for i in range(3)])
    a6_rot = Rot(kb, [a64b[:, i, :] for i in range(3)])
    sx_rot = Rot(kb, [sxb[:, i, :] for i in range(4)])

    v2t_evs = []
    v2w_last = []
    for half in range(2):
        wlev = kb.dma("sync", "fr", v2w[:, :], v2_sp[half, :, :],
                      deps=list(v2_evs) + v2w_last)
        if half == 0 and "v2w0" in dbg:
            kb.dma("pool", "fo", dbg["v2w0"], v2w[:, :], deps=[wlev])
        for kvh in (2 * half, 2 * half + 1):
            base = (kvh % 2) * 64
            for g in range(NTT):
                bt, bd, bi = bb.take()
                src = v2w[base:base + 64, g * 128:(g + 1) * 128]
                idn = iden_sb[base:base + 64, base:base + 64]
                st = pe(lambda e, src=src, bt=bt, idn=idn: e.transpose(
                    bt[0:128, 0:64], src, idn), [wlev] + bd)
                tev = kb.ev(st)
                st = dve(lambda e, bt=bt, kvh=kvh, g=g: e.tensor_copy(
                    out=v2t[:, kvh, g * 64:(g + 1) * 64], in_=bt[0:128, 0:64]), [tev])
                eev = kb.ev(st)
                bb.done(bi, eev)
                v2t_evs.append(eev)
            pump(2)
        v2w_last = [kb.tail("pe")]
    v2t_done = v2t_evs[-1]
    if "v2t" in dbg:
        kb.dma("pool", "fo", dbg["v2t"], v2t[:, 0, :], deps=[v2t_done])
    if P4B_SUB == "v2t":
        maybe_stop("p4b")

    rotA = Rot(kb, [psB[0][:, 0:512], psB[0][:, 512:1024]])
    rotB = Rot(kb, [psB[1][:, 0:512], psB[1][:, 512:1024]])
    rotA.evs = [list(bb.evs[0]), list(bb.evs[0])]
    rotB.evs = [list(bb.evs[1]), list(bb.evs[1])]

    cur = {"kvh": -1, "k2dup_ready": None, "last_qk": None, "qlev": {}}
    qc2_free = [[], []]
    att_evs = []
    stripes = [(och, hh, gq) for och in range(8) for hh in range(2)
               for gq in range(8)]

    def p4b_stageA(idx):
        och, hh, gq = stripes[idx]
        sl = och % 2
        base = hh * 64
        kvh = och // 2
        if hh == 0 and gq == 0:
            cur["qlev"][och] = kb.dma("sync", "fr", qc2[:, sl, :], q2_sp[och, :, :],
                                      deps=[q2sp_evs[och]] + qc2_free[sl])
            if kvh != cur["kvh"]:
                cur["kvh"] = kvh
                src_half = k2_sp[kvh // 2, (kvh % 2) * 64:(kvh % 2) * 64 + 64, :]
                dd = list(k2_evs)
                if cur["last_qk"] is not None:
                    dd.append(cur["last_qk"])
                e1 = kb.dma("sync", "fr", k2dup[0:64, :], src_half, deps=dd)
                e2 = kb.dma("sync", "fr", k2dup[64:128, :], src_half, deps=[e1])
                cur["k2dup_ready"] = e2
        sc, scd, sci = sc_rot.take()
        sx, sxd, sxi = sx_rot.take()
        btq, bqd, bqi = fb.take()
        tp = (base, 0) if base else None
        deps0 = [cur["qlev"][och], cur["k2dup_ready"], v2t_done] + bqd
        for j in range(4):
            g = gq * 4 + j
            st = pe(lambda e, j=j, g=g, btq=btq, sl=sl, base=base, tp=tp: e.matmul(
                btq[:, j * 128:(j + 1) * 128],
                qc2[base:base + 64, sl, g * 128:(g + 1) * 128],
                k2dup[base:base + 64, g * 128:(g + 1) * 128],
                start=True, stop=True, tile_position=tp), deps0 if j == 0 else [])
        qkev = kb.ev(st)
        cur["last_qk"] = qkev
        if hh == 1 and gq == 7:
            qc2_free[sl] = [qkev]
        st = dve(lambda e, sc=sc, btq=btq: e.scalar_tensor_tensor(
            out=sc, in0=btq[:, 0:512], scalar=0.125, in1=mask_sb[:, :],
            op0=ALU.mult, op1=ALU.add), [qkev, mask_ld] + scd)
        mev = kb.ev(st)
        fb.done(bqi, mev)
        mxevs = []
        for j in range(4):
            st = dve(lambda e, sc=sc, sx=sx, j=j: e.tensor_reduce(
                out=sx[:, j:j + 1], in_=sc[:, j * 128:(j + 1) * 128], axis=AXL.X,
                op=ALU.max, negate=True), ([mev] + sxd) if j == 0 else [])
            mxevs.append(kb.ev(st))
        pr, prd, pri = pr_rot.take()
        exevs = []
        for j in range(4):
            st = act(lambda e, pr=pr, sc=sc, sx=sx, j=j: e.activation(
                out=pr[:, j * 128:(j + 1) * 128], in_=sc[:, j * 128:(j + 1) * 128],
                func=AF.Exp, bias=sx[:, j:j + 1], accum_out=sx[:, 4 + j:5 + j]),
                [mxevs[j]] + (prd if j == 0 else []))
            exevs.append(kb.ev(st))
        sc_rot.done(sci, exevs[-1])
        st = dve(lambda e, sx=sx: e.reciprocal(out=sx[:, 8:12], in_=sx[:, 4:8]),
                 [exevs[-1]])
        rcev = kb.ev(st)
        return dict(och=och, gq=gq, base=base, kvh=kvh, pr=pr, pri=pri,
                    sx=sx, sxi=sxi, exev=exevs[-1], rcev=rcev)

    def p4b_stageB(s):
        base, kvh, gq, och = s["base"], s["kvh"], s["gq"], s["och"]
        btA, adp, ai = rotA.take()
        for j in range(4):
            st = pe(lambda e, btA=btA, pr=s["pr"], j=j: e.transpose(
                btA[:, j * 128:(j + 1) * 128], pr[:, j * 128:(j + 1) * 128],
                iden_sb[:, :]), ([s["exev"]] + adp) if j == 0 else [])
        tprev = kb.ev(st)
        pr_rot.done(s["pri"], tprev)
        pt, ptd, pti = pt_rot.take()
        st = dve(lambda e, pt=pt, btA=btA: e.tensor_copy(out=pt, in_=btA[:, 0:512]),
                 [tprev] + ptd)
        ptcev = kb.ev(st)
        rotA.done(ai, ptcev)
        btv, bvd, bvi = fb.take()
        for j in range(4):
            g = gq * 4 + j
            st = pe(lambda e, btv=btv, pt=pt, kvh=kvh, g=g, j=j: e.matmul(
                btv[:, j * 64:(j + 1) * 64], pt[:, j * 128:(j + 1) * 128],
                v2t[:, kvh, g * 64:(g + 1) * 64], start=True, stop=True),
                ([ptcev] + bvd) if j == 0 else [])
        avev = kb.ev(st)
        pt_rot.done(pti, avev)
        a6, a6d, a6i = a6_rot.take()
        scevs = []
        for j in range(4):
            st = act(lambda e, a6=a6, btv=btv, sx=s["sx"], j=j: e.activation(
                out=a6[:, j * 64:(j + 1) * 64], in_=btv[:, j * 64:(j + 1) * 64],
                func=AF.Copy, scale=sx[:, 8 + j:9 + j]),
                ([avev, s["rcev"]] + a6d) if j == 0 else [])
            scevs.append(kb.ev(st))
        fb.done(bvi, scevs[-1])
        sx_rot.done(s["sxi"], scevs[-1])
        btB, bbd, bi2 = rotB.take()
        tp2 = (0, base) if base else None
        for j in range(4):
            st = pe(lambda e, btB=btB, a6=a6, j=j, base=base, tp2=tp2: e.transpose(
                btB[base:base + 64, j * 128:(j + 1) * 128], a6[:, j * 64:(j + 1) * 64],
                iden_sb[:, :], tile_position=tp2), ([scevs[-1]] + bbd) if j == 0 else [])
        t2ev = kb.ev(st)
        a6_rot.done(a6i, t2ev)
        dst = att_fm[:, och, :].rearrange("p (b n) -> p b n", n=256)[:, :, gq * 32:(gq + 1) * 32]
        dst = dst.rearrange("p b n -> p n b")[base:base + 64]
        src = btB[base:base + 64, 0:512].rearrange("p (n b) -> p n b", b=16)
        st = dve(lambda e, dst=dst, src=src: e.tensor_copy(out=dst, in_=src), [t2ev])
        fev = kb.ev(st)
        rotB.done(bi2, fev)
        att_evs.append(fev)

    pend = []
    for idx in range(len(stripes)):
        pend.append(p4b_stageA(idx))
        if len(pend) > 2:
            p4b_stageB(pend.pop(0))
        if idx % 4 == 0:
            pump(1)
    for s_ in pend:
        p4b_stageB(s_)
    att_done = att_evs[-1]
    tap("attfm0", att_fm[:, 0, :], [att_done])
    maybe_stop("p4b")
    kb.fence()
    p4b.close()

    p5 = ExitStack()
    wp2_sb = p5.enter_context(nc.sbuf_tensor("wp2sb", [128, 8, C], BF16))
    stage5 = p5.enter_context(nc.sbuf_tensor("stage5", [128, 3, 1024], F32))
    u5 = p5.enter_context(nc.sbuf_tensor("u5", [128, 2, 512], F32))
    v5 = p5.enter_context(nc.sbuf_tensor("v5", [128, 2, 512], F32))
    h1f = p5.enter_context(nc.sbuf_tensor("h1f", [128, 8, 512], F32))
    h1bfb = p5.enter_context(nc.sbuf_tensor("h1bfb", [128, 2, 1024], BF16))
    h1tmb = p5.enter_context(nc.sbuf_tensor("h1tmb", [128, 2, 1024], BF16))
    xmstg = p5.enter_context(nc.sbuf_tensor("xmstg", [128, 2, 512], BF16))
    st5_rot = Rot(kb, [stage5[:, i, :] for i in range(3)])
    u5_rot = Rot(kb, [u5[:, 0, :], u5[:, 1, :]])
    v5_rot = Rot(kb, [v5[:, 0, :], v5[:, 1, :]])
    h1bf_rot = Rot(kb, [h1bfb[:, 0, :], h1bfb[:, 1, :]])
    h1tm_rot = Rot(kb, [h1tmb[:, 0, :], h1tmb[:, 1, :]])
    xm_rot = Rot(kb, [xmstg[:, 0, :], xmstg[:, 1, :]])

    wp2_evs = []
    for ch in range(8):
        sg, sd, si = st5_rot.take()
        lev = kb.dma("sync", "fw", sg, wp2_h[ch * 128:(ch + 1) * 128, :], deps=sd)
        st = dve(lambda e, sg=sg, ch=ch: e.tensor_copy(out=wp2_sb[:, ch, :], in_=sg), [lev])
        ev_ = kb.ev(st)
        st5_rot.done(si, ev_)
        wp2_evs.append(ev_)
    wp2_done = wp2_evs[-1]

    prev_bc2 = []
    xm_sp_evs = {}
    h1_free = [[] for _ in range(8)]
    fb5 = Rot(kb, psF[0:4])
    bcA, bcB = psF[4], psF[5]
    bc_prev = [list(fb.evs[4]), list(fb.evs[5])]

    def bcast_at(bank, row_tensor, row_base, col0, n, deps):
        tp = (row_base, 0) if row_base else None
        st = pe(lambda e: e.matmul(bank[:, 0:n], ones4[row_base:row_base + 1, 0:128],
                                   row_tensor[row_base:row_base + 1, col0:col0 + n],
                                   start=True, stop=True, tile_position=tp), list(deps))
        return kb.ev(st)

    for ts in range(NTS):
        bmu_ev = bcast_at(bcA, rows_a, 32, ts * 512, 512, [x1_rows_done] + bc_prev[0])
        br0_ev = bcast_at(bcB, rows_b, 0, ts * 512, 512, [x1_rows_done] + bc_prev[1])
        bmu, br0 = bcA, bcB
        h1f_evs = []
        last_h1 = None
        for och in range(8):
            btp, bdp, bip = fb5.take()
            for ch in range(8):
                st = pe(lambda e, ch=ch, och=och, ts=ts, btp=btp: e.matmul(
                    btp[:, 0:512], wp2_sb[:, ch, och * 128:(och + 1) * 128],
                    att_fm[:, ch, ts * 512:(ts + 1) * 512],
                    start=(ch == 0), stop=(ch == 7)),
                    ([wp2_done, att_done] + bdp) if ch == 0 else [])
            p2ev = kb.ev(st)
            sg, sd, si = st5_rot.take()
            lev = kb.dma("sync", "fr", sg[:, 0:512], x1fm_sp[och, :, ts * 512:(ts + 1) * 512],
                         deps=sd)
            ua, ud, ui = u5_rot.take()
            st = dve(lambda e, ua=ua, sg=sg, bmu=bmu: e.tensor_add(ua, sg[:, 0:512],
                                                                   bmu[:, 0:512]),
                     [lev, bmu_ev] + ud)
            uev = kb.ev(st)
            st5_rot.done(si, uev)
            va, vd, vi = v5_rot.take()
            st = dve(lambda e, va=va, ua=ua, br0=br0: e.tensor_mul(va, ua, br0[:, 0:512]),
                     [uev, br0_ev] + vd)
            wev = kb.ev(st)
            u5_rot.done(ui, wev)
            st = dve(lambda e, va=va, btp=btp, och=och: e.tensor_add(
                h1f[:, och, :], va, btp[:, 0:512]), [wev, p2ev] + h1_free[och])
            h1ev = kb.ev(st)
            v5_rot.done(vi, h1ev)
            fb5.done(bip, h1ev)
            h1f_evs.append(h1ev)
            last_h1 = h1ev
        row2_evs = []
        h1tm_spill_evs = []
        for ttl in range(4):
            tt = ts * 4 + ttl
            hb, hbd, hbi = h1bf_rot.take()
            st = dve(lambda e, hb=hb, ttl=ttl: e.tensor_copy(
                out=hb.rearrange("p (a b) -> p a b", b=128),
                in_=h1f[:, :, ttl * 128:(ttl + 1) * 128]), h1f_evs + hbd)
            cbev = kb.ev(st)
            btx, bdx, bix = bb.take()
            for ch in range(8):
                st = pe(lambda e, ch=ch, btx=btx, hb=hb: e.transpose(
                    btx[:, ch * 128:(ch + 1) * 128], hb[:, ch * 128:(ch + 1) * 128],
                    iden_sb[:, :]), ([cbev] + bdx) if ch == 0 else [])
            tev = kb.ev(st)
            h1bf_rot.done(hbi, tev)
            ht, htd, hti = h1tm_rot.take()
            st = dve(lambda e, ht=ht, btx=btx: e.tensor_copy(out=ht, in_=btx[:, 0:1024]),
                     [tev] + htd)
            htev = kb.ev(st)
            bb.done(bix, htev)
            l2 = ln_stats([(ht[:, 0:512], 512), (ht[:, 512:1024], 512)],
                          tt, FAM_LN2, [htev])
            r1 = make_row(statw[:, tt, FAM_LN2 + 1:FAM_LN2 + 2], rows_b, 32, ttl * 128,
                          [l2["negmu"]] + prev_bc2)
            r2_ = make_row(statw[:, tt, FAM_LN2:FAM_LN2 + 1], rows_b, 64, ttl * 128,
                           [l2["r"]] + prev_bc2)
            row2_evs += [r1, r2_]
            spv = kb.dma("pool", "fp", h1tm_sp[tt, :, :], ht, deps=[htev])
            h1tm_rot.done(hti, [spv, l2["r"]])
            if tt == 0:
                tap("h1t0", ht, [htev])
        bm2_ev = bcast_at(bcA, rows_b, 32, 0, 512, row2_evs + [last_h1])
        br2_ev = bcast_at(bcB, rows_b, 64, 0, 512, row2_evs + [last_h1])
        bm2, br2 = bcA, bcB
        prev_bc2 = [bm2_ev, br2_ev]
        last_xm = None
        for ch in range(8):
            ua, ud, ui = u5_rot.take()
            st = dve(lambda e, ua=ua, ch=ch, bm2=bm2: e.tensor_add(
                ua, h1f[:, ch, :], bm2[:, 0:512]), [bm2_ev] + h1f_evs + ud)
            uev = kb.ev(st)
            va, vd, vi = v5_rot.take()
            st = dve(lambda e, va=va, ua=ua, ch=ch, br2=br2: e.scalar_tensor_tensor(
                out=va, in0=ua, scalar=lncol[:, ch, 2:3], in1=br2[:, 0:512],
                op0=ALU.mult, op1=ALU.mult), [uev, br2_ev] + vd)
            vev = kb.ev(st)
            u5_rot.done(ui, vev)
            xa, xd, xi = xm_rot.take()
            st = dve(lambda e, xa=xa, va=va, ch=ch: e.tensor_scalar_add(
                xa, va, lncol[:, ch, 3:4]), [vev] + xd)
            xev = kb.ev(st)
            v5_rot.done(vi, xev)
            h1_free[ch] = [xev]
            spv = kb.dma("pool", "fp", xm_sp[ch, :, ts * 512:(ts + 1) * 512], xa, deps=[xev])
            xm_rot.done(xi, spv)
            xm_sp_evs[(ts, ch)] = spv
            last_xm = xev
        bc_prev = [[last_xm], [last_xm]]
        pump(4)

    if "xmsp" in dbg:
        kb.dma("pool", "fo", dbg["xmsp"], xm_sp[0, :, :], deps=[xm_sp_evs[(NTS - 1, 7)]])
    fb.evs[4] = list(bc_prev[0])
    fb.evs[5] = list(bc_prev[1])
    for i in range(4):
        fb.evs[i] = list(fb5.evs[i])
    maybe_stop("p5")
    pump_flush()
    kb.fence()
    p5.close()
    p45.close()
    p0.close()
    prow.close()

    p6 = ExitStack()
    w1_sb = p6.enter_context(nc.sbuf_tensor("w1sb", [128, 8, FF], BF16))
    w2_sb = p6.enter_context(nc.sbuf_tensor("w2sb", [128, 16, C], BF16))
    xmg = p6.enter_context(nc.sbuf_tensor("xmg", [128, 8, TG], BF16))
    hmid = p6.enter_context(nc.sbuf_tensor("hmid", [128, 16, TG], BF16))
    accb = p6.enter_context(nc.sbuf_tensor("accb", [128, 2, 8, C], F32))
    wr_sb = p6.enter_context(nc.sbuf_tensor("wrsb", [128, 8, 8], BF16))
    stage6 = p6.enter_context(nc.sbuf_tensor("stage6", [128, 1, 64], F32))
    gt6 = p6.enter_context(nc.sbuf_tensor("gt6", [128, 2, 24], F32))
    h1r = p6.enter_context(nc.sbuf_tensor("h1r", [128, 2, C], BF16))
    xrb = p6.enter_context(nc.sbuf_tensor("xrb", [128, 2, C], F32))
    otb = p6.enter_context(nc.sbuf_tensor("otb", [128, 2, C], F32))
    st6_rot = Rot(kb, [stage6[:, 0, :]])
    gt6_rot = Rot(kb, [gt6[:, 0, :], gt6[:, 1, :]])
    h1r_rot = Rot(kb, [h1r[:, 0, :], h1r[:, 1, :]])
    xrb_rot = Rot(kb, [xrb[:, 0, :], xrb[:, 1, :]])
    otb_rot = Rot(kb, [otb[:, 0, :], otb[:, 1, :]])

    sg, sd, si = st6_rot.take()
    sg_v = sg[:, 0:64].rearrange("p (a b) -> p a b", b=8)
    lev = kb.dma("sync", "fw", sg_v, wr_h.rearrange("(a p) q -> p a q", p=128), deps=sd)
    st = dve(lambda e: e.tensor_copy(out=wr_sb[:, :, :], in_=sg_v), [lev])
    wr_done = kb.ev(st)
    st6_rot.done(si, wr_done)

    out_evs = []
    acc_free = [[[] for _ in range(8)] for _ in range(2)]
    acc_ev = [[[] for _ in range(8)] for _ in range(2)]

    def emit_tail(g, ttl):
        ping = g % 2
        tt = g * 8 + ttl
        hr, hrd, hri = h1r_rot.take()
        hlev = kb.dma("sync", "fr", hr, h1tm_sp[tt, :, :], deps=hrd)
        fa = accb[:, ping, ttl, :]
        st = dve(lambda e, fa=fa, hr=hr: e.tensor_add(fa, fa, hr),
                 [hlev] + acc_ev[ping][ttl])
        fev = kb.ev(st)
        h1r_rot.done(hri, fev)
        fstats = ln_stats([(fa[:, 0:512], 512), (fa[:, 512:1024], 512)],
                          tt, FAM_FFN, [fev])
        xr, xrd, xri = xrb_rot.take()
        xlev = kb.dma("sync", "fr", xr, x1tm_sp[tt, :, :], deps=xrd)
        ob, obd, obi = otb_rot.take()
        st = dve(lambda e, ob=ob, fa=fa, tt=tt: e.tensor_scalar(
            out=ob, in0=fa, scalar1=statw[:, tt, FAM_FFN + 1:FAM_FFN + 2],
            scalar2=statw[:, tt, FAM_FFN:FAM_FFN + 1], op0=ALU.add, op1=ALU.mult),
            [fstats["r"], fstats["negmu"]] + obd)
        t2ev = kb.ev(st)
        acc_free[ping][ttl] = [t2ev]
        st = dve(lambda e, ob=ob, xr=xr: e.tensor_add(ob, ob, xr), [t2ev, xlev])
        oev = kb.ev(st)
        xrb_rot.done(xri, oev)
        sev = kb.dma("pool", "fo", out_h[tt * 128:(tt + 1) * 128, :], ob, deps=[oev])
        otb_rot.done(obi, sev)
        out_evs.append(sev)
    w1_ready = None
    w2_ready = None
    xmg_free = []
    for grp in range(NGRP):
        xmg_evs = []
        for ch in range(8):
            dep = [xm_sp_evs[(2 * grp, ch)], xm_sp_evs[(2 * grp + 1, ch)]] + xmg_free
            lev = kb.dma("sync", "fr", xmg[:, ch, :], xm_sp[ch, :, grp * TG:(grp + 1) * TG],
                         deps=dep)
            xmg_evs.append(lev)
        xmg_done = xmg_evs[-1]
        gate_evs = [None] * 8
        for ttl in range(8):
            tt = grp * 8 + ttl
            bt, bd, bi = fb.take()
            for ch in range(8):
                st = pe(lambda e, ch=ch, ttl=ttl, bt=bt: e.matmul(
                    bt[:, 0:8], xmg[:, ch, ttl * 128:(ttl + 1) * 128], wr_sb[:, ch, :],
                    start=(ch == 0), stop=(ch == 7)),
                    ([wr_done] + xmg_evs + bd) if ch == 0 else [])
            mmev = kb.ev(st)
            ga, gd, gi = gt6_rot.take()
            st = dve(lambda e, ga=ga, bt=bt: e.tensor_reduce(
                out=ga[:, 0:1], in_=bt[:, 0:8], axis=AXL.X, op=ALU.max, negate=True),
                [mmev] + gd)
            st = act(lambda e, ga=ga, bt=bt: e.activation(
                out=ga[:, 3:11], in_=bt[:, 0:8], func=AF.Exp, bias=ga[:, 0:1],
                accum_out=ga[:, 1:2]), [kb.ev(st)])
            exv = kb.ev(st)
            fb.done(bi, exv)
            st = dve(lambda e, ga=ga: e.reciprocal(out=ga[:, 2:3], in_=ga[:, 1:2]), [exv])
            st = dve(lambda e, ga=ga: e.tensor_scalar(
                out=ga[:, 3:11], in0=ga[:, 3:11], scalar1=ga[:, 2:3], scalar2=None,
                op0=ALU.mult), [kb.ev(st)])
            gev = kb.ev(st)
            st = dve(lambda e, ga=ga: e.tensor_reduce(
                out=ga[:, 19:20], in_=ga[:, 3:11], axis=AXL.X, op=ALU.max), [gev])
            st = dve(lambda e, ga=ga: e.tensor_scalar(
                out=ga[:, 11:19], in0=ga[:, 3:11], scalar1=ga[:, 19:20], scalar2=None,
                op0=ALU.is_equal), [kb.ev(st)])
            st = dve(lambda e, ga=ga: e.scalar_tensor_tensor(
                out=ga[:, 11:19], in0=ga[:, 11:19], scalar=-1e30, in1=ga[:, 3:11],
                op0=ALU.mult, op1=ALU.add), [kb.ev(st)])
            st = dve(lambda e, ga=ga: e.tensor_reduce(
                out=ga[:, 20:21], in_=ga[:, 11:19], axis=AXL.X, op=ALU.max), [kb.ev(st)])
            st = dve(lambda e, ga=ga: e.tensor_scalar(
                out=ga[:, 11:19], in0=ga[:, 3:11], scalar1=ga[:, 20:21], scalar2=None,
                op0=ALU.is_ge), [kb.ev(st)])
            st = dve(lambda e, ga=ga, tt=tt: e.tensor_mul(
                gates[:, tt, :], ga[:, 3:11], ga[:, 11:19]), [kb.ev(st)])
            gate_evs[ttl] = kb.ev(st)
            gt6_rot.done(gi, gate_evs[ttl])
        if grp == 0:
            tap("gates", gates[:, 0, :], [gate_evs[0]])
        for e_ in range(NE):
            if w1_ready is None:
                w1_ready = kb.dma("sync", "fr", w1_sb[:, :, :],
                                  w1b[e_].rearrange("a p f -> p a f"),
                                  deps=_pump["done"][("w1", e_)])
            if w2_ready is None:
                w2_ready = kb.dma("sync", "fr", w2_sb[:, :, :],
                                  w2b[e_].rearrange("a p f -> p a f"),
                                  deps=_pump["done"][("w2", e_)])
            gelu_evs = []
            last_w1_mm = None
            for f in range(16):
                bts = []
                dps = [w1_ready, xmg_done]
                for kk in range(2):
                    bt, bd, bi = fb.take()
                    bts.append([bt, bi])
                    dps += bd
                for ch in range(8):
                    for kk in range(2):
                        bt = bts[kk][0]
                        st = pe(lambda e, ch=ch, kk=kk, f=f, bt=bt: e.matmul(
                            bt[:, 0:512], w1_sb[:, ch, f * 128:(f + 1) * 128],
                            xmg[:, ch, kk * 512:(kk + 1) * 512],
                            start=(ch == 0), stop=(ch == 7)),
                            dps if (ch == 0 and kk == 0) else [])
                last_w1_mm = st
                mmev = kb.ev(st)
                for kk in range(2):
                    bt, bi = bts[kk]
                    st = act(lambda e, bt=bt, f=f, kk=kk: e.activation(
                        out=hmid[:, f, kk * 512:(kk + 1) * 512], in_=bt[:, 0:512],
                        func=AF.Gelu_apprx_tanh), [mmev])
                    eev = kb.ev(st)
                    gelu_evs.append(eev)
                    fb.done(bi, eev)
            w1_free = kb.ev(last_w1_mm)
            if e_ == NE - 1:
                xmg_free = [w1_free]
            if not (grp == NGRP - 1 and e_ == NE - 1):
                ne_ = (e_ + 1) % NE
                w1_ready = kb.dma("sync", "fr", w1_sb[:, :, :],
                                  w1b[ne_].rearrange("a p f -> p a f"),
                                  deps=[w1_free] + _pump["done"][("w1", ne_)])
            gelu_done = gelu_evs[-1]
            if grp > 0 and e_ < 4:
                emit_tail(grp - 1, 2 * e_)
                emit_tail(grp - 1, 2 * e_ + 1)
            last_w2_mm = None
            for ttl in range(8):
                tt = grp * 8 + ttl
                for oh in range(2):
                    bt, bd, bi = fb.take()
                    for f in range(16):
                        st = pe(lambda e, f=f, ttl=ttl, oh=oh, bt=bt: e.matmul(
                            bt[:, 0:512], hmid[:, f, ttl * 128:(ttl + 1) * 128],
                            w2_sb[:, f, oh * 512:(oh + 1) * 512],
                            start=(f == 0), stop=(f == 15)),
                            ([w2_ready, gelu_done] + bd) if f == 0 else [])
                    last_w2_mm = st
                    mmev = kb.ev(st)
                    adst = accb[:, grp % 2, ttl, oh * 512:(oh + 1) * 512]
                    gcol = gates[:, tt, e_:e_ + 1]
                    if e_ == 0:
                        st = dve(lambda e, adst=adst, bt=bt, gcol=gcol: e.tensor_scalar(
                            out=adst, in0=bt[:, 0:512], scalar1=gcol, scalar2=None,
                            op0=ALU.mult),
                            [mmev, gate_evs[ttl]] + acc_free[grp % 2][ttl])
                    else:
                        st = dve(lambda e, adst=adst, bt=bt, gcol=gcol: e.scalar_tensor_tensor(
                            out=adst, in0=bt[:, 0:512], scalar=gcol, in1=adst,
                            op0=ALU.mult, op1=ALU.add), [mmev])
                    aev = kb.ev(st)
                    if oh == 0:
                        acc_ev[grp % 2][ttl] = [aev]
                    else:
                        acc_ev[grp % 2][ttl].append(aev)
                    fb.done(bi, aev)
            w2_free = kb.ev(last_w2_mm)
            if not (grp == NGRP - 1 and e_ == NE - 1):
                ne_ = (e_ + 1) % NE
                w2_ready = kb.dma("sync", "fr", w2_sb[:, :, :],
                                  w2b[ne_].rearrange("a p f -> p a f"),
                                  deps=[w2_free] + _pump["done"][("w2", ne_)])
    for ttl in range(8):
        emit_tail(NGRP - 1, ttl)

    kb.op("pool", lambda e: e.nop(), deps=list(out_evs[-24:]))


_NC_CACHE = {}


def _get_nc():
    key = tuple(DEBUG_TAPS)
    if key not in _NC_CACHE:
        _NC_CACHE[key] = build_program()
    return _NC_CACHE[key]


def _host_inputs(inputs):
    f32 = np.float32
    x = np.asarray(inputs["L0_attn_x"], f32)
    wq1T = np.ascontiguousarray(np.asarray(inputs["L0_attn_qkv_weight"], f32).T)
    wp1T = np.ascontiguousarray(np.asarray(inputs["L0_attn_proj_weight"], f32).T)
    bp1 = np.asarray(inputs["L0_attn_proj_bias"], f32).reshape(1, C)
    wq2T = np.ascontiguousarray(np.asarray(inputs["L0_ffn_qkv_weight"], f32).T)
    wp2T = np.ascontiguousarray(np.asarray(inputs["L0_ffn_proj_weight"], f32).T)
    wrT = np.ascontiguousarray(np.asarray(inputs["L0_ffn_router_weight"], f32).T)
    w1 = np.ascontiguousarray(np.asarray(inputs["L0_ffn_moe_w1"], f32))
    w2 = np.ascontiguousarray(np.asarray(inputs["L0_ffn_moe_w2"], f32))
    g1 = np.asarray(inputs["L0_ffn_ln1_weight"], f32)
    b1 = np.asarray(inputs["L0_ffn_ln1_bias"], f32)
    g2 = np.asarray(inputs["L0_ffn_ln2_weight"], f32)
    b2 = np.asarray(inputs["L0_ffn_ln2_bias"], f32)
    lncols = np.stack([v.reshape(8, 128).T for v in (g1, b1, g2, b2)], axis=2)
    lncols = np.ascontiguousarray(lncols)

    iden = np.eye(128, dtype=np.float32).astype(ml_dtypes.bfloat16)
    ones4 = np.ones((128, 512), np.float32).astype(ml_dtypes.bfloat16)
    onescol = np.ones((128, 1), np.float32).astype(ml_dtypes.bfloat16)
    mask = np.full((128, 128), NEG_BIG, np.float32)
    for i in range(8):
        for s_ in range(16):
            for t_ in range(s_ + 1):
                mask[i * 16 + s_, i * 16 + t_] = 0.0
    mask = np.ascontiguousarray(np.tile(mask, (1, 4)))

    shared = dict(wq1T=wq1T, wp1T=wp1T, bp1=bp1, wq2T=wq2T, wp2T=wp2T, wrT=wrT,
                  w1=w1, w2=w2, lncols=lncols, iden=iden, ones4=ones4,
                  onescol=onescol, mask=mask)
    in_maps = []
    for c in range(NCORE):
        xs = np.ascontiguousarray(x[:, c * NLOC:(c + 1) * NLOC, :].reshape(T, C))
        m = dict(shared)
        m["x_tm"] = xs
        m["x_fm"] = np.ascontiguousarray(xs.T)
        in_maps.append(m)
    return in_maps


def kernel(**inputs):
    nc = _get_nc()
    in_maps = _host_inputs(inputs)
    res = run_bass_kernel_spmd(nc, in_maps, core_ids=list(range(NCORE)))
    kernel.last_results = res
    out = np.empty((B, NSEQ, C), np.float32)
    for c in range(NCORE):
        out[:, c * NLOC:(c + 1) * NLOC, :] = \
            np.asarray(res.results[c]["out"]).reshape(B, NLOC, C)
    return out



# version 08 (final)
# speedup vs baseline: 1.1071x; 1.0053x over previous
import numpy as np
import ml_dtypes
from contextlib import ExitStack

import concourse.bass as bass
import concourse.mybir as mybir
from concourse.bass_utils import run_bass_kernel_spmd

F32 = mybir.dt.float32
BF16 = mybir.dt.bfloat16
AF = mybir.ActivationFunctionType
ALU = mybir.AluOpType
AXL = mybir.AxisListType

NCORE = 8
B = 16
NSEQ = 2048
NLOC = NSEQ // NCORE
C = 1024
T = B * NLOC
NTT = T // 128
NTS = T // 512
H1, D1 = 32, 32
H2, D2, KVH = 16, 64, 4
FF = 2048
NE = 8
EPS_LN = 1e-5
EPS_ATTN = 1e-15
NEG_BIG = -3.0e38
VKW = B * 8 * 33
NGRP = 4
TG = T // NGRP

FAM_X, FAM_X1, FAM_Y, FAM_LN2, FAM_FFN = 0, 10, 22, 30, 38
SW = 46

DEBUG_TAPS = []
STOP_AFTER = None
P3_TILES = NTT
P3_SUB = "all"
P4_SUB = "all"
P4B_SUB = "all"
P4B_CHAIN = 8
DEBUG_SHAPES = {
    "k0": (128, 1024), "v0": (128, 1056), "vkbf": (128, VKW),
    "qh0": (128, T), "projin0": (128, C), "y0": (128, C),
    "x1t0": (128, C), "qin0": (128, T), "q20": (128, T),
    "attfm0": (128, T), "h1t0": (128, C), "xm0": (128, T),
    "gates": (128, NE), "acc0": (128, C), "k2sp": (128, T), "v2t": (128, 2048),
    "xmsp": (128, T), "v2sp": (128, T), "v2w0": (128, T), "cb0": (128, T), "rows_a": (128, T), "rows_b": (128, T), "statw": (128, NTT * SW),
}

ENGINES = ("pe", "act", "dve", "pool", "sync")
FAMS = ("cc",)
NDMASEM = 24


class Ev:
    __slots__ = ("sem", "value", "step")
    def __init__(self, sem, step):
        self.sem = sem
        self.value = None
        self.step = step


class Step:
    __slots__ = ("fn", "waits", "ev", "inc", "_sem", "_inc_amt", "_engine")
    def __init__(self, fn):
        self.fn = fn
        self.waits = []
        self.ev = None
        self.inc = 0


class _StopBuild(Exception):
    pass


class KB:
    def __init__(self, nc, ctx):
        self.nc = nc
        self.streams = {e: [] for e in ENGINES}
        self.sems = {}
        for s in ENGINES + FAMS:
            self.sems[s] = ctx.enter_context(nc.semaphore("s_" + s))
        self.dma_pools = {}
        for q in ("sync", "pool", "act", "dve"):
            n = NDMASEM if q in ("sync", "pool") else 4
            sems = []
            for i in range(n):
                nm = "d_%s%d" % (q, i)
                self.sems[nm] = ctx.enter_context(nc.semaphore(nm))
                sems.append(nm)
            self.dma_pools[q] = {"sems": sems, "i": 0, "last": [None] * n}
        self.fence_evs = []
        self.fence_todo = set()

    def op(self, engine, fn, deps=(), sem=None, inc=1):
        st = Step(fn)
        if engine in self.fence_todo:
            self.fence_todo.discard(engine)
            st.waits.extend(self.fence_evs)
        for d in deps:
            if d is not None:
                st.waits.append(d)
        st._sem = sem or engine
        st._inc_amt = inc
        st._engine = engine
        self.streams[engine].append(st)
        return st

    def ev(self, step):
        if step.ev is None:
            step.ev = Ev(step._sem, step)
            step.inc = step._inc_amt
        return step.ev

    def tail(self, engine):
        s = self.streams[engine]
        return self.ev(s[-1]) if s else None

    def fence(self, extra=()):
        evs = [self.tail(e) for e in ENGINES] + list(extra)
        for p in self.dma_pools.values():
            evs += [e for e in p["last"] if e is not None]
        self.fence_evs = [e for e in evs if e is not None]
        self.fence_todo = set(ENGINES)

    def dma(self, queue, fam, out, in_, deps=(), **kw):
        p = self.dma_pools[queue]
        n = len(p["sems"])
        i = p["i"] % n
        p["i"] += 1
        dd = list(deps)
        if p["last"][i] is not None:
            dd.append(p["last"][i])
        st = self.op(queue,
                     lambda e, out=out, in_=in_, kw=kw: e.dma_start(out=out, in_=in_, **kw),
                     deps=dd, sem=p["sems"][i], inc=16)
        ev = self.ev(st)
        p["last"][i] = ev
        return ev

    def finalize(self):
        counts = {s: 0 for s in self.sems}
        for e in ENGINES:
            for st in self.streams[e]:
                if st.ev is not None:
                    counts[st._sem] += st.inc
                    st.ev.value = counts[st._sem]

    def emit(self, engine, eng):
        seen = {}
        for st in self.streams[engine]:
            for w in st.waits:
                assert w.value is not None, "un-finalized event"
                if seen.get(w.sem, 0) >= w.value:
                    continue
                seen[w.sem] = w.value
                eng.wait_ge(self.sems[w.sem], w.value)
            ins = st.fn(eng)
            if st.ev is not None:
                ins.then_inc(self.sems[st._sem], st.inc)


class Rot:
    def __init__(self, kb, items):
        self.items = items
        self.evs = [[] for _ in items]
        self.i = 0

    def take(self):
        i = self.i % len(self.items)
        self.i += 1
        return self.items[i], list(self.evs[i]), i

    def done(self, i, evs):
        if not isinstance(evs, (list, tuple)):
            evs = [evs]
        self.evs[i] = [e for e in evs if e is not None]


def bc_last(ap, n):
    return bass.AP(ap.tensor, ap.offset, [list(d) for d in ap.ap] + [[0, n]])


def build_program():
    nc = bass.Bass("TRN2", target_bir_lowering=False)
    ctx = ExitStack()
    kb = KB(nc, ctx)
    try:
        _build_phases(nc, ctx, kb)
    except _StopBuild:
        pass
    _do_finalize(kb, nc)
    try:
        ctx.close()
    except AssertionError:
        pass
    return nc


def _do_finalize(kb, nc):
    tails = [kb.tail(e) for e in ENGINES]
    kb.op("pool", lambda e: e.nop(), deps=[t for t in tails if t is not None])
    kb.finalize()
    with nc.Block() as block:
        @block.sync
        def _(eng):
            kb.emit("sync", eng)

        @block.scalar
        def _(eng):
            kb.emit("act", eng)

        @block.vector
        def _(eng):
            kb.emit("dve", eng)

        @block.tensor
        def _(eng):
            kb.emit("pe", eng)

        @block.gpsimd
        def _(eng):
            kb.emit("pool", eng)


def _build_phases(nc, ctx, kb):
    E = ctx.enter_context

    def din(name, shape, dt=F32):
        return nc.dram_tensor(name, shape, dt, kind="ExternalInput").ap()

    x_tm_h = din("x_tm", [T, C])
    x_fm_h = din("x_fm", [C, T])
    wq1_h = din("wq1T", [C, 3 * C])
    wp1_h = din("wp1T", [C, C])
    bp1_h = din("bp1", [1, C])
    wq2_h = din("wq2T", [C, 1536])
    wp2_h = din("wp2T", [C, C])
    wr_h = din("wrT", [C, NE])
    w1_h = din("w1", [NE, C, FF])
    w2_h = din("w2", [NE, FF, C])
    lncol_h = din("lncols", [128, 8, 4])
    iden_h = din("iden", [128, 128], BF16)
    ones4_h = din("ones4", [128, 512], BF16)
    onesc_h = din("onescol", [128, 1], BF16)
    mask_h = din("mask", [128, 512])

    out_h = nc.dram_tensor("out", [T, C], F32, kind="ExternalOutput").ap()

    vk_in = nc.dram_tensor("vk_in", [128, VKW], F32, kind="Internal").ap()
    vk_out = nc.dram_tensor("vk_out", [128, VKW], F32, kind="Internal").ap()
    x1fm_sp = nc.dram_tensor("x1fm_sp", [8, 128, T], F32, kind="Internal").ap()
    x1tm_sp = nc.dram_tensor("x1tm_sp", [NTT, 128, C], F32, kind="Internal").ap()
    h1tm_sp = nc.dram_tensor("h1tm_sp", [NTT, 128, C], BF16, kind="Internal").ap()
    q2_sp = nc.dram_tensor("q2_sp", [8, 128, T], BF16, kind="Internal").ap()
    projin_sp = nc.dram_tensor("projin_sp", [NTT, 128, C], BF16, kind="Internal").ap()
    k2_sp = nc.dram_tensor("k2_sp", [2, 128, T], BF16, kind="Internal").ap()
    v2_sp = nc.dram_tensor("v2_sp", [2, 128, T], BF16, kind="Internal").ap()
    xm_sp = nc.dram_tensor("xm_sp", [8, 128, T], BF16, kind="Internal").ap()
    w1b = nc.dram_tensor("w1b", [NE, 8, 128, FF], BF16, kind="Internal").ap()
    w2b = nc.dram_tensor("w2b", [NE, 16, 128, C], BF16, kind="Internal").ap()

    dbg = {}
    _BF_TAPS = {"k2sp", "v2sp", "q20", "projin0", "xmsp"}
    for name in DEBUG_TAPS:
        dt_ = BF16 if name in _BF_TAPS else F32
        dbg[name] = nc.dram_tensor("dbg_" + name, list(DEBUG_SHAPES[name]), dt_,
                                   kind="ExternalOutput").ap()

    def tap(name, src_ap, deps):
        if name in dbg:
            tmp = dbg[name]
            return kb.dma("pool", "fo", tmp, src_ap, deps=deps)
        return None

    statw = E(nc.sbuf_tensor("statw", [128, NTT, SW], F32))
    gates = E(nc.sbuf_tensor("gates", [128, NTT, NE], F32))
    lncol = E(nc.sbuf_tensor("lncolsb", [128, 8, 4], F32))
    iden_sb = E(nc.sbuf_tensor("idensb", [128, 128], BF16))
    ones4 = E(nc.sbuf_tensor("ones4sb", [128, 512], BF16))
    onesc = E(nc.sbuf_tensor("onescsb", [128, 1], BF16))
    statbf = E(nc.sbuf_tensor("statbf", [128, 4], BF16))
    cknst = E(nc.sbuf_tensor("cknst", [128, 2], F32))
    sqs = E(nc.sbuf_tensor("sqs", [128, 512], F32))

    psF = [E(nc.psum_tensor("pf%d" % i, [128, 512], F32)) for i in range(6)]
    psB = [E(nc.psum_tensor("pb%d" % i, [128, 1024], BF16)) for i in range(2)]
    fb = Rot(kb, psF)
    bb = Rot(kb, psB)
    statbf_rot = Rot(kb, [statbf[:, i:i + 1] for i in range(4)])

    prow = ExitStack()
    rows_a = prow.enter_context(nc.sbuf_tensor("rows_a", [128, T], BF16))
    rows_b = prow.enter_context(nc.sbuf_tensor("rows_b", [128, T], BF16))
    colsums = prow.enter_context(nc.sbuf_tensor("colsums", [128, 4096], BF16))

    def maybe_stop(tag):
        if STOP_AFTER == tag:
            raise _StopBuild()

    def pe(fn, deps=()):
        return kb.op("pe", fn, deps)

    def dve(fn, deps=()):
        return kb.op("dve", fn, deps)

    def act(fn, deps=()):
        return kb.op("act", fn, deps)

    def pool(fn, deps=()):
        return kb.op("pool", fn, deps)

    def ln_stats(pieces, tt, fam, deps):
        s = lambda k: statw[:, tt, fam + k:fam + k + 1]
        n_feat = float(sum(w for _, w in pieces))
        ap0, w0 = pieces[0]
        st = dve(lambda e: e.reduce_sum(out=s(2), in_=ap0, axis=AXL.X), deps)
        ev_sum = kb.ev(st)
        st = act(lambda e: e.activation(out=sqs[:, 0:w0], in_=ap0, func=AF.Square,
                                        accum_out=s(3)), deps + [ev_sum])
        ev_sq = kb.ev(st)
        for ap1, w1_ in pieces[1:]:
            st = dve(lambda e, ap1=ap1: e.reduce_sum(out=s(6), in_=ap1, axis=AXL.X), deps)
            e2 = kb.ev(st)
            st = pool(lambda e: e.tensor_add(s(2), s(2), s(6)), [ev_sum, e2])
            ev_sum = kb.ev(st)
            st = act(lambda e, ap1=ap1, w1_=w1_: e.activation(
                out=sqs[:, 0:w1_], in_=ap1, func=AF.Square, accum_out=s(7)), deps + [e2, ev_sq])
            e2 = kb.ev(st)
            st = pool(lambda e: e.tensor_add(s(3), s(3), s(7)), [ev_sq, e2])
            ev_sq = kb.ev(st)
        st = pool(lambda e: e.tensor_scalar_mul(s(4), s(2), 1.0 / n_feat), [ev_sum])
        ev_mu = kb.ev(st)
        st = pool(lambda e: e.tensor_scalar_mul(s(1), s(2), -1.0 / n_feat), [ev_sum])
        ev_negmu = kb.ev(st)
        st = pool(lambda e: e.tensor_scalar_mul(s(5), s(3), 1.0 / n_feat), [ev_sq])
        kb.ev(st)
        st = pool(lambda e: e.tensor_mul(s(6), s(4), s(4)), [ev_mu])
        m2ev = kb.ev(st)
        st = pool(lambda e: e.tensor_sub(s(5), s(5), s(6)), [m2ev])
        ev_var = kb.ev(st)
        st = act(lambda e: e.activation(out=s(7), in_=s(5), func=AF.Ln, bias=cknst[:, 0:1]), [ev_var])
        lnev = kb.ev(st)
        st = act(lambda e: e.activation(out=s(0), in_=s(7), func=AF.Exp, scale=cknst[:, 1:2]), [lnev])
        ev_r = kb.ev(st)
        return {"r": ev_r, "negmu": ev_negmu, "var": ev_var, "var_ap": s(5)}

    def make_row(col_ap, dst_tensor, row_base, col0, deps, ncols=128):
        cb, cbd, cbi = statbf_rot.take()
        st = dve(lambda e: e.tensor_copy(out=cb, in_=col_ap), list(deps) + cbd)
        cev = kb.ev(st)
        bt, bd, bi = bb.take()
        tp = (0, row_base) if row_base else None
        st = pe(lambda e: e.transpose(bt[row_base:row_base + 1, 0:ncols], cb, iden_sb[:, :],
                                      tile_position=tp), [cev] + bd)
        tev = kb.ev(st)
        st = dve(lambda e: e.tensor_copy(
            out=dst_tensor[row_base:row_base + 1, col0:col0 + ncols],
            in_=bt[row_base:row_base + 1, 0:ncols]), [tev])
        rev = kb.ev(st)
        bb.done(bi, rev)
        statbf_rot.done(cbi, rev)
        return rev

    def bcast(row_tensor, row_base, col0, n, deps):
        bt, bd, bi = fb.take()
        tp = (row_base, 0) if row_base else None
        st = pe(lambda e: e.matmul(bt[:, 0:n], ones4[row_base:row_base + 1, 0:128],
                                   row_tensor[row_base:row_base + 1, col0:col0 + n],
                                   start=True, stop=True, tile_position=tp), list(deps) + bd)
        return bt, kb.ev(st), bi

    kb.ev(kb.op("dve", lambda e: e.memset(cknst[:, 0:1], EPS_LN)))
    kb.ev(kb.op("dve", lambda e: e.memset(cknst[:, 1:2], -0.5)))
    ld_c = []
    for dst, src in ((iden_sb[:, :], iden_h[:, :]), (ones4[:, :], ones4_h[:, :]),
                     (onesc[:, :], onesc_h[:, :]),
                     (lncol[:, :, :], lncol_h[:, :, :])):
        ld_c.append(kb.dma("sync", "fc", dst, src))


    p0 = ExitStack()
    NPSL = 2
    wstage = p0.enter_context(nc.sbuf_tensor("wstage", [128, NPSL, C], F32))
    wcst = p0.enter_context(nc.sbuf_tensor("wcst", [128, NPSL, C], BF16))
    _jobs = [("w1", e_, ch, hf) for e_ in range(NE) for ch in range(8)
             for hf in range(2)] + \
            [("w2", e_, ch, 0) for e_ in range(NE) for ch in range(16)]

    def _job_aps(i):
        kind, e_, ch, hf = _jobs[i]
        if kind == "w1":
            src = w1_h[e_, ch * 128:(ch + 1) * 128, hf * C:(hf + 1) * C]
            dst = w1b[e_, ch, :, hf * C:(hf + 1) * C]
        else:
            src = w2_h[e_, ch * 128:(ch + 1) * 128, :]
            dst = w2b[e_, ch, :, :]
        return kind, e_, src, dst

    _pump = {"issue": 0, "proc": 0, "load": {}, "store": [None] * NPSL, "done": {}}

    def _pump_issue_to(bound):
        while _pump["issue"] < min(bound, len(_jobs)):
            i = _pump["issue"]
            _pump["issue"] = i + 1
            kind, e_, src, dst = _job_aps(i)
            sl = i % NPSL
            dep = [_pump["store"][sl]] if _pump["store"][sl] else []
            _pump["load"][i] = kb.dma("pool", "fs", wstage[:, sl, :], src, deps=dep)

    def pump(k=1):
        for _ in range(k):
            _pump_issue_to(_pump["proc"] + NPSL)
            i = _pump["proc"]
            if i >= len(_jobs):
                return
            _pump["proc"] = i + 1
            kind, e_, src, dst = _job_aps(i)
            sl = i % NPSL
            dep = [_pump["load"].pop(i)]
            if _pump["store"][sl] is not None:
                dep.append(_pump["store"][sl])
            st = act(lambda e, sl=sl: e.copy(out=wcst[:, sl, :],
                                             in_=wstage[:, sl, :]), dep)
            cev = kb.ev(st)
            sev = kb.dma("pool", "fs", dst, wcst[:, sl, :], deps=[cev])
            _pump["store"][sl] = sev
            _pump["done"].setdefault((kind, e_), []).append(sev)

    def pump_flush():
        while _pump["proc"] < len(_jobs):
            pump(2)


    def cc_test(tag):
        if STOP_AFTER == tag:
            z0 = kb.dma("pool", "fp", vk_in[0:128, 0:128], sqs[:, 0:128], deps=list(ld_c))
            st_ = kb.op("pool", lambda e: e.collective_compute(
                "AllReduce", ALU.add, replica_groups=[list(range(NCORE))],
                ins=[vk_in.opt()], outs=[vk_out.opt()]), deps=[z0], sem="cc", inc=1)
            kb.ev(st_)
            raise _StopBuild()
    if STOP_AFTER == "cc0":
        z0 = kb.dma("pool", "fp", vk_in[0:128, 0:128], sqs[:, 0:128], deps=list(ld_c))
        st = kb.op("pool", lambda e: e.collective_compute(
            "AllReduce", ALU.add, replica_groups=[list(range(NCORE))],
            ins=[vk_in.opt()], outs=[vk_out.opt()]), deps=[z0], sem="cc", inc=1)
        kb.ev(st)
        raise _StopBuild()

    p12 = ExitStack()
    xfmbf = p12.enter_context(nc.sbuf_tensor("xfmbf", [128, 8, T], BF16))
    stage_t = p12.enter_context(nc.sbuf_tensor("stage", [128, 3, 2048], F32))
    xslab = p12.enter_context(nc.sbuf_tensor("xslab", [128, 2, C], F32))
    p1only = ExitStack()
    wkv_sb = p1only.enter_context(nc.sbuf_tensor("wkv", [128, 8, 2048], BF16))
    kpool = p1only.enter_context(nc.sbuf_tensor("kpool", [128, 3, 1024], BF16))
    vpool = p1only.enter_context(nc.sbuf_tensor("vpool", [128, 3, 1056], BF16))
    vksb = p1only.enter_context(nc.sbuf_tensor("vksb", [128, VKW], F32))
    st_rot = Rot(kb, [stage_t[:, i, :] for i in range(3)])
    xslab_rot = Rot(kb, [xslab[:, 0, :], xslab[:, 1, :]])

    xstat_r = []
    for tt in range(NTT):
        sl_ap, sl_dep, sl_i = xslab_rot.take()
        lev = kb.dma("sync", "fx", sl_ap, x_tm_h[tt * 128:(tt + 1) * 128, :],
                     deps=sl_dep + (list(ld_c) if tt == 0 else []))
        evs = ln_stats([(sl_ap[:, 0:512], 512), (sl_ap[:, 512:1024], 512)],
                       tt, FAM_X, [lev])
        rev = make_row(statw[:, tt, FAM_X + 1:FAM_X + 2], rows_a, 0, tt * 128, [evs["negmu"]])
        xstat_r.append(evs["r"])
        xslab_rot.done(sl_i, [evs["r"], rev])
        pump(1)
    negmu0_rows_done = kb.tail("dve")
    cc_test("cc1")

    xfm_evs = []
    for ch in range(8):
        for half in range(2):
            sg, sd, si = st_rot.take()
            lev = kb.dma("sync", "fx", sg,
                         x_fm_h[ch * 128:(ch + 1) * 128, half * 2048:(half + 1) * 2048], deps=sd)
            st = dve(lambda e, sg=sg, ch=ch, half=half: e.tensor_copy(
                out=xfmbf[:, ch, half * 2048:(half + 1) * 2048], in_=sg), [lev])
            ev_ = kb.ev(st)
            st_rot.done(si, ev_)
            xfm_evs.append(ev_)
    xfm_done = xfm_evs[-1]

    wkv_evs = []
    for ch in range(8):
        sg, sd, si = st_rot.take()
        lev = kb.dma("sync", "fw", sg, wq1_h[ch * 128:(ch + 1) * 128, 1024:3072], deps=sd)
        st = dve(lambda e, sg=sg, ch=ch: e.tensor_copy(out=wkv_sb[:, ch, :], in_=sg), [lev])
        ev_ = kb.ev(st)
        st_rot.done(si, ev_)
        wkv_evs.append(ev_)
    wkv_done = wkv_evs[-1]
    cc_test("cc2")

    csk_evs = []
    for kvo in range(4):
        bt, bd, bi = fb.take()
        for ch in range(8):
            st = pe(lambda e, ch=ch, kvo=kvo, bt=bt: e.matmul(
                bt[0:1, 0:512], onesc[:, :], wkv_sb[:, ch, kvo * 512:(kvo + 1) * 512],
                start=(ch == 0), stop=(ch == 7)),
                ([wkv_done, ] + list(ld_c) + bd) if ch == 0 else [])
        mev = kb.ev(st)
        st = dve(lambda e, kvo=kvo, bt=bt: e.tensor_copy(
            out=colsums[0:1, kvo * 512:(kvo + 1) * 512], in_=bt[0:1, 0:512]), [mev])
        cev = kb.ev(st)
        fb.done(bi, cev)
        csk_evs.append(cev)
    colsum_kv_ev = csk_evs[-1]

    vones = []
    for i in range(3):
        ap = vpool[:, i, :].rearrange("p (h e) -> p h e", e=33)[:, :, 32:33]
        st = dve(lambda e, ap=ap: e.memset(ap, 1.0), [])
        vones.append(kb.ev(st))

    cc_test("cc3")
    fb_kv = Rot(kb, psF[0:2])
    kv_ready = [None] * NTT
    kv_free = [[] for _ in range(3)]
    vk_bank_busy = [[] for _ in range(4)]
    vk_evac_evs = []

    for b in range(B):
        for half in range(2):
            tt = 2 * b + half
            slot = tt % 3
            base_deps = [xfm_done, wkv_done, colsum_kv_ev, negmu0_rows_done,
                         xstat_r[tt], vones[slot]] + kv_free[slot]
            evac_evs = []
            for kh in range(2):
                bts = []
                dps = list(base_deps) if kh == 0 else []
                for kk in range(2):
                    bt, bd, bi = fb_kv.take()
                    bts.append((bt, bi))
                    dps += bd
                for ch in range(8):
                    for kk in range(2):
                        kvo = kh * 2 + kk
                        bt = bts[kk][0]
                        st = pe(lambda e, ch=ch, kvo=kvo, bt=bt, tt=tt: e.matmul(
                            bt[:, 0:512], xfmbf[:, ch, tt * 128:(tt + 1) * 128],
                            wkv_sb[:, ch, kvo * 512:(kvo + 1) * 512],
                            start=(ch == 0), stop=False),
                            dps if (ch == 0 and kk == 0) else [])
                for kk in range(2):
                    kvo = kh * 2 + kk
                    bt = bts[kk][0]
                    st = pe(lambda e, kvo=kvo, bt=bt, tt=tt: e.matmul(
                        bt[:, 0:512], rows_a[0:1, tt * 128:(tt + 1) * 128],
                        colsums[0:1, kvo * 512:(kvo + 1) * 512],
                        start=False, stop=True), [])
                    mmev = kb.ev(st)
                    if kvo < 2:
                        dst = kpool[:, slot, kvo * 512:(kvo + 1) * 512]
                    else:
                        vv = vpool[:, slot, :].rearrange("p (h e) -> p h e", e=33)
                        h0 = (kvo - 2) * 16
                        dst = vv[:, h0:h0 + 16, 0:32]
                    st = act(lambda e, dst=dst, bt=bt, tt=tt: e.activation(
                        out=dst, in_=bt[:, 0:512], func=AF.Relu,
                        scale=statw[:, tt, FAM_X:FAM_X + 1]), [mmev])
                    eev = kb.ev(st)
                    evac_evs.append(eev)
                    fb_kv.done(bts[kk][1], eev)
            kv_ready[tt] = evac_evs
            if tt == 0:
                tap("k0", kpool[:, 0, :], evac_evs)
                tap("v0", vpool[:, 0, :], evac_evs)
            pump(2)
        sa, sb_ = (2 * b) % 3, (2 * b + 1) % 3
        vdeps = kv_ready[2 * b] + kv_ready[2 * b + 1]
        for hg in range(8):
            for j in range(4):
                h = 4 * hg + j
                bt = psF[j + 2]
                dps = []
                if hg == 0 and j == 0:
                    dps += vdeps
                if hg == 0:
                    dps += vk_bank_busy[j]
                st = pe(lambda e, h=h, j=j, bt=bt, sa=sa, hg=hg: e.matmul(
                    bt[32 * j:32 * (j + 1), hg * 33:(hg + 1) * 33],
                    kpool[:, sa, h * 32:(h + 1) * 32],
                    vpool[:, sa, h * 33:(h + 1) * 33],
                    start=True, stop=False, tile_position=(0, 32 * j)), dps)
                st = pe(lambda e, h=h, j=j, bt=bt, sb_=sb_, hg=hg: e.matmul(
                    bt[32 * j:32 * (j + 1), hg * 33:(hg + 1) * 33],
                    kpool[:, sb_, h * 32:(h + 1) * 32],
                    vpool[:, sb_, h * 33:(h + 1) * 33],
                    start=False, stop=True, tile_position=(0, 32 * j)), [])
        pe_b_ev = kb.ev(st)
        kv_free[sa] = [pe_b_ev]
        kv_free[sb_] = [pe_b_ev]
        for j in range(4):
            bt = psF[j + 2]
            st = dve(lambda e, j=j, bt=bt, b=b: e.tensor_copy(
                out=vksb[32 * j:32 * (j + 1), b * 264:(b + 1) * 264],
                in_=bt[32 * j:32 * (j + 1), 0:264]), [pe_b_ev])
            eev = kb.ev(st)
            vk_bank_busy[j] = [eev]
            vk_evac_evs.append(eev)
        pump(2)

    cc_test("cc4")
    for j in range(4):
        fb.evs[2 + j] = vk_bank_busy[j]
    fb.evs[0] = list(fb_kv.evs[0])
    fb.evs[1] = list(fb_kv.evs[1])

    maybe_stop("p1")
    vkin_ev = kb.dma("pool", "fp", vk_in, vksb[:, :], deps=vk_evac_evs)
    maybe_stop("p1cc0")
    if STOP_AFTER == "cc5":
        st = kb.op("pool", lambda e: e.collective_compute(
            "AllReduce", ALU.add, replica_groups=[list(range(NCORE))],
            ins=[vk_in.opt()], outs=[vk_out.opt()]), deps=[vkin_ev], sem="cc", inc=1)
        kb.ev(st)
        raise _StopBuild()
    kb.fence(extra=[vkin_ev])
    p1only.close()
    if STOP_AFTER and STOP_AFTER.startswith("cct"):
        n = int(STOP_AFTER[3:])
        specs = [("vkbf", [128, VKW], BF16), ("qch", [128, 2, T], BF16),
                 ("wqs", [128, 2, 8, 128], BF16), ("dentmp", [128, 4, 140], F32),
                 ("pib", [128, 4, 128], BF16)]
        for nm, shp, dt_ in specs[:n]:
            nc.sbuf_tensor(nm, shp, dt_).__enter__()
        st = kb.op("pool", lambda e: e.collective_compute(
            "AllReduce", ALU.add, replica_groups=[list(range(NCORE))],
            ins=[vk_in.opt()], outs=[vk_out.opt()]), deps=[vkin_ev], sem="cc", inc=1)
        kb.ev(st)
        raise _StopBuild()
    if STOP_AFTER and STOP_AFTER.startswith("ccsz"):
        nbytes = int(STOP_AFTER[4:]) * 1024
        dummy = nc.sbuf_tensor("dummy_cc9", [128, nbytes], mybir.dt.uint8)
        dummy.__enter__()
        st = kb.op("pool", lambda e: e.collective_compute(
            "AllReduce", ALU.add, replica_groups=[list(range(NCORE))],
            ins=[vk_in.opt()], outs=[vk_out.opt()]), deps=[vkin_ev], sem="cc", inc=1)
        kb.ev(st)
        raise _StopBuild()
    if STOP_AFTER == "cc6":
        st = kb.op("pool", lambda e: e.collective_compute(
            "AllReduce", ALU.add, replica_groups=[list(range(NCORE))],
            ins=[vk_in.opt()], outs=[vk_out.opt()]), deps=[vkin_ev], sem="cc", inc=1)
        kb.ev(st)
        raise _StopBuild()

    p2 = ExitStack()
    vkbf = p2.enter_context(nc.sbuf_tensor("vkbf", [128, VKW], BF16))
    qch = p2.enter_context(nc.sbuf_tensor("qch", [128, 2, T], BF16))
    wqs = p2.enter_context(nc.sbuf_tensor("wqs", [128, 2, 8, 128], BF16))
    dentmp = p2.enter_context(nc.sbuf_tensor("dentmp", [128, 4, 140], F32))
    pib = p2.enter_context(nc.sbuf_tensor("pib", [128, 4, 128], BF16))
    den_rot = Rot(kb, [dentmp[:, i, :] for i in range(4)])
    pib_rot = Rot(kb, [pib[:, i, :] for i in range(4)])

    maybe_stop("p1f")
    st = kb.op("pool", lambda e: e.collective_compute(
        "AllReduce", ALU.add, replica_groups=[list(range(NCORE))],
        ins=[vk_in.opt()], outs=[vk_out.opt()]), deps=[vkin_ev], sem="cc", inc=1)
    cc_ev = kb.ev(st)
    maybe_stop("p1cc")

    rel_evs = []
    for p_ in range(3):
        w = 2048 if p_ < 2 else VKW - 4096
        sg, sd, si = st_rot.take()
        lev = kb.dma("sync", "fr", sg[:, 0:w], vk_out[:, p_ * 2048:p_ * 2048 + w],
                     deps=[cc_ev] + sd)
        st = dve(lambda e, sg=sg, p_=p_, w=w: e.tensor_copy(
            out=vkbf[:, p_ * 2048:p_ * 2048 + w], in_=sg[:, 0:w]), [lev])
        ev_ = kb.ev(st)
        st_rot.done(si, ev_)
        rel_evs.append(ev_)
    vk_ready_ev = rel_evs[-1]
    tap("vkbf", vkbf[:, :], [vk_ready_ev])

    qch_free = [[], []]
    projin_evs = {}
    for ci in range(8):
        sl = ci % 2
        sg, sd, si = st_rot.take()
        sg_ap = sg[:, 0:1024].rearrange("p (a b) -> p a b", b=128)
        lev = kb.dma("sync", "fw", sg_ap,
                     wq1_h[:, ci * 128:(ci + 1) * 128].rearrange("(a p) q -> p a q", p=128),
                     deps=sd)
        st = dve(lambda e, sl=sl, sg_ap=sg_ap: e.tensor_copy(out=wqs[:, sl, :, :], in_=sg_ap), [lev])
        wq_ev = kb.ev(st)
        st_rot.done(si, wq_ev)
        bt, bd, bi = fb.take()
        for ch in range(8):
            st = pe(lambda e, ch=ch, bt=bt, sl=sl: e.matmul(
                bt[0:1, 0:128], onesc[:, :], wqs[:, sl, ch, :],
                start=(ch == 0), stop=(ch == 7)), ([wq_ev] + bd) if ch == 0 else [])
        mev = kb.ev(st)
        st = dve(lambda e, ci=ci, bt=bt: e.tensor_copy(
            out=colsums[0:1, 2048 + ci * 128:2048 + (ci + 1) * 128], in_=bt[0:1, 0:128]), [mev])
        csq_ev = kb.ev(st)
        fb.done(bi, csq_ev)
        qhat_evs = []
        for ts in range(NTS):
            bt, bd, bi = fb.take()
            for ch in range(8):
                dps = []
                if ch == 0:
                    dps = [csq_ev, wq_ev, negmu0_rows_done] + bd
                    if ts == 0:
                        dps += qch_free[sl]
                st = pe(lambda e, ch=ch, ts=ts, bt=bt, sl=sl: e.matmul(
                    bt[:, 0:512], wqs[:, sl, ch, :], xfmbf[:, ch, ts * 512:(ts + 1) * 512],
                    start=(ch == 0), stop=False), dps)
            st = pe(lambda e, ci=ci, ts=ts, bt=bt: e.matmul(
                bt[:, 0:512], colsums[0:1, 2048 + ci * 128:2048 + (ci + 1) * 128],
                rows_a[0:1, ts * 512:(ts + 1) * 512], start=False, stop=True), [])
            mmev = kb.ev(st)
            st = act(lambda e, sl=sl, ts=ts, bt=bt: e.activation(
                out=qch[:, sl, ts * 512:(ts + 1) * 512], in_=bt[:, 0:512], func=AF.Relu), [mmev])
            eev = kb.ev(st)
            qhat_evs.append(eev)
            fb.done(bi, eev)
        qhat_ev = qhat_evs[-1]
        if ci == 0:
            tap("qh0", qch[:, 0, :], [qhat_ev])
        last_vkq = None
        for tt in range(NTT):
            b = tt // 2
            jb = []
            for j in range(4):
                bt, bd, bi = fb.take()
                st = pe(lambda e, j=j, tt=tt, b=b, ci=ci, sl=sl, bt=bt: e.matmul(
                    bt[:, 0:33],
                    qch[32 * j:32 * (j + 1), sl, tt * 128:(tt + 1) * 128],
                    vkbf[32 * j:32 * (j + 1), (b * 8 + ci) * 33:(b * 8 + ci + 1) * 33],
                    start=True, stop=True, tile_position=(32 * j, 0)),
                    [qhat_ev, vk_ready_ev] + bd)
                jb.append((bt, kb.ev(st), bi))
                last_vkq = st
            dap, dd, di = den_rot.take()
            con_evs = []
            for j in range(4):
                btj, mmev, bij = jb[j]
                st = act(lambda e, j=j, btj=btj, dap=dap: e.copy(
                    out=dap[:, j * 33:(j + 1) * 33], in_=btj[:, 0:33]),
                    [mmev] + (dd if j == 0 else []))
                cev = kb.ev(st)
                con_evs.append(cev)
                fb.done(bij, cev)
            dv = dap[:, 0:132].rearrange("p (j e) -> p j e", e=33)
            st = dve(lambda e, dap=dap, dv=dv: e.tensor_scalar_add(
                dap[:, 132:136].rearrange("p (j o) -> p j o", o=1), dv[:, :, 32:33], EPS_ATTN),
                con_evs)
            st = dve(lambda e, dap=dap: e.reciprocal(
                out=dap[:, 136:140].rearrange("p (j o) -> p j o", o=1),
                in_=dap[:, 132:136].rearrange("p (j o) -> p j o", o=1)), [kb.ev(st)])
            rec_ev = kb.ev(st)
            rec_bc = bc_last(dap[:, 136:140], 32)
            pa, pad, pai = pib_rot.take()
            dst = pa.rearrange("p (j d) -> p j d", d=32)
            st = pool(lambda e, dv=dv, rec_bc=rec_bc, dst=dst: e.tensor_tensor(
                out=dst, in0=dv[:, :, 0:32], in1=rec_bc, op=ALU.mult), [rec_ev] + pad)
            dev_ = kb.ev(st)
            den_rot.done(di, dev_)
            spv = kb.dma("pool", "fp", projin_sp[tt, :, ci * 128:(ci + 1) * 128], pa,
                         deps=[dev_])
            pib_rot.done(pai, spv)
            projin_evs[(ci, tt)] = spv
        qch_free[sl] = [kb.ev(last_vkq)]
        pump(4)

    maybe_stop("p2")
    if "projin0" in dbg:
        kb.dma("pool", "fo", dbg["projin0"], projin_sp[0, :, :], deps=[projin_evs[(7, 0)]])
    kb.fence()
    p2.close()
    p12.close()

    p3 = ExitStack()
    wp1_sb = p3.enter_context(nc.sbuf_tensor("wp1sb", [128, 8, C], BF16))
    pit = p3.enter_context(nc.sbuf_tensor("pit", [128, 4, C], BF16))
    stage3 = p3.enter_context(nc.sbuf_tensor("stage3", [128, 4, 1024], F32))
    pif = p3.enter_context(nc.sbuf_tensor("pif", [128, 2, 8, 128], BF16))
    t1buf = p3.enter_context(nc.sbuf_tensor("t1buf", [128, 2, C], F32))
    t1bf = p3.enter_context(nc.sbuf_tensor("t1bf", [128, 2, C], BF16))
    t1fm = p3.enter_context(nc.sbuf_tensor("t1fm", [128, 2, 8, 128], BF16))
    x1tb = p3.enter_context(nc.sbuf_tensor("x1tb", [128, 3, C], F32))
    x1fb = p3.enter_context(nc.sbuf_tensor("x1fb", [128, 3, 8, 128], F32))
    xsl3 = p3.enter_context(nc.sbuf_tensor("xsl3", [128, 4, C], F32))

    st3_rot = Rot(kb, [stage3[:, i, :] for i in range(4)])
    pif_rot = Rot(kb, [pif[:, 0], pif[:, 1]])
    t1_rot = Rot(kb, [t1buf[:, 0, :], t1buf[:, 1, :]])
    t1bf_rot = Rot(kb, [t1bf[:, 0, :], t1bf[:, 1, :]])
    t1fm_rot = Rot(kb, [t1fm[:, 0], t1fm[:, 1]])
    x1t_rot = Rot(kb, [x1tb[:, i, :] for i in range(3)])
    x1f_rot = Rot(kb, [x1fb[:, i] for i in range(3)])
    xsl3_rot = Rot(kb, [xsl3[:, i, :] for i in range(4)])
    pit_rot = Rot(kb, [pit[:, i, :] for i in range(4)])

    wp1_evs = []
    for ch in range(8):
        sg, sd, si = st3_rot.take()
        lev = kb.dma("sync", "fw", sg, wp1_h[ch * 128:(ch + 1) * 128, :], deps=sd)
        st = dve(lambda e, sg=sg, ch=ch: e.tensor_copy(out=wp1_sb[:, ch, :], in_=sg), [lev])
        ev_ = kb.ev(st)
        st3_rot.done(si, ev_)
        wp1_evs.append(ev_)
    wp1_done = wp1_evs[-1]
    sgb, sdb, sib = st3_rot.take()
    lev = kb.dma("sync", "fw", sgb[0:1, 0:1024], bp1_h[:, :], deps=sdb)
    st = dve(lambda e, sgb=sgb: e.tensor_copy(out=colsums[0:1, 3072:4096], in_=sgb[0:1, 0:1024]),
             [lev])
    bias_ev = kb.ev(st)
    st3_rot.done(sib, bias_ev)

    def p3_stageA(tt):
        pta, ptd_, pti_ = pit_rot.take()
        plev = kb.dma("sync", "fr", pta, projin_sp[tt, :, :], deps=ptd_)
        pf, pfd, pfi = pif_rot.take()
        bt, bd, bi = bb.take()
        for ch in range(8):
            st = pe(lambda e, ch=ch, bt=bt, pta=pta: e.transpose(
                bt[:, ch * 128:(ch + 1) * 128], pta[:, ch * 128:(ch + 1) * 128],
                iden_sb[:, :]), ([plev] + bd + pfd) if ch == 0 else [])
        tev = kb.ev(st)
        pit_rot.done(pti_, tev)
        st = dve(lambda e, bt=bt, pf=pf: e.tensor_copy(
            out=pf.rearrange("p a b -> p (a b)"), in_=bt[:, 0:1024]), [tev])
        pif_ev = kb.ev(st)
        bb.done(bi, pif_ev)
        ybanks = []
        for oh in range(2):
            btY, bdY, biY = fb.take()
            for ch in range(8):
                st = pe(lambda e, ch=ch, oh=oh, btY=btY, pf=pf: e.matmul(
                    btY[:, 0:512], pf[:, ch, :], wp1_sb[:, ch, oh * 512:(oh + 1) * 512],
                    start=(ch == 0), stop=False),
                    ([pif_ev, wp1_done, bias_ev] + bdY) if ch == 0 else [])
            st = pe(lambda e, oh=oh, btY=btY: e.matmul(
                btY[:, 0:512], ones4[0:1, 0:128],
                colsums[0:1, 3072 + oh * 512:3072 + (oh + 1) * 512],
                start=False, stop=True), [])
            ybanks.append([btY, kb.ev(st), biY])
        pif_rot.done(pfi, [ybanks[0][1], ybanks[1][1]])
        return dict(tt=tt, ybanks=ybanks)

    def p3_stageB(sa):
        tt = sa["tt"]
        ybanks = sa["ybanks"]
        yevs = ln_stats([(ybanks[0][0][:, 0:512], 512), (ybanks[1][0][:, 0:512], 512)],
                        tt, FAM_Y, [ybanks[0][1], ybanks[1][1]])
        t1a, t1d, t1i = t1_rot.take()
        yb_ = statw[:, tt, FAM_Y + 2:FAM_Y + 3]
        st = pool(lambda e, yb_=yb_, tt=tt: e.tensor_mul(
            yb_, statw[:, tt, FAM_Y:FAM_Y + 1], statw[:, tt, FAM_Y + 1:FAM_Y + 2]),
            [yevs["r"], yevs["negmu"]])
        ybev = kb.ev(st)
        t1evs = []
        for oh in range(2):
            btY = ybanks[oh][0]
            st = act(lambda e, btY=btY, oh=oh, t1a=t1a, tt=tt, yb_=yb_: e.activation(
                out=t1a[:, oh * 512:(oh + 1) * 512], in_=btY[:, 0:512],
                func=AF.Identity, bias=yb_, scale=statw[:, tt, FAM_Y:FAM_Y + 1]),
                [ybev, ybanks[oh][1]] + (t1d if oh == 0 else []))
            ev_ = kb.ev(st)
            t1evs.append(ev_)
            fb.done(ybanks[oh][2], ev_)
        xs, xsd, xsi = xsl3_rot.take()
        xlev = kb.dma("sync", "fx", xs, x_tm_h[tt * 128:(tt + 1) * 128, :], deps=xsd)
        x1a, x1d, x1i = x1t_rot.take()
        st = dve(lambda e, x1a=x1a, xs=xs, t1a=t1a: e.tensor_add(x1a, xs, t1a),
                 t1evs + [xlev] + x1d)
        x1ev = kb.ev(st)
        xsl3_rot.done(xsi, x1ev)
        spev = kb.dma("pool", "fp", x1tm_sp[tt, :, :], x1a, deps=[x1ev])
        if tt == 0:
            tap("y0", t1a, t1evs)
            tap("x1t0", x1a, [x1ev])
        x1evs = ln_stats([(x1a[:, 0:512], 512), (x1a[:, 512:1024], 512)],
                         tt, FAM_X1, [x1ev])
        x1t_rot.done(x1i, [spev, x1evs["r"]])
        s0 = statw[:, tt, FAM_X1 + 0:FAM_X1 + 1]
        s3 = statw[:, tt, FAM_X1 + 3:FAM_X1 + 4]
        s4 = statw[:, tt, FAM_X1 + 4:FAM_X1 + 5]
        s6 = statw[:, tt, FAM_X1 + 6:FAM_X1 + 7]
        s8 = statw[:, tt, FAM_X1 + 8:FAM_X1 + 9]
        va_ = x1evs["var_ap"]
        st = pool(lambda e, s3=s3, va_=va_, s0=s0: e.tensor_mul(s3, va_, s0),
                  [x1evs["var"], x1evs["r"]])
        st = pool(lambda e, s3=s3, s0=s0: e.tensor_mul(s3, s3, s0), [kb.ev(st)])
        v1ev = kb.ev(st)
        st = act(lambda e, s4=s4, s3=s3: e.activation(out=s4, in_=s3, func=AF.Ln,
                                                      bias=cknst[:, 0:1]), [v1ev])
        st = act(lambda e, s6=s6, s4=s4: e.activation(out=s6, in_=s4, func=AF.Exp,
                                                      scale=cknst[:, 1:2]), [kb.ev(st)])
        r1ev = kb.ev(st)
        st = pool(lambda e, s8=s8, s6=s6, s0=s0: e.tensor_mul(s8, s6, s0), [r1ev])
        aev = kb.ev(st)
        make_row(statw[:, tt, FAM_X1 + 1:FAM_X1 + 2], rows_a, 32, tt * 128,
                 [x1evs["negmu"]])
        make_row(s8, rows_a, 64, tt * 128, [aev])
        make_row(s0, rows_b, 0, tt * 128, [x1evs["r"]])
        tb, tbd, tbi = t1bf_rot.take()
        st = act(lambda e, tb=tb, t1a=t1a: e.copy(out=tb, in_=t1a), t1evs + tbd)
        tbev = kb.ev(st)
        t1_rot.done(t1i, [x1ev, tbev])
        tf, tfd, tfi = t1fm_rot.take()
        bt2, bd2, bi2 = bb.take()
        for ch in range(8):
            st = pe(lambda e, ch=ch, bt2=bt2, tb=tb: e.transpose(
                bt2[:, ch * 128:(ch + 1) * 128], tb[:, ch * 128:(ch + 1) * 128], iden_sb[:, :]),
                ([tbev] + bd2) if ch == 0 else [])
        tev2 = kb.ev(st)
        t1bf_rot.done(tbi, tev2)
        st = dve(lambda e, bt2=bt2, tf=tf: e.tensor_copy(
            out=tf.rearrange("p a b -> p (a b)"), in_=bt2[:, 0:1024]), [tev2] + tfd)
        tfev = kb.ev(st)
        bb.done(bi2, tfev)
        x1f, x1fd, x1fi = x1f_rot.take()
        sgx, sgd, sgi = st3_rot.take()
        sgx_v = sgx.rearrange("p (a b) -> p a b", b=128)
        xcl = kb.dma("sync", "fx", sgx_v,
                     x_fm_h[:, tt * 128:(tt + 1) * 128].rearrange("(a p) q -> p a q", p=128),
                     deps=sgd)
        st = dve(lambda e, x1f=x1f, sgx_v=sgx_v, tf=tf: e.tensor_add(
            x1f.rearrange("p a b -> p (a b)"), sgx_v.rearrange("p a b -> p (a b)"),
            tf.rearrange("p a b -> p (a b)")), [xcl, tfev] + x1fd)
        x1fev = kb.ev(st)
        st3_rot.done(sgi, x1fev)
        t1fm_rot.done(tfi, x1fev)
        dst = x1fm_sp[:, :, tt * 128:(tt + 1) * 128].rearrange("a p t -> p a t")
        spev2 = kb.dma("pool", "fp", dst, x1f[:, :, :], deps=[x1fev])
        x1f_rot.done(x1fi, spev2)
        pump(4)

    p3_pend = None
    for tt in range(P3_TILES):
        sa = p3_stageA(tt)
        if p3_pend is not None:
            p3_stageB(p3_pend)
        p3_pend = sa
    p3_stageB(p3_pend)

    x1_rows_done = kb.tail("dve")
    if "statw" in dbg:
        kb.dma("pool", "fo", dbg["statw"], statw[:, :, :].rearrange("p a b -> p (a b)"),
               deps=[x1_rows_done])
    if "rows_a" in dbg:
        kb.dma("pool", "fo", dbg["rows_a"], rows_a[:, :], deps=[x1_rows_done])
    if "rows_b" in dbg:
        kb.dma("pool", "fo", dbg["rows_b"], rows_b[:, :], deps=[x1_rows_done])
    maybe_stop("p3")
    kb.fence()
    p3.close()

    p4a = ExitStack()
    qin = p4a.enter_context(nc.sbuf_tensor("qin", [128, 8, T], BF16))
    wq2_sb = p4a.enter_context(nc.sbuf_tensor("wq2sb", [128, 8, 1536], BF16))
    stage4 = p4a.enter_context(nc.sbuf_tensor("stage4", [128, 2, 1536], F32))
    u4 = p4a.enter_context(nc.sbuf_tensor("u4", [128, 2, 512], F32))
    v4 = p4a.enter_context(nc.sbuf_tensor("v4", [128, 2, 512], F32))
    q2stg = p4a.enter_context(nc.sbuf_tensor("q2stg", [128, 2, T], BF16))
    st4_rot = Rot(kb, [stage4[:, 0, :], stage4[:, 1, :]])
    u4_rot = Rot(kb, [u4[:, 0, :], u4[:, 1, :]])
    v4_rot = Rot(kb, [v4[:, 0, :], v4[:, 1, :]])
    cb_rot = Rot(kb, [q2stg[:, 0, :], q2stg[:, 1, :]])

    wq2_evs = []
    for ch in range(8):
        sg, sd, si = st4_rot.take()
        lev = kb.dma("sync", "fw", sg, wq2_h[ch * 128:(ch + 1) * 128, :], deps=sd)
        st = dve(lambda e, sg=sg, ch=ch: e.tensor_copy(out=wq2_sb[:, ch, :], in_=sg), [lev])
        ev_ = kb.ev(st)
        st4_rot.done(si, ev_)
        wq2_evs.append(ev_)
    wq2_done = wq2_evs[-1]
    if P4_SUB == "w":
        maybe_stop("p4a")

    qin_evs = []
    for ts in range(NTS):
        bmu, bmu_ev, bmu_i = bcast(rows_a, 32, ts * 512, 512, [x1_rows_done])
        ba, ba_ev, ba_i = bcast(rows_a, 64, ts * 512, 512, [x1_rows_done])
        last = None
        for ch in range(8):
            sg, sd, si = st4_rot.take()
            lev = kb.dma("sync", "fr", sg[:, 0:512], x1fm_sp[ch, :, ts * 512:(ts + 1) * 512],
                         deps=sd)
            ua, ud, ui = u4_rot.take()
            st = dve(lambda e, ua=ua, sg=sg, bmu=bmu: e.tensor_add(ua, sg[:, 0:512],
                                                                   bmu[:, 0:512]),
                     [lev, bmu_ev] + ud)
            uev = kb.ev(st)
            st4_rot.done(si, uev)
            va, vd, vi = v4_rot.take()
            st = dve(lambda e, va=va, ua=ua, ch=ch, ba=ba: e.scalar_tensor_tensor(
                out=va, in0=ua, scalar=lncol[:, ch, 0:1], in1=ba[:, 0:512],
                op0=ALU.mult, op1=ALU.mult), [uev, ba_ev] + vd)
            vev = kb.ev(st)
            u4_rot.done(ui, vev)
            st = dve(lambda e, va=va, ch=ch, ts=ts: e.tensor_scalar_add(
                qin[:, ch, ts * 512:(ts + 1) * 512], va, lncol[:, ch, 1:2]), [vev])
            qev = kb.ev(st)
            v4_rot.done(vi, qev)
            qin_evs.append(qev)
            last = qev
        fb.done(bmu_i, last)
        fb.done(ba_i, last)
        pump(1)
    qin_done = qin_evs[-1]
    tap("qin0", qin[:, 0, :], [qin_done])
    if P4_SUB == "qin":
        maybe_stop("p4a")

    def qkv2_block(och, dram, didx):
        cb, cbd, cbi = cb_rot.take()
        cb_v = cb.rearrange("p (n b) -> p b n", b=16)
        evac_evs = []
        for tsp in range(4):
            bts = []
            dps = [qin_done, wq2_done] + (cbd if tsp == 0 else [])
            for kk in range(2):
                bt, bd, bi = fb.take()
                bts.append([bt, bi])
                dps += bd
            for ch in range(8):
                for kk in range(2):
                    ts = tsp * 2 + kk
                    bt = bts[kk][0]
                    st = pe(lambda e, ch=ch, ts=ts, bt=bt, och=och: e.matmul(
                        bt[:, 0:512], wq2_sb[:, ch, och * 128:(och + 1) * 128],
                        qin[:, ch, ts * 512:(ts + 1) * 512],
                        start=(ch == 0), stop=(ch == 7)),
                        dps if (ch == 0 and kk == 0) else [])
            tailev = kb.ev(st)
            for kk in range(2):
                ts = tsp * 2 + kk
                bt, bi = bts[kk]
                dst = cb_v[:, 2 * ts:2 * ts + 2, :]
                srcv = bt[:, 0:512].rearrange("p (b n) -> p b n", n=256)
                st = act(lambda e, dst=dst, srcv=srcv: e.copy(out=dst, in_=srcv), [tailev])
                eev = kb.ev(st)
                fb.done(bi, eev)
                evac_evs.append(eev)
        spv = kb.dma("pool", "fp", dram[didx, :, :], cb, deps=evac_evs)
        cb_rot.done(cbi, spv)
        return [spv]

    k2_evs = qkv2_block(8, k2_sp, 0) + qkv2_block(9, k2_sp, 1)
    if "cb0" in dbg:
        kb.dma("pool", "fo", dbg["cb0"], q2stg[:, 0, :], deps=k2_evs)
    if P4_SUB == "kv":
        maybe_stop("p4a")
    v2_evs = qkv2_block(10, v2_sp, 0) + qkv2_block(11, v2_sp, 1)
    if P4_SUB == "v":
        maybe_stop("p4a")
    q2sp_evs = []
    for och in range(8):
        q2sp_evs += qkv2_block(och, q2_sp, och)
        pump(2)
        if P4_SUB == "q%d" % och:
            maybe_stop("p4a")
    assert len(q2sp_evs) == 8
    maybe_stop("p4a")

    kb.fence()
    p4a.close()

    p45 = ExitStack()
    att_fm = p45.enter_context(nc.sbuf_tensor("attfm", [128, 8, T], BF16))
    p4b = ExitStack()
    qc2 = p4b.enter_context(nc.sbuf_tensor("qc2", [128, 2, T], BF16))
    v2w = p4b.enter_context(nc.sbuf_tensor("v2w", [128, T], BF16))
    k2dup = p4b.enter_context(nc.sbuf_tensor("k2dup", [128, T], BF16))
    v2t = p4b.enter_context(nc.sbuf_tensor("v2t", [128, KVH, 2048], BF16))
    scb = p4b.enter_context(nc.sbuf_tensor("scb", [128, 4, 512], F32))
    prb = p4b.enter_context(nc.sbuf_tensor("prb", [128, 4, 512], BF16))
    ptb = p4b.enter_context(nc.sbuf_tensor("ptb", [128, 3, 512], BF16))
    a64b = p4b.enter_context(nc.sbuf_tensor("a64b", [128, 3, 256], BF16))
    sxb = p4b.enter_context(nc.sbuf_tensor("sxb", [128, 4, 12], F32))
    mask_sb = p4b.enter_context(nc.sbuf_tensor("masksb", [128, 512], F32))
    mask_ld = kb.dma("sync", "fc", mask_sb[:, :], mask_h[:, :])
    sc_rot = Rot(kb, [scb[:, i, :] for i in range(4)])
    pr_rot = Rot(kb, [prb[:, i, :] for i in range(4)])
    pt_rot = Rot(kb, [ptb[:, i, :] for i in range(3)])
    a6_rot = Rot(kb, [a64b[:, i, :] for i in range(3)])
    sx_rot = Rot(kb, [sxb[:, i, :] for i in range(4)])

    v2t_evs = []
    v2w_last = []
    for half in range(2):
        wlev = kb.dma("sync", "fr", v2w[:, :], v2_sp[half, :, :],
                      deps=list(v2_evs) + v2w_last)
        if half == 0 and "v2w0" in dbg:
            kb.dma("pool", "fo", dbg["v2w0"], v2w[:, :], deps=[wlev])
        for kvh in (2 * half, 2 * half + 1):
            base = (kvh % 2) * 64
            for g in range(NTT):
                bt, bd, bi = bb.take()
                src = v2w[base:base + 64, g * 128:(g + 1) * 128]
                idn = iden_sb[base:base + 64, base:base + 64]
                st = pe(lambda e, src=src, bt=bt, idn=idn: e.transpose(
                    bt[0:128, 0:64], src, idn), [wlev] + bd)
                tev = kb.ev(st)
                st = dve(lambda e, bt=bt, kvh=kvh, g=g: e.tensor_copy(
                    out=v2t[:, kvh, g * 64:(g + 1) * 64], in_=bt[0:128, 0:64]), [tev])
                eev = kb.ev(st)
                bb.done(bi, eev)
                v2t_evs.append(eev)
            pump(2)
        v2w_last = [kb.tail("pe")]
    v2t_done = v2t_evs[-1]
    if "v2t" in dbg:
        kb.dma("pool", "fo", dbg["v2t"], v2t[:, 0, :], deps=[v2t_done])
    if P4B_SUB == "v2t":
        maybe_stop("p4b")

    rotA = Rot(kb, [psB[0][:, 0:512], psB[0][:, 512:1024]])
    rotB = Rot(kb, [psB[1][:, 0:512], psB[1][:, 512:1024]])
    rotA.evs = [list(bb.evs[0]), list(bb.evs[0])]
    rotB.evs = [list(bb.evs[1]), list(bb.evs[1])]

    cur = {"kvh": -1, "k2dup_ready": None, "last_qk": None, "qlev": {}}
    qc2_free = [[], []]
    att_evs = []
    stripes = [(och, hh, gq) for och in range(8) for hh in range(2)
               for gq in range(8)]

    def p4b_stageA(idx):
        och, hh, gq = stripes[idx]
        sl = och % 2
        base = hh * 64
        kvh = och // 2
        if hh == 0 and gq == 0:
            cur["qlev"][och] = kb.dma("sync", "fr", qc2[:, sl, :], q2_sp[och, :, :],
                                      deps=[q2sp_evs[och]] + qc2_free[sl])
            if kvh != cur["kvh"]:
                cur["kvh"] = kvh
                src_half = k2_sp[kvh // 2, (kvh % 2) * 64:(kvh % 2) * 64 + 64, :]
                dd = list(k2_evs)
                if cur["last_qk"] is not None:
                    dd.append(cur["last_qk"])
                e1 = kb.dma("sync", "fr", k2dup[0:64, :], src_half, deps=dd)
                e2 = kb.dma("sync", "fr", k2dup[64:128, :], src_half, deps=[e1])
                cur["k2dup_ready"] = e2
        sc, scd, sci = sc_rot.take()
        sx, sxd, sxi = sx_rot.take()
        btq, bqd, bqi = fb.take()
        tp = (base, 0) if base else None
        deps0 = [cur["qlev"][och], cur["k2dup_ready"], v2t_done] + bqd
        for j in range(4):
            g = gq * 4 + j
            st = pe(lambda e, j=j, g=g, btq=btq, sl=sl, base=base, tp=tp: e.matmul(
                btq[:, j * 128:(j + 1) * 128],
                qc2[base:base + 64, sl, g * 128:(g + 1) * 128],
                k2dup[base:base + 64, g * 128:(g + 1) * 128],
                start=True, stop=True, tile_position=tp), deps0 if j == 0 else [])
        qkev = kb.ev(st)
        cur["last_qk"] = qkev
        if hh == 1 and gq == 7:
            qc2_free[sl] = [qkev]
        st = dve(lambda e, sc=sc, btq=btq: e.scalar_tensor_tensor(
            out=sc, in0=btq[:, 0:512], scalar=0.125, in1=mask_sb[:, :],
            op0=ALU.mult, op1=ALU.add), [qkev, mask_ld] + scd)
        mev = kb.ev(st)
        fb.done(bqi, mev)
        mxevs = []
        for j in range(4):
            st = dve(lambda e, sc=sc, sx=sx, j=j: e.tensor_reduce(
                out=sx[:, j:j + 1], in_=sc[:, j * 128:(j + 1) * 128], axis=AXL.X,
                op=ALU.max, negate=True), ([mev] + sxd) if j == 0 else [])
            mxevs.append(kb.ev(st))
        pr, prd, pri = pr_rot.take()
        exevs = []
        for j in range(4):
            st = act(lambda e, pr=pr, sc=sc, sx=sx, j=j: e.activation(
                out=pr[:, j * 128:(j + 1) * 128], in_=sc[:, j * 128:(j + 1) * 128],
                func=AF.Exp, bias=sx[:, j:j + 1], accum_out=sx[:, 4 + j:5 + j]),
                [mxevs[j]] + (prd if j == 0 else []))
            exevs.append(kb.ev(st))
        sc_rot.done(sci, exevs[-1])
        st = dve(lambda e, sx=sx: e.reciprocal(out=sx[:, 8:12], in_=sx[:, 4:8]),
                 [exevs[-1]])
        rcev = kb.ev(st)
        return dict(och=och, gq=gq, base=base, kvh=kvh, pr=pr, pri=pri,
                    sx=sx, sxi=sxi, exev=exevs[-1], rcev=rcev)

    def p4b_stageB(s):
        base, kvh, gq, och = s["base"], s["kvh"], s["gq"], s["och"]
        btA, adp, ai = rotA.take()
        for j in range(4):
            st = pe(lambda e, btA=btA, pr=s["pr"], j=j: e.transpose(
                btA[:, j * 128:(j + 1) * 128], pr[:, j * 128:(j + 1) * 128],
                iden_sb[:, :]), ([s["exev"]] + adp) if j == 0 else [])
        tprev = kb.ev(st)
        pr_rot.done(s["pri"], tprev)
        pt, ptd, pti = pt_rot.take()
        st = dve(lambda e, pt=pt, btA=btA: e.tensor_copy(out=pt, in_=btA[:, 0:512]),
                 [tprev] + ptd)
        ptcev = kb.ev(st)
        rotA.done(ai, ptcev)
        btv, bvd, bvi = fb.take()
        for j in range(4):
            g = gq * 4 + j
            st = pe(lambda e, btv=btv, pt=pt, kvh=kvh, g=g, j=j: e.matmul(
                btv[:, j * 64:(j + 1) * 64], pt[:, j * 128:(j + 1) * 128],
                v2t[:, kvh, g * 64:(g + 1) * 64], start=True, stop=True),
                ([ptcev] + bvd) if j == 0 else [])
        avev = kb.ev(st)
        pt_rot.done(pti, avev)
        a6, a6d, a6i = a6_rot.take()
        scevs = []
        for j in range(4):
            st = act(lambda e, a6=a6, btv=btv, sx=s["sx"], j=j: e.activation(
                out=a6[:, j * 64:(j + 1) * 64], in_=btv[:, j * 64:(j + 1) * 64],
                func=AF.Copy, scale=sx[:, 8 + j:9 + j]),
                ([avev, s["rcev"]] + a6d) if j == 0 else [])
            scevs.append(kb.ev(st))
        fb.done(bvi, scevs[-1])
        sx_rot.done(s["sxi"], scevs[-1])
        btB, bbd, bi2 = rotB.take()
        tp2 = (0, base) if base else None
        for j in range(4):
            st = pe(lambda e, btB=btB, a6=a6, j=j, base=base, tp2=tp2: e.transpose(
                btB[base:base + 64, j * 128:(j + 1) * 128], a6[:, j * 64:(j + 1) * 64],
                iden_sb[:, :], tile_position=tp2), ([scevs[-1]] + bbd) if j == 0 else [])
        t2ev = kb.ev(st)
        a6_rot.done(a6i, t2ev)
        dst = att_fm[:, och, :].rearrange("p (b n) -> p b n", n=256)[:, :, gq * 32:(gq + 1) * 32]
        dst = dst.rearrange("p b n -> p n b")[base:base + 64]
        src = btB[base:base + 64, 0:512].rearrange("p (n b) -> p n b", b=16)
        st = dve(lambda e, dst=dst, src=src: e.tensor_copy(out=dst, in_=src), [t2ev])
        fev = kb.ev(st)
        rotB.done(bi2, fev)
        att_evs.append(fev)

    pend = []
    for idx in range(len(stripes)):
        pend.append(p4b_stageA(idx))
        if len(pend) > 2:
            p4b_stageB(pend.pop(0))
        if idx % 4 == 0:
            pump(1)
    for s_ in pend:
        p4b_stageB(s_)
    att_done = att_evs[-1]
    tap("attfm0", att_fm[:, 0, :], [att_done])
    maybe_stop("p4b")
    kb.fence()
    p4b.close()

    p5 = ExitStack()
    wp2_sb = p5.enter_context(nc.sbuf_tensor("wp2sb", [128, 8, C], BF16))
    stage5 = p5.enter_context(nc.sbuf_tensor("stage5", [128, 3, 1024], F32))
    u5 = p5.enter_context(nc.sbuf_tensor("u5", [128, 2, 512], F32))
    v5 = p5.enter_context(nc.sbuf_tensor("v5", [128, 2, 512], F32))
    h1f = p5.enter_context(nc.sbuf_tensor("h1f", [128, 8, 512], F32))
    h1bfb = p5.enter_context(nc.sbuf_tensor("h1bfb", [128, 2, 1024], BF16))
    h1tmb = p5.enter_context(nc.sbuf_tensor("h1tmb", [128, 2, 1024], BF16))
    xmstg = p5.enter_context(nc.sbuf_tensor("xmstg", [128, 2, 512], BF16))
    st5_rot = Rot(kb, [stage5[:, i, :] for i in range(3)])
    u5_rot = Rot(kb, [u5[:, 0, :], u5[:, 1, :]])
    v5_rot = Rot(kb, [v5[:, 0, :], v5[:, 1, :]])
    h1bf_rot = Rot(kb, [h1bfb[:, 0, :], h1bfb[:, 1, :]])
    h1tm_rot = Rot(kb, [h1tmb[:, 0, :], h1tmb[:, 1, :]])
    xm_rot = Rot(kb, [xmstg[:, 0, :], xmstg[:, 1, :]])

    wp2_evs = []
    for ch in range(8):
        sg, sd, si = st5_rot.take()
        lev = kb.dma("sync", "fw", sg, wp2_h[ch * 128:(ch + 1) * 128, :], deps=sd)
        st = dve(lambda e, sg=sg, ch=ch: e.tensor_copy(out=wp2_sb[:, ch, :], in_=sg), [lev])
        ev_ = kb.ev(st)
        st5_rot.done(si, ev_)
        wp2_evs.append(ev_)
    wp2_done = wp2_evs[-1]

    prev_bc2 = []
    xm_sp_evs = {}
    h1_free = [[] for _ in range(8)]
    fb5 = Rot(kb, psF[0:4])
    bcA, bcB = psF[4], psF[5]
    bc_prev = [list(fb.evs[4]), list(fb.evs[5])]

    def bcast_at(bank, row_tensor, row_base, col0, n, deps):
        tp = (row_base, 0) if row_base else None
        st = pe(lambda e: e.matmul(bank[:, 0:n], ones4[row_base:row_base + 1, 0:128],
                                   row_tensor[row_base:row_base + 1, col0:col0 + n],
                                   start=True, stop=True, tile_position=tp), list(deps))
        return kb.ev(st)

    for ts in range(NTS):
        bmu_ev = bcast_at(bcA, rows_a, 32, ts * 512, 512, [x1_rows_done] + bc_prev[0])
        br0_ev = bcast_at(bcB, rows_b, 0, ts * 512, 512, [x1_rows_done] + bc_prev[1])
        bmu, br0 = bcA, bcB
        h1f_evs = []
        last_h1 = None
        for och in range(8):
            btp, bdp, bip = fb5.take()
            for ch in range(8):
                st = pe(lambda e, ch=ch, och=och, ts=ts, btp=btp: e.matmul(
                    btp[:, 0:512], wp2_sb[:, ch, och * 128:(och + 1) * 128],
                    att_fm[:, ch, ts * 512:(ts + 1) * 512],
                    start=(ch == 0), stop=(ch == 7)),
                    ([wp2_done, att_done] + bdp) if ch == 0 else [])
            p2ev = kb.ev(st)
            sg, sd, si = st5_rot.take()
            lev = kb.dma("sync", "fr", sg[:, 0:512], x1fm_sp[och, :, ts * 512:(ts + 1) * 512],
                         deps=sd)
            ua, ud, ui = u5_rot.take()
            st = dve(lambda e, ua=ua, sg=sg, bmu=bmu: e.tensor_add(ua, sg[:, 0:512],
                                                                   bmu[:, 0:512]),
                     [lev, bmu_ev] + ud)
            uev = kb.ev(st)
            st5_rot.done(si, uev)
            va, vd, vi = v5_rot.take()
            st = dve(lambda e, va=va, ua=ua, br0=br0: e.tensor_mul(va, ua, br0[:, 0:512]),
                     [uev, br0_ev] + vd)
            wev = kb.ev(st)
            u5_rot.done(ui, wev)
            st = dve(lambda e, va=va, btp=btp, och=och: e.tensor_add(
                h1f[:, och, :], va, btp[:, 0:512]), [wev, p2ev] + h1_free[och])
            h1ev = kb.ev(st)
            v5_rot.done(vi, h1ev)
            fb5.done(bip, h1ev)
            h1f_evs.append(h1ev)
            last_h1 = h1ev
        row2_evs = []
        h1tm_spill_evs = []
        for ttl in range(4):
            tt = ts * 4 + ttl
            hb, hbd, hbi = h1bf_rot.take()
            st = dve(lambda e, hb=hb, ttl=ttl: e.tensor_copy(
                out=hb.rearrange("p (a b) -> p a b", b=128),
                in_=h1f[:, :, ttl * 128:(ttl + 1) * 128]), h1f_evs + hbd)
            cbev = kb.ev(st)
            btx, bdx, bix = bb.take()
            for ch in range(8):
                st = pe(lambda e, ch=ch, btx=btx, hb=hb: e.transpose(
                    btx[:, ch * 128:(ch + 1) * 128], hb[:, ch * 128:(ch + 1) * 128],
                    iden_sb[:, :]), ([cbev] + bdx) if ch == 0 else [])
            tev = kb.ev(st)
            h1bf_rot.done(hbi, tev)
            ht, htd, hti = h1tm_rot.take()
            st = dve(lambda e, ht=ht, btx=btx: e.tensor_copy(out=ht, in_=btx[:, 0:1024]),
                     [tev] + htd)
            htev = kb.ev(st)
            bb.done(bix, htev)
            l2 = ln_stats([(ht[:, 0:512], 512), (ht[:, 512:1024], 512)],
                          tt, FAM_LN2, [htev])
            r1 = make_row(statw[:, tt, FAM_LN2 + 1:FAM_LN2 + 2], rows_b, 32, ttl * 128,
                          [l2["negmu"]] + prev_bc2)
            r2_ = make_row(statw[:, tt, FAM_LN2:FAM_LN2 + 1], rows_b, 64, ttl * 128,
                           [l2["r"]] + prev_bc2)
            row2_evs += [r1, r2_]
            spv = kb.dma("pool", "fp", h1tm_sp[tt, :, :], ht, deps=[htev])
            h1tm_rot.done(hti, [spv, l2["r"]])
            if tt == 0:
                tap("h1t0", ht, [htev])
        bm2_ev = bcast_at(bcA, rows_b, 32, 0, 512, row2_evs + [last_h1])
        br2_ev = bcast_at(bcB, rows_b, 64, 0, 512, row2_evs + [last_h1])
        bm2, br2 = bcA, bcB
        prev_bc2 = [bm2_ev, br2_ev]
        last_xm = None
        for ch in range(8):
            ua, ud, ui = u5_rot.take()
            st = dve(lambda e, ua=ua, ch=ch, bm2=bm2: e.tensor_add(
                ua, h1f[:, ch, :], bm2[:, 0:512]), [bm2_ev] + h1f_evs + ud)
            uev = kb.ev(st)
            va, vd, vi = v5_rot.take()
            st = dve(lambda e, va=va, ua=ua, ch=ch, br2=br2: e.scalar_tensor_tensor(
                out=va, in0=ua, scalar=lncol[:, ch, 2:3], in1=br2[:, 0:512],
                op0=ALU.mult, op1=ALU.mult), [uev, br2_ev] + vd)
            vev = kb.ev(st)
            u5_rot.done(ui, vev)
            xa, xd, xi = xm_rot.take()
            st = dve(lambda e, xa=xa, va=va, ch=ch: e.tensor_scalar_add(
                xa, va, lncol[:, ch, 3:4]), [vev] + xd)
            xev = kb.ev(st)
            v5_rot.done(vi, xev)
            h1_free[ch] = [xev]
            spv = kb.dma("pool", "fp", xm_sp[ch, :, ts * 512:(ts + 1) * 512], xa, deps=[xev])
            xm_rot.done(xi, spv)
            xm_sp_evs[(ts, ch)] = spv
            last_xm = xev
        bc_prev = [[last_xm], [last_xm]]
        pump(4)

    if "xmsp" in dbg:
        kb.dma("pool", "fo", dbg["xmsp"], xm_sp[0, :, :], deps=[xm_sp_evs[(NTS - 1, 7)]])
    fb.evs[4] = list(bc_prev[0])
    fb.evs[5] = list(bc_prev[1])
    for i in range(4):
        fb.evs[i] = list(fb5.evs[i])
    maybe_stop("p5")
    pump_flush()
    kb.fence()
    p5.close()
    p45.close()
    p0.close()
    prow.close()

    p6 = ExitStack()
    w1_sb = p6.enter_context(nc.sbuf_tensor("w1sb", [128, 8, FF], BF16))
    w2_sb = p6.enter_context(nc.sbuf_tensor("w2sb", [128, 16, C], BF16))
    xmg = p6.enter_context(nc.sbuf_tensor("xmg", [128, 8, TG], BF16))
    hmid = p6.enter_context(nc.sbuf_tensor("hmid", [128, 16, TG], BF16))
    accb = p6.enter_context(nc.sbuf_tensor("accb", [128, 2, 8, C], F32))
    wr_sb = p6.enter_context(nc.sbuf_tensor("wrsb", [128, 8, 8], BF16))
    stage6 = p6.enter_context(nc.sbuf_tensor("stage6", [128, 1, 64], F32))
    gt6 = p6.enter_context(nc.sbuf_tensor("gt6", [128, 2, 24], F32))
    h1r = p6.enter_context(nc.sbuf_tensor("h1r", [128, 2, C], BF16))
    xrb = p6.enter_context(nc.sbuf_tensor("xrb", [128, 2, C], F32))
    otb = p6.enter_context(nc.sbuf_tensor("otb", [128, 2, C], F32))
    st6_rot = Rot(kb, [stage6[:, 0, :]])
    gt6_rot = Rot(kb, [gt6[:, 0, :], gt6[:, 1, :]])
    h1r_rot = Rot(kb, [h1r[:, 0, :], h1r[:, 1, :]])
    xrb_rot = Rot(kb, [xrb[:, 0, :], xrb[:, 1, :]])
    otb_rot = Rot(kb, [otb[:, 0, :], otb[:, 1, :]])

    sg, sd, si = st6_rot.take()
    sg_v = sg[:, 0:64].rearrange("p (a b) -> p a b", b=8)
    lev = kb.dma("sync", "fw", sg_v, wr_h.rearrange("(a p) q -> p a q", p=128), deps=sd)
    st = dve(lambda e: e.tensor_copy(out=wr_sb[:, :, :], in_=sg_v), [lev])
    wr_done = kb.ev(st)
    st6_rot.done(si, wr_done)

    out_evs = []
    acc_free = [[[] for _ in range(8)] for _ in range(2)]
    acc_ev = [[[] for _ in range(8)] for _ in range(2)]

    def emit_tail(g, ttl):
        ping = g % 2
        tt = g * 8 + ttl
        hr, hrd, hri = h1r_rot.take()
        hlev = kb.dma("sync", "fr", hr, h1tm_sp[tt, :, :], deps=hrd)
        fa = accb[:, ping, ttl, :]
        st = dve(lambda e, fa=fa, hr=hr: e.tensor_add(fa, fa, hr),
                 [hlev] + acc_ev[ping][ttl])
        fev = kb.ev(st)
        h1r_rot.done(hri, fev)
        fstats = ln_stats([(fa[:, 0:512], 512), (fa[:, 512:1024], 512)],
                          tt, FAM_FFN, [fev])
        xr, xrd, xri = xrb_rot.take()
        xlev = kb.dma("sync", "fr", xr, x1tm_sp[tt, :, :], deps=xrd)
        ob, obd, obi = otb_rot.take()
        st = dve(lambda e, ob=ob, fa=fa, tt=tt: e.tensor_scalar(
            out=ob, in0=fa, scalar1=statw[:, tt, FAM_FFN + 1:FAM_FFN + 2],
            scalar2=statw[:, tt, FAM_FFN:FAM_FFN + 1], op0=ALU.add, op1=ALU.mult),
            [fstats["r"], fstats["negmu"]] + obd)
        t2ev = kb.ev(st)
        acc_free[ping][ttl] = [t2ev]
        st = dve(lambda e, ob=ob, xr=xr: e.tensor_add(ob, ob, xr), [t2ev, xlev])
        oev = kb.ev(st)
        xrb_rot.done(xri, oev)
        sev = kb.dma("pool", "fo", out_h[tt * 128:(tt + 1) * 128, :], ob, deps=[oev])
        otb_rot.done(obi, sev)
        out_evs.append(sev)
    w1_ready = None
    w2_ready = None
    xmg_free = []
    for grp in range(NGRP):
        xmg_evs = []
        for ch in range(8):
            dep = [xm_sp_evs[(2 * grp, ch)], xm_sp_evs[(2 * grp + 1, ch)]] + xmg_free
            lev = kb.dma("sync", "fr", xmg[:, ch, :], xm_sp[ch, :, grp * TG:(grp + 1) * TG],
                         deps=dep)
            xmg_evs.append(lev)
        xmg_done = xmg_evs[-1]
        gate_evs = [None] * 8
        for ttl in range(8):
            tt = grp * 8 + ttl
            bt, bd, bi = fb.take()
            for ch in range(8):
                st = pe(lambda e, ch=ch, ttl=ttl, bt=bt: e.matmul(
                    bt[:, 0:8], xmg[:, ch, ttl * 128:(ttl + 1) * 128], wr_sb[:, ch, :],
                    start=(ch == 0), stop=(ch == 7)),
                    ([wr_done] + xmg_evs + bd) if ch == 0 else [])
            mmev = kb.ev(st)
            ga, gd, gi = gt6_rot.take()
            st = dve(lambda e, ga=ga, bt=bt: e.tensor_reduce(
                out=ga[:, 0:1], in_=bt[:, 0:8], axis=AXL.X, op=ALU.max, negate=True),
                [mmev] + gd)
            st = act(lambda e, ga=ga, bt=bt: e.activation(
                out=ga[:, 3:11], in_=bt[:, 0:8], func=AF.Exp, bias=ga[:, 0:1],
                accum_out=ga[:, 1:2]), [kb.ev(st)])
            exv = kb.ev(st)
            fb.done(bi, exv)
            st = dve(lambda e, ga=ga: e.reciprocal(out=ga[:, 2:3], in_=ga[:, 1:2]), [exv])
            st = dve(lambda e, ga=ga: e.tensor_scalar(
                out=ga[:, 3:11], in0=ga[:, 3:11], scalar1=ga[:, 2:3], scalar2=None,
                op0=ALU.mult), [kb.ev(st)])
            gev = kb.ev(st)
            st = dve(lambda e, ga=ga: e.tensor_reduce(
                out=ga[:, 19:20], in_=ga[:, 3:11], axis=AXL.X, op=ALU.max), [gev])
            st = dve(lambda e, ga=ga: e.tensor_scalar(
                out=ga[:, 11:19], in0=ga[:, 3:11], scalar1=ga[:, 19:20], scalar2=None,
                op0=ALU.is_equal), [kb.ev(st)])
            st = dve(lambda e, ga=ga: e.scalar_tensor_tensor(
                out=ga[:, 11:19], in0=ga[:, 11:19], scalar=-1e30, in1=ga[:, 3:11],
                op0=ALU.mult, op1=ALU.add), [kb.ev(st)])
            st = dve(lambda e, ga=ga: e.tensor_reduce(
                out=ga[:, 20:21], in_=ga[:, 11:19], axis=AXL.X, op=ALU.max), [kb.ev(st)])
            st = dve(lambda e, ga=ga: e.tensor_scalar(
                out=ga[:, 11:19], in0=ga[:, 3:11], scalar1=ga[:, 20:21], scalar2=None,
                op0=ALU.is_ge), [kb.ev(st)])
            st = dve(lambda e, ga=ga, tt=tt: e.tensor_mul(
                gates[:, tt, :], ga[:, 3:11], ga[:, 11:19]), [kb.ev(st)])
            gate_evs[ttl] = kb.ev(st)
            gt6_rot.done(gi, gate_evs[ttl])
        if grp == 0:
            tap("gates", gates[:, 0, :], [gate_evs[0]])
        for e_ in range(NE):
            if w1_ready is None:
                w1_ready = kb.dma("sync", "fr", w1_sb[:, :, :],
                                  w1b[e_].rearrange("a p f -> p a f"),
                                  deps=_pump["done"][("w1", e_)])
            if w2_ready is None:
                w2_ready = kb.dma("sync", "fr", w2_sb[:, :, :],
                                  w2b[e_].rearrange("a p f -> p a f"),
                                  deps=_pump["done"][("w2", e_)])
            gelu_evs = []
            last_w1_mm = None
            for f in range(16):
                bts = []
                dps = [w1_ready, xmg_done]
                for kk in range(2):
                    bt, bd, bi = fb.take()
                    bts.append([bt, bi])
                    dps += bd
                for ch in range(8):
                    for kk in range(2):
                        bt = bts[kk][0]
                        st = pe(lambda e, ch=ch, kk=kk, f=f, bt=bt: e.matmul(
                            bt[:, 0:512], w1_sb[:, ch, f * 128:(f + 1) * 128],
                            xmg[:, ch, kk * 512:(kk + 1) * 512],
                            start=(ch == 0), stop=(ch == 7)),
                            dps if (ch == 0 and kk == 0) else [])
                last_w1_mm = st
                mmev = kb.ev(st)
                for kk in range(2):
                    bt, bi = bts[kk]
                    st = act(lambda e, bt=bt, f=f, kk=kk: e.activation(
                        out=hmid[:, f, kk * 512:(kk + 1) * 512], in_=bt[:, 0:512],
                        func=AF.Gelu_apprx_tanh), [mmev])
                    eev = kb.ev(st)
                    gelu_evs.append(eev)
                    fb.done(bi, eev)
            w1_free = kb.ev(last_w1_mm)
            if e_ == NE - 1:
                xmg_free = [w1_free]
            if not (grp == NGRP - 1 and e_ == NE - 1):
                ne_ = (e_ + 1) % NE
                w1_ready = kb.dma("sync", "fr", w1_sb[:, :, :],
                                  w1b[ne_].rearrange("a p f -> p a f"),
                                  deps=[w1_free] + _pump["done"][("w1", ne_)])
            gelu_done = gelu_evs[-1]
            if grp > 0 and e_ < 4:
                emit_tail(grp - 1, 2 * e_)
                emit_tail(grp - 1, 2 * e_ + 1)
            last_w2_mm = None
            for ttl in range(8):
                tt = grp * 8 + ttl
                for oh in range(2):
                    bt, bd, bi = fb.take()
                    for f in range(16):
                        st = pe(lambda e, f=f, ttl=ttl, oh=oh, bt=bt: e.matmul(
                            bt[:, 0:512], hmid[:, f, ttl * 128:(ttl + 1) * 128],
                            w2_sb[:, f, oh * 512:(oh + 1) * 512],
                            start=(f == 0), stop=(f == 15)),
                            ([w2_ready, gelu_done] + bd) if f == 0 else [])
                    last_w2_mm = st
                    mmev = kb.ev(st)
                    adst = accb[:, grp % 2, ttl, oh * 512:(oh + 1) * 512]
                    gcol = gates[:, tt, e_:e_ + 1]
                    if e_ == 0:
                        st = dve(lambda e, adst=adst, bt=bt, gcol=gcol: e.tensor_scalar(
                            out=adst, in0=bt[:, 0:512], scalar1=gcol, scalar2=None,
                            op0=ALU.mult),
                            [mmev, gate_evs[ttl]] + acc_free[grp % 2][ttl])
                    else:
                        st = dve(lambda e, adst=adst, bt=bt, gcol=gcol: e.scalar_tensor_tensor(
                            out=adst, in0=bt[:, 0:512], scalar=gcol, in1=adst,
                            op0=ALU.mult, op1=ALU.add), [mmev])
                    aev = kb.ev(st)
                    if oh == 0:
                        acc_ev[grp % 2][ttl] = [aev]
                    else:
                        acc_ev[grp % 2][ttl].append(aev)
                    fb.done(bi, aev)
            w2_free = kb.ev(last_w2_mm)
            if not (grp == NGRP - 1 and e_ == NE - 1):
                ne_ = (e_ + 1) % NE
                w2_ready = kb.dma("sync", "fr", w2_sb[:, :, :],
                                  w2b[ne_].rearrange("a p f -> p a f"),
                                  deps=[w2_free] + _pump["done"][("w2", ne_)])
    for ttl in range(8):
        emit_tail(NGRP - 1, ttl)

    kb.op("pool", lambda e: e.nop(), deps=list(out_evs[-24:]))


_NC_CACHE = {}


def _get_nc():
    key = tuple(DEBUG_TAPS)
    if key not in _NC_CACHE:
        _NC_CACHE[key] = build_program()
    return _NC_CACHE[key]


def _host_inputs(inputs):
    f32 = np.float32
    x = np.asarray(inputs["L0_attn_x"], f32)
    wq1T = np.ascontiguousarray(np.asarray(inputs["L0_attn_qkv_weight"], f32).T)
    wp1T = np.ascontiguousarray(np.asarray(inputs["L0_attn_proj_weight"], f32).T)
    bp1 = np.asarray(inputs["L0_attn_proj_bias"], f32).reshape(1, C)
    wq2T = np.ascontiguousarray(np.asarray(inputs["L0_ffn_qkv_weight"], f32).T)
    wp2T = np.ascontiguousarray(np.asarray(inputs["L0_ffn_proj_weight"], f32).T)
    wrT = np.ascontiguousarray(np.asarray(inputs["L0_ffn_router_weight"], f32).T)
    w1 = np.ascontiguousarray(np.asarray(inputs["L0_ffn_moe_w1"], f32))
    w2 = np.ascontiguousarray(np.asarray(inputs["L0_ffn_moe_w2"], f32))
    g1 = np.asarray(inputs["L0_ffn_ln1_weight"], f32)
    b1 = np.asarray(inputs["L0_ffn_ln1_bias"], f32)
    g2 = np.asarray(inputs["L0_ffn_ln2_weight"], f32)
    b2 = np.asarray(inputs["L0_ffn_ln2_bias"], f32)
    lncols = np.stack([v.reshape(8, 128).T for v in (g1, b1, g2, b2)], axis=2)
    lncols = np.ascontiguousarray(lncols)

    iden = np.eye(128, dtype=np.float32).astype(ml_dtypes.bfloat16)
    ones4 = np.ones((128, 512), np.float32).astype(ml_dtypes.bfloat16)
    onescol = np.ones((128, 1), np.float32).astype(ml_dtypes.bfloat16)
    mask = np.full((128, 128), NEG_BIG, np.float32)
    for i in range(8):
        for s_ in range(16):
            for t_ in range(s_ + 1):
                mask[i * 16 + s_, i * 16 + t_] = 0.0
    mask = np.ascontiguousarray(np.tile(mask, (1, 4)))

    shared = dict(wq1T=wq1T, wp1T=wp1T, bp1=bp1, wq2T=wq2T, wp2T=wp2T, wrT=wrT,
                  w1=w1, w2=w2, lncols=lncols, iden=iden, ones4=ones4,
                  onescol=onescol, mask=mask)
    in_maps = []
    for c in range(NCORE):
        xs = np.ascontiguousarray(x[:, c * NLOC:(c + 1) * NLOC, :].reshape(T, C))
        m = dict(shared)
        m["x_tm"] = xs
        m["x_fm"] = np.ascontiguousarray(xs.T)
        in_maps.append(m)
    return in_maps


def kernel(**inputs):
    nc = _get_nc()
    in_maps = _host_inputs(inputs)
    res = run_bass_kernel_spmd(nc, in_maps, core_ids=list(range(NCORE)))
    kernel.last_results = res
    out = np.empty((B, NSEQ, C), np.float32)
    for c in range(NCORE):
        out[:, c * NLOC:(c + 1) * NLOC, :] = \
            np.asarray(res.results[c]["out"]).reshape(B, NLOC, C)
    return out

